# version 06
# baseline (speedup 1.0000x reference)
import math
import numpy as np
import ml_dtypes
import concourse.bass as bass
import concourse.mybir as mybir
from concourse.bass_utils import run_bass_kernel_spmd

F32 = mybir.dt.float32
BF16 = mybir.dt.bfloat16
AF = mybir.ActivationFunctionType
ALU = mybir.AluOpType
AX = mybir.AxisListType

B, S, D = 2, 4096, 4096
H, NOPE, ROPE, VD, KVR = 32, 64, 64, 128, 256
NQ = 1024
NJ = 8
TG = 512
NTG = S // TG
EPS_LN = 1e-5
EPS_ATTN = 1e-6
NEG = -1.0e30
_mm_ = 0.1 * 1.0 * math.log(128.0) + 1.0
SOFTMAX_SCALE = (128.0 ** -0.5) * _mm_ * _mm_

C_EPSLN, C_EPSAT, C_NEGH, C_I1024, C_I256, C_I4096, C_BA2, C_BB2 = range(8)


def _yarn_cos_sin(seq_len, dim=ROPE, base=10000.0, factor=128.0, beta_fast=32.0,
                  beta_slow=1.0, orig_max=8192, mscale=1.0, mscale_all_dim=1.0):
    def corr_dim(r):
        return dim * math.log(orig_max / (r * 2 * math.pi)) / (2 * math.log(base))
    low = max(math.floor(corr_dim(beta_fast)), 0)
    high = min(math.ceil(corr_dim(beta_slow)), dim - 1)
    hi = high + 0.001 if low == high else float(high)
    ramp = np.clip((np.arange(dim // 2, dtype=np.float32) - low) / (hi - low), 0.0, 1.0)
    inv_freq_mask = 1.0 - ramp
    freq_extra = 1.0 / base ** (np.arange(0, dim, 2, dtype=np.float32) / dim)
    freq_inter = freq_extra / factor
    inv_freq = freq_inter * (1.0 - inv_freq_mask) + freq_extra * inv_freq_mask
    t = np.arange(seq_len, dtype=np.float32)
    freqs = np.outer(t, inv_freq)
    emb = np.concatenate([freqs, freqs], axis=-1)
    def gm(s, m):
        return 1.0 if s <= 1 else 0.1 * m * math.log(s) + 1.0
    ms = gm(factor, mscale) / gm(factor, mscale_all_dim)
    return (np.cos(emb) * ms).astype(np.float32), (np.sin(emb) * ms).astype(np.float32)


def _qpos(c):
    return np.concatenate([512 * j + 128 * c + np.arange(128) for j in range(NJ)])


class KB:
    def __init__(self, nc):
        self.nc = nc
        self.sems = {}
        self.cnt = {}
        self.q = None

    def mksem(self, stk, *names):
        for n in names:
            self.sems[n] = stk.enter_context(self.nc.semaphore(n))
            self.cnt[n] = 0

    def begin(self):
        self.q = {e: [] for e in "tvsgy"}

    def wait(self, eng, sname, val=None):
        v = self.cnt[sname] if val is None else val
        if v > 0:
            h = self.sems[sname]
            self.q[eng].append(lambda e, h=h, v=v: e.wait_ge(h, v))
        return v

    def op(self, eng, fn, inc=None, incby=1):
        if inc is not None:
            h = self.sems[inc]
            self.cnt[inc] += incby
            self.q[eng].append(lambda e, fn=fn, h=h, k=incby: fn(e).then_inc(h, k))
            return self.cnt[inc]
        self.q[eng].append(fn)
        return None

    def flush(self):
        qq = self.q
        with self.nc.Block() as blk:
            if qq["y"]:
                @blk.sync
                def _(e, _q=qq["y"]):
                    for f in _q:
                        f(e)
            if qq["g"]:
                @blk.gpsimd
                def _(e, _q=qq["g"]):
                    for f in _q:
                        f(e)
            if qq["t"]:
                @blk.tensor
                def _(e, _q=qq["t"]):
                    for f in _q:
                        f(e)
            if qq["v"]:
                @blk.vector
                def _(e, _q=qq["v"]):
                    for f in _q:
                        f(e)
            if qq["s"]:
                @blk.scalar
                def _(e, _q=qq["s"]):
                    for f in _q:
                        f(e)
        self.q = None


def build_program(debug=False):
    from contextlib import ExitStack
    nc = bass.Bass("TRN2", target_bir_lowering=False)

    din = lambda n, s, d=F32: nc.dram_tensor(n, s, d, kind="ExternalInput")
    xT = din("xT", [D, S])
    x_tm = din("x_tm", [S, D])
    xq_tm = din("xq_tm", [NQ, D])
    xqT = din("xqT", [D, NQ])
    wqaT = din("wqaT", [D, 1024])
    wqbnT = din("wqbnT", [1024, 2048])
    wqbpT = din("wqbpT", [1024, 4096])
    wkvaT = din("wkvaT", [D, 384])
    wkvbnT = din("wkvbnT", [KVR, 2048])
    wkvbvT = din("wkvbvT", [KVR, 4096])
    woT = din("woT", [D, D])
    WT = din("WT", [D, D])
    qln = din("qln", [128, 8])
    kvln = din("kvln", [128, 2])
    bqv = din("bqv", [128, 32])
    CSK = din("CSK", [128, S])
    CSQ = din("CSQ", [128, NQ])
    MASKW = din("MASKW", [128, 512])
    FOLD_K = din("FOLD_K", [128, 64], BF16)
    FOLD_LO = din("FOLD_LO", [128, 128], BF16)
    FOLD_HI = din("FOLD_HI", [128, 128], BF16)
    I128B = din("I128B", [128, 128], BF16)
    I128F = din("I128F", [128, 128])
    ONESB = din("ONESB", [128, 1], BF16)
    ONESFR = din("ONESFR", [1, 128])
    ONESFC = din("ONESFC", [128, 1])
    CONSTS = din("CONSTS", [128, 8])

    xqf_scr = nc.dram_tensor("xqf_scr", [D, NQ], BF16,
                             kind="ExternalOutput" if debug else "Internal")
    out = nc.dram_tensor("out", [D, NQ], F32, kind="ExternalOutput")
    if debug:
        d_statc = nc.dram_tensor("d_statc", [128, 96], F32, kind="ExternalOutput")
        d_qstat = nc.dram_tensor("d_qstat", [128, 96], F32, kind="ExternalOutput")
        d_sw = nc.dram_tensor("d_sw", [128, 2], F32, kind="ExternalOutput")
        d_ckvn = nc.dram_tensor("d_ckvn", [128, 2 * S], BF16, kind="ExternalOutput")
        d_kpe2 = nc.dram_tensor("d_kpe2", [128, S], BF16, kind="ExternalOutput")
        d_q1n = nc.dram_tensor("d_q1n", [128, 8 * NQ], BF16, kind="ExternalOutput")
        d_oh = nc.dram_tensor("d_oh", [128, H * NQ], BF16, kind="ExternalOutput")
        d_kf0 = nc.dram_tensor("d_kf0", [128, S], BF16, kind="ExternalOutput")
        d_kf1 = nc.dram_tensor("d_kf1", [128, S], BF16, kind="ExternalOutput")
        d_vv0 = nc.dram_tensor("d_vv0", [128, S], BF16, kind="ExternalOutput")
        d_qf0 = nc.dram_tensor("d_qf0", [128, NQ], BF16, kind="ExternalOutput")
        d_qf1 = nc.dram_tensor("d_qf1", [128, NQ], BF16, kind="ExternalOutput")
        d_p0 = nc.dram_tensor("d_p0", [128, S], BF16, kind="ExternalOutput")
        d_sum = nc.dram_tensor("d_sum", [128, 16], F32, kind="ExternalOutput")
        d_acc1 = nc.dram_tensor("d_acc1", [128, NQ], F32, kind="ExternalOutput")
        d_ffny = nc.dram_tensor("d_ffny", [128, NQ], F32, kind="ExternalOutput")
        d_rows3 = nc.dram_tensor("d_rows3", [1, 3 * NQ], F32, kind="ExternalOutput")

    from contextlib import ExitStack as _ES
    stk = _ES()
    sb = lambda n, s, d: stk.enter_context(nc.sbuf_tensor(n, s, d))

    kb = KB(nc)
    kb.mksem(stk, "DS", "DG", "T", "V", "Sg")

    consts = sb("consts", [128, 8], F32)
    ones_b = sb("ones_b", [128, 1], BF16)
    onesf_r = sb("onesf_r", [1, 128], F32)
    onesf_c = sb("onesf_c", [128, 1], F32)
    i128_b = sb("i128_b", [128, 128], BF16)
    i128_f = sb("i128_f", [128, 128], F32)
    fold_k = sb("fold_k", [128, 64], BF16)
    fold_lo = sb("fold_lo", [128, 128], BF16)
    fold_hi = sb("fold_hi", [128, 128], BF16)
    maskw = sb("maskw", [128, 512], F32)
    csq = sb("csq", [128, NQ], F32)
    qln_sb = sb("qln_sb", [128, 8], F32)
    kvln_sb = sb("kvln_sb", [128, 2], F32)
    bq_sb = sb("bq_sb", [128, 32], F32)

    statc = sb("statc", [128, 96], F32)
    qsum = sb("qsum", [128, 8], F32)
    qss = sb("qss", [128, 8], F32)
    qmx = sb("qmx", [128, 8], F32)
    qmn = sb("qmn", [128, 8], F32)
    qvar = sb("qvar", [128, 8], F32)
    wc = sb("wc", [128, 8], F32)
    wc2 = sb("wc2", [128, 8], F32)
    wc3 = sb("wc3", [128, 8], F32)
    wc4 = sb("wc4", [128, 8], F32)
    sumc = sb("sumc", [128, 16], F32)
    qm = sb("qm", [128, 8], F32)
    qrs = sb("qrs", [128, 8], F32)
    qrs2 = sb("qrs2", [128, 8], F32)
    qR = sb("qR", [128, 8], F32)
    qDIV = sb("qDIV", [128, 8], F32)
    nmR = sb("nmR", [128, 8], F32)
    nmRp = sb("nmRp", [128, 8], F32)
    nmRh = sb("nmRh", [128, 8], F32)
    swcol = sb("swcol", [128, 8], F32)
    winv = sb("winv", [128, 8], F32)

    ckv_n = sb("ckv_n", [128, 2 * S], BF16)
    kpe2 = sb("kpe2", [128, S], BF16)
    q1n = sb("q1n", [128, 8 * NQ], BF16)

    def make_row_ops(colsrc_fn, ctmp_ap, ncols, tp_ps, rowtmp, dstrow):
        kb.wait("v", "Sg")
        kb.wait("v", "DG")
        kb.op("v", colsrc_fn, inc="V")
        kb.wait("t", "V")
        kb.op("t", lambda t: t.transpose(tp_ps[0:ncols, 0:128], ctmp_ap, i128_f[:]),
              inc="T")
        kb.wait("v", "T")
        kb.op("v", lambda v: v.tensor_copy(rowtmp[0:ncols, :], tp_ps[0:ncols, 0:128]),
              inc="V")
        kb.wait("g", "V")
        kb.wait("g", "T")
        kb.op("g", lambda g: g.dma_start(
            dstrow[:].rearrange("o (a b) -> o a b", a=ncols),
            rowtmp[0:ncols, :]), inc="DG", incby=16)

    kb.begin()
    for dst_, src_ in [(consts, CONSTS), (maskw, MASKW), (csq, CSQ),
                       (qln_sb, qln), (kvln_sb, kvln), (bq_sb, bqv),
                       (onesf_r, ONESFR), (onesf_c, ONESFC), (i128_f, I128F),
                       (ones_b, ONESB), (i128_b, I128B), (fold_k, FOLD_K),
                       (fold_lo, FOLD_LO), (fold_hi, FOLD_HI)]:
        kb.op("y", lambda y, d=dst_, s=src_: y.dma_start(d[:], s[:]),
              inc="DS", incby=16)
    kb.wait("y", "DS")
    kb.flush()

    with _ES() as loc:
        lsb = lambda n, s, d: loc.enter_context(nc.sbuf_tensor(n, s, d))
        lps = lambda n, s, d: loc.enter_context(nc.psum_tensor(n, s, d))
        kb.begin()
        xtm_ring = [lsb(f"xtm{i}", [128, D], F32) for i in range(2)]
        sq_scr = lsb("sq_scr", [128, D], BF16)
        sxc = lsb("sxc", [128, 32], F32)
        sxxc = lsb("sxxc", [128, 32], F32)
        wt_ring = [lsb(f"wtp{i}", [128, D], F32) for i in range(2)]
        wabs = lsb("wabs", [128, 32], F32)
        ones8f = lsb("ones8f", [128, 8], F32)
        ones128 = lsb("ones128", [128, 128], F32)
        bc8_ps = lps("bc8_ps", [128, 8], F32)

        kb.op("v", lambda v: v.memset(ones128[:], 1.0), inc="V")
        v_done = {}
        s_done = {}
        for ti in range(32):
            buf = xtm_ring[ti % 2]
            kb.wait("y", "V", v_done.get(ti - 2, 0))
            kb.wait("y", "Sg", s_done.get(ti - 2, 0))
            dtgt = kb.op("y", lambda y, buf=buf, ti=ti:
                         y.dma_start(buf[:], x_tm[128 * ti:128 * ti + 128, :]),
                         inc="DS", incby=16)
            kb.wait("v", "DS", dtgt)
            v_done[ti] = kb.op("v", lambda v, buf=buf, ti=ti:
                               v.reduce_sum(sxc[:, ti:ti + 1], buf[:], axis=AX.X),
                               inc="V")
            kb.wait("s", "DS", dtgt)
            s_done[ti] = kb.op("s", lambda s, buf=buf, ti=ti:
                               s.activation(sq_scr[:], buf[:], AF.Square,
                                            accum_out=sxxc[:, ti:ti + 1]),
                               inc="Sg")

        kb.wait("v", "V")
        kb.wait("v", "Sg")
        def derive_stats(v):
            v.tensor_scalar(statc[:, 0:64:2], sxc[:, 0:32], 1.0 / D, None,
                            op0=ALU.mult)
            v.tensor_scalar(statc[:, 1:64:2], sxxc[:, 0:32], 1.0 / D, None,
                            op0=ALU.mult)
            v.tensor_mul(statc[:, 64:96], statc[:, 0:64:2], statc[:, 0:64:2])
            return v.tensor_sub(statc[:, 1:64:2], statc[:, 1:64:2], statc[:, 64:96])
        kb.op("v", derive_stats, inc="V")
        kb.wait("s", "V")
        def rs_from_var(s):
            s.activation(statc[:, 1:64:2], statc[:, 1:64:2], AF.Ln,
                         bias=consts[:, C_EPSLN:C_EPSLN + 1], scale=1.0)
            return s.activation(statc[:, 1:64:2], statc[:, 1:64:2], AF.Exp,
                                bias=0.0, scale=consts[:, C_NEGH:C_NEGH + 1])
        kb.op("s", rs_from_var, inc="Sg")

        wv_done = {}
        for wi in range(32):
            buf = wt_ring[wi % 2]
            kb.wait("y", "V", wv_done.get(wi - 2, 0))
            dtgt = kb.op("y", lambda y, buf=buf, wi=wi:
                         y.dma_start(buf[:], WT[128 * wi:128 * wi + 128, :]),
                         inc="DS", incby=16)
            kb.wait("v", "DS", dtgt)
            wv_done[wi] = kb.op("v", lambda v, buf=buf, wi=wi:
                                v.reduce_sum(wabs[:, wi:wi + 1], buf[:], axis=AX.X,
                                             apply_absolute_value=True), inc="V")
        kb.wait("v", "V")
        def wsum_fin(v):
            v.reduce_sum(wabs[:, 0:1], wabs[:, 0:32], axis=AX.X)
            v.tensor_copy(wc2[:], consts[:])
            return v.tensor_scalar(ones8f[:], ones128[:, 0:8], wabs[:, 0:1],
                                   None, op0=ALU.mult)
        kb.op("v", wsum_fin, inc="V")
        kb.wait("t", "V")
        kb.op("t", lambda t: t.matmul(bc8_ps[:], ones128[:], ones8f[:],
                                      start=True, stop=True), inc="T")
        kb.wait("v", "T")
        def sw_fin(v):
            v.tensor_scalar(wc3[:], bc8_ps[:], 1.0 / (4096.0 * 4096.0), None,
                            op0=ALU.mult)
            v.tensor_copy(wc2[:], consts[:])
            v.tensor_scalar(winv[:], wc3[:], 1e-5, None, op0=ALU.max)
            v.tensor_copy(wc2[:], consts[:])
            return v.reciprocal(swcol[:], winv[:])
        kb.op("v", sw_fin, inc="V")

        kb.flush()

    with _ES() as loc:
        lsb = lambda n, s, d: loc.enter_context(nc.sbuf_tensor(n, s, d))
        lps = lambda n, s, d: loc.enter_context(nc.psum_tensor(n, s, d))
        kb.begin()
        kb.wait("y", "V")
        kb.wait("y", "Sg")
        kb.wait("y", "T")
        tp_b = lps("tp_b", [128, 1024], BF16)
        stage = lsb("stage", [128, 1024], BF16)
        qxr = [lsb(f"qxr{i}", [128, D], F32) for i in range(2)]
        sqf = lsb("sqf", [128, D], F32)
        ngf = lsb("ngf", [128, D], F32)
        wbuf = lsb("wbuf", [128, D], F32)
        bmask = lsb("bmask", [128, D], F32)
        xq_t = lsb("xq_t", [128, D], BF16)

        qv_done = {}
        for jt in range(NJ):
            buf = qxr[jt % 2]
            kb.wait("y", "V", qv_done.get(jt - 2, 0))
            dtgt = kb.op("y", lambda y, buf=buf, jt=jt:
                         y.dma_start(buf[:], xq_tm[128 * jt:128 * jt + 128, :]),
                         inc="DS", incby=16)
            kb.wait("v", "DS", dtgt)
            def qredu(v, buf=buf, jt=jt):
                v.tensor_mul(sqf[:], buf[:], buf[:])
                v.tensor_scalar(ngf[:], buf[:], -1.0, None, op0=ALU.mult)
                v.reduce_sum(qss[:, jt:jt + 1], sqf[:], axis=AX.X)
                v.reduce_max(qmn[:, jt:jt + 1], ngf[:], axis=AX.X)
                v.reduce_sum(qsum[:, jt:jt + 1], buf[:], axis=AX.X)
                return v.reduce_max(qmx[:, jt:jt + 1], buf[:], axis=AX.X)
            qv_done[jt] = kb.op("v", qredu, inc="V")

        kb.wait("v", "V")
        def qderive(v):
            v.tensor_scalar(qm[:], qsum[:], 1.0 / D, None, op0=ALU.mult)
            return v.tensor_scalar(wc[:], qss[:], 1.0 / D, None, op0=ALU.mult)
        kb.op("v", qderive, inc="V")
        kb.wait("s", "V")
        kb.op("s", lambda s: s.activation(wc2[:], qm[:], AF.Square), inc="Sg")
        kb.wait("v", "Sg")
        kb.op("v", lambda v: v.tensor_sub(qvar[:], wc[:], wc2[:]), inc="V")
        kb.wait("s", "V")
        def qrs_op(s):
            s.activation(qrs[:], qvar[:], AF.Ln,
                         bias=consts[:, C_EPSLN:C_EPSLN + 1], scale=1.0)
            return s.activation(qrs[:], qrs[:], AF.Exp, bias=0.0,
                                scale=consts[:, C_NEGH:C_NEGH + 1])
        kb.op("s", qrs_op, inc="Sg")
        kb.wait("v", "Sg")
        def qmean2(v):
            v.tensor_mul(wc2[:], qvar[:], qrs[:])
            v.tensor_copy(wc3[:], consts[:, 0:8])
            return v.tensor_mul(wc[:], wc2[:], qrs[:])
        kb.op("v", qmean2, inc="V")
        kb.wait("s", "V")
        def qrs2_op(s):
            s.activation(qrs2[:], wc[:], AF.Ln,
                         bias=consts[:, C_EPSLN:C_EPSLN + 1], scale=1.0)
            return s.activation(qrs2[:], qrs2[:], AF.Exp, bias=0.0,
                                scale=consts[:, C_NEGH:C_NEGH + 1])
        kb.op("s", qrs2_op, inc="Sg")
        kb.wait("v", "Sg")
        def qfin(v):
            v.tensor_sub(wc[:], qmx[:], qm[:])
            v.tensor_add(wc2[:], qm[:], qmn[:])
            v.tensor_mul(wc3[:], qrs[:], qrs2[:])
            v.tensor_max(wc4[:], wc[:], wc2[:])
            v.tensor_copy(wc[:], consts[:, 0:8])
            v.tensor_mul(wc2[:], wc4[:], wc3[:])
            v.tensor_copy(wc4[:], consts[:, 0:8])
            v.tensor_scalar(wc[:], wc2[:], 1e-5, None, op0=ALU.max)
            v.tensor_copy(wc4[:], consts[:, 0:8])
            v.tensor_mul(wc2[:], wc[:], winv[:])
            v.reciprocal(wc4[:], wc[:])
            v.tensor_scalar(qDIV[:], wc2[:], 0.5 / 127.0, None, op0=ALU.mult)
            v.tensor_mul(wc2[:], wc4[:], wc3[:])
            v.tensor_copy(wc4[:], consts[:, 0:8])
            v.tensor_scalar(qR[:], wc2[:], 127.0, None, op0=ALU.mult)
            v.tensor_copy(wc4[:], consts[:, 0:8])
            v.tensor_mul(wc2[:], qm[:], qR[:])
            v.tensor_copy(wc4[:], consts[:, 0:8])
            v.tensor_scalar(nmR[:], wc2[:], -1.0, None, op0=ALU.mult)
            v.tensor_copy(wc4[:], consts[:, 0:8])
            v.tensor_scalar(nmRp[:], nmR[:], 128.5, None, op0=ALU.add)
            return v.tensor_scalar(nmRh[:], nmR[:], 0.5, None, op0=ALU.add)
        kb.op("v", qfin, inc="V")

        kb.wait("y", "V")
        kb.wait("y", "Sg")
        wq_done = {}
        for jt in range(NJ):
            buf = qxr[jt % 2]
            kb.wait("y", "Sg", wq_done.get(jt - 2, 0))
            dtgt = kb.op("y", lambda y, buf=buf, jt=jt:
                         y.dma_start(buf[:], xq_tm[128 * jt:128 * jt + 128, :]),
                         inc="DS", incby=16)
            kb.wait("s", "DS", dtgt)
            kb.wait("s", "V")
            wq_done[jt] = kb.op("s", lambda s, buf=buf, jt=jt: s.activation(
                wbuf[:], buf[:], AF.Copy, bias=0.0,
                scale=qR[:, jt:jt + 1]), inc="Sg")
            kb.wait("v", "Sg")
            def qchain(v, jt=jt):
                v.tensor_scalar(sqf[:], wbuf[:], nmRp[:, jt:jt + 1], None,
                                op0=ALU.add)
                cur, nxt = sqf, ngf
                for k in (128.0, 64.0, 32.0, 16.0, 8.0, 4.0, 2.0, 1.0):
                    v.tensor_scalar(bmask[:], cur[:], k, None, op0=ALU.is_ge)
                    v.scalar_tensor_tensor(nxt[:], bmask[:], -k, cur[:],
                                           op0=ALU.mult, op1=ALU.add)
                    cur, nxt = nxt, cur
                v.tensor_sub(bmask[:], wbuf[:], cur[:])
                return v.tensor_scalar(xq_t[:], bmask[:], nmRh[:, jt:jt + 1],
                                       None, op0=ALU.add)
            kb.op("v", qchain, inc="V")
            for cg in range(4):
                kb.wait("t", "V")
                kb.wait("t", "DG")
                def qtr(t, cg=cg):
                    last = None
                    for u in range(8):
                        last = t.transpose(tp_b[:, 128 * u:128 * u + 128],
                                           xq_t[:, 128 * (8 * cg + u):
                                                 128 * (8 * cg + u) + 128],
                                           i128_b[:])
                    return last
                ttg = kb.op("t", qtr, inc="T")
                kb.wait("v", "T", ttg)
                kb.wait("v", "DG")
                kb.op("v", lambda v: v.tensor_copy(stage[:], tp_b[:]), inc="V")
                kb.wait("g", "V")
                def qst(g, jt=jt, cg=cg):
                    r0 = 1024 * cg
                    return g.dma_start(
                        xqf_scr[r0:r0 + 1024, 128 * jt:128 * jt + 128]
                        .rearrange("(u p) q -> p u q", p=128),
                        stage[:].rearrange("p (u q) -> p u q", q=128))
                kb.op("g", qst, inc="DG", incby=16)
        if debug:
            kb.wait("y", "V")
            kb.wait("y", "Sg")
            kb.op("y", lambda y: y.dma_start(d_statc[:], statc[:]), inc="DS", incby=16)
            for ci_, src_ in enumerate([qm, qrs, qrs2, qR, qDIV, qvar,
                                        qmx, qmn, qsum, qss, wc, wc3]):
                kb.op("y", lambda y, ci_=ci_, src_=src_: y.dma_start(
                    d_qstat[:, 8 * ci_:8 * ci_ + 8], src_[:]), inc="DS", incby=16)
            kb.op("y", lambda y: y.dma_start(d_sw[:, 0:1], swcol[:, 0:1], allow_slow_non_contiguous=True), inc="DS", incby=16)
            kb.op("y", lambda y: y.dma_start(d_sw[:, 1:2], winv[:, 0:1], allow_slow_non_contiguous=True), inc="DS", incby=16)
            kb.wait("y", "DS")
        kb.wait("g", "DG")
        kb.flush()

    with _ES() as loc:
        lsb = lambda n, s, d: loc.enter_context(nc.sbuf_tensor(n, s, d))
        lps = lambda n, s, d: loc.enter_context(nc.psum_tensor(n, s, d))
        kb.begin()
        wkva_bf = lsb("wkva_bf", [128, 32 * 384], BF16)
        csk_sb = lsb("csk_sb", [128, S], F32)
        xbf_ring = [lsb(f"xbf{i}", [128, 32 * TG], BF16) for i in range(2)]
        kv_ps = lps("kv_ps", [128, 3 * TG], F32)
        bc_ps = lps("bc_ps", [128, TG], F32)
        st_ps = lps("st_ps", [1, TG], F32)
        fo_ps = lps("fo_ps", [64, TG], F32)
        wsum_ps = lps("wsum_ps", [128, 3], F32)
        tp4_ps = lps("tp4_ps", [4, 128], F32)
        mbc = lsb("mbc", [128, TG], F32)
        nrsbc = lsb("nrsbc", [128, TG], F32)
        rcbc = lsb("rcbc", [128, TG], F32)
        ckv_f = lsb("ckv_f", [128, 2 * TG], F32)
        kpe_f = lsb("kpe_f", [128, TG], F32)
        w1 = lsb("w1", [128, TG], BF16)
        sqb = lsb("sqb", [128, 2 * TG], BF16)
        rrow = lsb("rrow", [1, TG], F32)
        wsumc = lsb("wsumc", [128, 3], F32)
        mseg = lsb("mseg", [1, TG], F32)
        nrseg = lsb("nrseg", [1, TG], F32)
        ctmp4 = lsb("ctmp4", [128, 4], F32)
        rowt4 = lsb("rowt4", [4, 128], F32)

        wk_tgt = kb.op("g", lambda g: g.dma_start(
            wkva_bf[:].rearrange("p (ci o) -> p ci o", o=384),
            wkvaT[:].rearrange("(ci p) o -> p ci o", p=128)), inc="DG", incby=16)
        csk_tgt = kb.op("y", lambda y: y.dma_start(csk_sb[:], CSK[:]),
                        inc="DS", incby=16)
        kb.wait("t", "DG", wk_tgt)
        def wsums(t):
            last = None
            for oc in range(3):
                for ci in range(32):
                    last = t.matmul(
                        wsum_ps[:, oc:oc + 1],
                        wkva_bf[:, ci * 384 + 128 * oc:ci * 384 + 128 * oc + 128],
                        ones_b[:], start=(ci == 0), stop=(ci == 31))
            return last
        kb.op("t", wsums, inc="T")
        kb.wait("v", "T")
        kb.op("v", lambda v: v.tensor_copy(wsumc[:], wsum_ps[:]), inc="V")

        xg_free = {}
        for g_ in range(NTG):
            buf = xbf_ring[g_ % 2]
            kb.wait("g", "T", xg_free.get(g_ - 2, 0))
            dtgt = kb.op("g", lambda g, buf=buf, g_=g_: g.dma_start(
                buf[:].rearrange("p (ci f) -> p ci f", f=TG),
                xT[:, TG * g_:TG * g_ + TG].rearrange("(ci p) f -> p ci f", p=128)),
                inc="DG", incby=16)

            make_row_ops(
                lambda v, g_=g_: v.tensor_copy(ctmp4[:],
                                               statc[:, 8 * g_:8 * g_ + 8:2]),
                ctmp4[:], 4, tp4_ps, rowt4, mseg)
            def nrs_src(v, g_=g_):
                return v.tensor_scalar(ctmp4[:], statc[:, 8 * g_ + 1:8 * g_ + 9:2],
                                       -1.0, None, op0=ALU.mult)
            make_row_ops(nrs_src, ctmp4[:], 4, tp4_ps, rowt4, nrseg)

            kb.wait("t", "V")
            kb.wait("t", "DG")
            mtg = kb.op("t", lambda t: t.matmul(bc_ps[:], onesf_r[:], mseg[:],
                                                start=True, stop=True), inc="T")
            kb.wait("v", "T", mtg)
            kb.op("v", lambda v: v.tensor_copy(mbc[:], bc_ps[:]), inc="V")
            kb.wait("t", "V")
            mtg = kb.op("t", lambda t: t.matmul(bc_ps[:], onesf_r[:], nrseg[:],
                                                start=True, stop=True), inc="T")
            kb.wait("v", "T", mtg)
            kb.op("v", lambda v: v.tensor_copy(nrsbc[:], bc_ps[:]), inc="V")

            kb.wait("t", "DG", dtgt)
            kb.wait("t", "V")
            def kva(t, buf=buf):
                last = None
                for oc in range(3):
                    for ci in range(32):
                        last = t.matmul(
                            kv_ps[:, TG * oc:TG * oc + TG],
                            wkva_bf[:, ci * 384 + 128 * oc:ci * 384 + 128 * oc + 128],
                            buf[:, ci * TG:ci * TG + TG],
                            start=(ci == 0), stop=(ci == 31))
                return last
            mm_tgt = kb.op("t", kva, inc="T")
            xg_free[g_] = mm_tgt

            kb.wait("v", "T", mm_tgt)
            def corr(v):
                for oc in range(2):
                    v.scalar_tensor_tensor(ckv_f[:, TG * oc:TG * oc + TG], mbc[:],
                                           wsumc[:, oc:oc + 1],
                                           kv_ps[:, TG * oc:TG * oc + TG],
                                           op0=ALU.mult, op1=ALU.subtract)
                    v.tensor_mul(ckv_f[:, TG * oc:TG * oc + TG],
                                 ckv_f[:, TG * oc:TG * oc + TG], nrsbc[:])
                v.scalar_tensor_tensor(kpe_f[:], mbc[:], wsumc[:, 2:3],
                                       kv_ps[:, 2 * TG:3 * TG],
                                       op0=ALU.mult, op1=ALU.subtract)
                return v.tensor_mul(kpe_f[:], kpe_f[:], nrsbc[:])
            kb.op("v", corr, inc="V")

            kb.wait("s", "V")
            kb.op("s", lambda s: s.activation(sqb[:], ckv_f[:], AF.Square), inc="Sg")
            kb.wait("t", "Sg")
            def sumsq(t):
                t.matmul(st_ps[:], ones_b[:], sqb[:, 0:TG], start=True, stop=False)
                return t.matmul(st_ps[:], ones_b[:], sqb[:, TG:2 * TG],
                                start=False, stop=True)
            mtg = kb.op("t", sumsq, inc="T")
            kb.wait("s", "T", mtg)
            def rsc(s):
                s.activation(rrow[:], st_ps[:], AF.Ln,
                             bias=consts[0:1, C_EPSAT:C_EPSAT + 1],
                             scale=consts[0:1, C_I256:C_I256 + 1])
                return s.activation(rrow[:], rrow[:], AF.Exp, bias=0.0,
                                    scale=consts[0:1, C_NEGH:C_NEGH + 1])
            kb.op("s", rsc, inc="Sg")
            kb.wait("t", "Sg")
            mtg = kb.op("t", lambda t: t.matmul(bc_ps[:], onesf_r[:], rrow[:],
                                                start=True, stop=True), inc="T")
            kb.wait("v", "T", mtg)
            def ckvn(v):
                v.tensor_copy(rcbc[:], bc_ps[:])
                v.tensor_mul(ckv_f[:, 0:TG], ckv_f[:, 0:TG], rcbc[:])
                return v.tensor_mul(ckv_f[:, TG:2 * TG], ckv_f[:, TG:2 * TG],
                                    rcbc[:])
            kb.op("v", ckvn, inc="V")
            kb.wait("s", "V")
            def ckv_gamma(s, g_=g_):
                s.activation(ckv_n[:, 0 * S + TG * g_:0 * S + TG * g_ + TG],
                             ckv_f[:, 0:TG], AF.Copy, bias=0.0,
                             scale=kvln_sb[:, 0:1])
                return s.activation(ckv_n[:, 1 * S + TG * g_:1 * S + TG * g_ + TG],
                                    ckv_f[:, TG:2 * TG], AF.Copy, bias=0.0,
                                    scale=kvln_sb[:, 1:2])
            kb.op("s", ckv_gamma, inc="Sg")

            kb.wait("v", "DS", csk_tgt)
            def rope1(v, g_=g_):
                return v.tensor_tensor(w1[:], kpe_f[:],
                                       csk_sb[:, TG * g_:TG * g_ + TG], op=ALU.mult)
            kb.op("v", rope1, inc="V")
            kb.wait("t", "V")
            kb.wait("t", "Sg")
            mtg = kb.op("t", lambda t: t.matmul(fo_ps[:], fold_k[:], w1[:],
                                                start=True, stop=True), inc="T")
            kb.wait("s", "T", mtg)
            kb.op("s", lambda s, g_=g_: s.copy(kpe2[0:64, TG * g_:TG * g_ + TG],
                                               fo_ps[:]), inc="Sg")

        kb.wait("g", "Sg")
        kb.op("g", lambda g: g.dma_start(kpe2[64:128, :], kpe2[0:64, :]),
              inc="DG", incby=16)
        kb.wait("g", "DG")
        if debug:
            kb.op("g", lambda g: g.dma_start(d_ckvn[:], ckv_n[:]), inc="DG", incby=16)
            kb.op("g", lambda g: g.dma_start(d_kpe2[:], kpe2[:]), inc="DG", incby=16)
            kb.wait("g", "DG")
        kb.flush()

    with _ES() as loc:
        lsb = lambda n, s, d: loc.enter_context(nc.sbuf_tensor(n, s, d))
        lps = lambda n, s, d: loc.enter_context(nc.psum_tensor(n, s, d))
        kb.begin()
        xbfq = lsb("xbfq", [128, 32 * NQ], BF16)
        wqa_ring = [lsb(f"wqa{i}", [128, 32 * 128], BF16) for i in range(2)]
        q1_ps = lps("q1_ps", [128, 512], F32)
        ws_ps = lps("ws_ps", [128, 1], F32)
        st2_ps = lps("st2_ps", [1, 512], F32)
        bc2_ps = lps("bc2_ps", [128, 512], F32)
        tp8_ps = lps("tp8_ps", [8, 128], F32)
        mbcq = lsb("mbcq", [128, 512], F32)
        nbcq = lsb("nbcq", [128, 512], F32)
        rbcq = lsb("rbcq", [128, 512], F32)
        q1f = lsb("q1f", [128, 8 * 512], F32)
        sq2 = [lsb(f"sq2{i}", [128, 512], BF16) for i in range(2)]
        r2row = lsb("r2row", [1, 512], F32)
        wsq = lsb("wsq", [128, 8], F32)
        mqrow = lsb("mqrow", [1, NQ], F32)
        nrsqrow = lsb("nrsqrow", [1, NQ], F32)
        ctmp8 = lsb("ctmp8", [128, 8], F32)
        rowt8 = lsb("rowt8", [8, 128], F32)

        xq_tgt = kb.op("g", lambda g: g.dma_start(
            xbfq[:].rearrange("p (ci f) -> p ci f", f=NQ),
            xqT[:].rearrange("(ci p) f -> p ci f", p=128)), inc="DG", incby=16)

        make_row_ops(lambda v: v.tensor_copy(ctmp8[:], qm[:]),
                     ctmp8[:], 8, tp8_ps, rowt8, mqrow)
        make_row_ops(lambda v: v.tensor_scalar(ctmp8[:], qrs[:], -1.0,
                                               None, op0=ALU.mult),
                     ctmp8[:], 8, tp8_ps, rowt8, nrsqrow)

        for th in range(2):
            kb.wait("t", "V")
            kb.wait("t", "DG")
            mtg = kb.op("t", lambda t, th=th: t.matmul(
                bc2_ps[:], onesf_r[:], mqrow[:, 512 * th:512 * th + 512],
                start=True, stop=True), inc="T")
            kb.wait("v", "T", mtg)
            kb.op("v", lambda v: v.tensor_copy(mbcq[:], bc2_ps[:]), inc="V")
            kb.wait("t", "V")
            mtg = kb.op("t", lambda t, th=th: t.matmul(
                bc2_ps[:], onesf_r[:], nrsqrow[:, 512 * th:512 * th + 512],
                start=True, stop=True), inc="T")
            kb.wait("v", "T", mtg)
            kb.op("v", lambda v: v.tensor_copy(nbcq[:], bc2_ps[:]), inc="V")

            wq_free = {}
            for oc in range(8):
                wbuf = wqa_ring[oc % 2]
                kb.wait("g", "T", wq_free.get(oc - 2, 0))
                dtgt = kb.op("g", lambda g, wbuf=wbuf, oc=oc: g.dma_start(
                    wbuf[:].rearrange("p (ci f) -> p ci f", f=128),
                    wqaT[:, 128 * oc:128 * oc + 128]
                    .rearrange("(ci p) f -> p ci f", p=128)), inc="DG", incby=16)
                kb.wait("t", "DG", dtgt)
                kb.wait("t", "V")
                if th == 0:
                    def wsum_q(t, wbuf=wbuf):
                        last = None
                        for ci in range(32):
                            last = t.matmul(ws_ps[:],
                                            wbuf[:, 128 * ci:128 * ci + 128],
                                            ones_b[:], start=(ci == 0),
                                            stop=(ci == 31))
                        return last
                    kb.op("t", wsum_q, inc="T")
                    kb.wait("v", "T")
                    kb.op("v", lambda v, oc=oc: v.tensor_copy(wsq[:, oc:oc + 1],
                                                              ws_ps[:]), inc="V")
                    kb.wait("t", "V")
                kb.wait("t", "DG", xq_tgt)
                def q1mm(t, wbuf=wbuf, th=th):
                    last = None
                    for ci in range(32):
                        last = t.matmul(
                            q1_ps[:], wbuf[:, 128 * ci:128 * ci + 128],
                            xbfq[:, ci * NQ + 512 * th:ci * NQ + 512 * th + 512],
                            start=(ci == 0), stop=(ci == 31))
                    return last
                mm_tgt = kb.op("t", q1mm, inc="T")
                wq_free[oc] = mm_tgt
                kb.wait("v", "T", mm_tgt)
                def q1corr(v, oc=oc):
                    v.scalar_tensor_tensor(q1f[:, 512 * oc:512 * oc + 512], mbcq[:],
                                           wsq[:, oc:oc + 1], q1_ps[:],
                                           op0=ALU.mult, op1=ALU.subtract)
                    return v.tensor_mul(q1f[:, 512 * oc:512 * oc + 512],
                                        q1f[:, 512 * oc:512 * oc + 512], nbcq[:])
                kb.op("v", q1corr, inc="V")

            for oc in range(8):
                kb.wait("s", "V")
                kb.wait("s", "T")
                kb.op("s", lambda s, oc=oc: s.activation(
                    sq2[oc % 2][:], q1f[:, 512 * oc:512 * oc + 512], AF.Square),
                    inc="Sg")
                kb.wait("t", "Sg")
                kb.op("t", lambda t, oc=oc: t.matmul(
                    st2_ps[:], ones_b[:], sq2[oc % 2][:],
                    start=(oc == 0), stop=(oc == 7)), inc="T")
            kb.wait("s", "T")
            def rs2q(s):
                s.activation(r2row[:], st2_ps[:], AF.Ln,
                             bias=consts[0:1, C_EPSAT:C_EPSAT + 1],
                             scale=consts[0:1, C_I1024:C_I1024 + 1])
                return s.activation(r2row[:], r2row[:], AF.Exp, bias=0.0,
                                    scale=consts[0:1, C_NEGH:C_NEGH + 1])
            kb.op("s", rs2q, inc="Sg")
            kb.wait("v", "Sg")
            kb.op("v", lambda v: v.tensor_scalar(r2row[:], r2row[:], SOFTMAX_SCALE,
                                                 None, op0=ALU.mult), inc="V")
            kb.wait("t", "V")
            mtg = kb.op("t", lambda t: t.matmul(bc2_ps[:], onesf_r[:], r2row[:],
                                                start=True, stop=True), inc="T")
            kb.wait("v", "T", mtg)
            def q1n_mul(v):
                v.tensor_copy(rbcq[:], bc2_ps[:])
                last = None
                for oc in range(8):
                    last = v.tensor_mul(q1f[:, 512 * oc:512 * oc + 512],
                                        q1f[:, 512 * oc:512 * oc + 512], rbcq[:])
                return last
            kb.op("v", q1n_mul, inc="V")
            kb.wait("s", "V")
            def q1n_gam(s, th=th):
                last = None
                for oc in range(8):
                    last = s.activation(
                        q1n[:, oc * NQ + 512 * th:oc * NQ + 512 * th + 512],
                        q1f[:, 512 * oc:512 * oc + 512], AF.Copy, bias=0.0,
                        scale=qln_sb[:, oc:oc + 1])
                return last
            kb.op("s", q1n_gam, inc="Sg")
        kb.wait("t", "Sg")
        if debug:
            kb.wait("g", "Sg")
            kb.op("g", lambda g: g.dma_start(d_q1n[:], q1n[:]), inc="DG", incby=16)
            kb.wait("g", "DG")
        kb.flush()

    oh_scr = nc.dram_tensor("oh_scr", [D, NQ], BF16, kind="Internal")

    with _ES() as loc:
        lsb = lambda n, s, d: loc.enter_context(nc.sbuf_tensor(n, s, d))
        lps = lambda n, s, d: loc.enter_context(nc.psum_tensor(n, s, d))
        kb.begin()
        wbn_ring = [lsb(f"wbn{i}", [128, 2 * 128], BF16) for i in range(2)]
        wbv_ring = [lsb(f"wbv{i}", [128, 2 * 256], BF16) for i in range(2)]
        wqn_ring = [lsb(f"wqn{i}", [128, 8 * 128], BF16) for i in range(2)]
        wqp_ring = [lsb(f"wqp{i}", [128, 8 * 256], BF16) for i in range(2)]
        kf = [lsb(f"kf{i}", [128, S], BF16) for i in range(2)]
        vv = [lsb(f"vv{i}", [128, S], BF16) for i in range(2)]
        qf = [lsb(f"qf{i}", [128, NQ], BF16) for i in range(2)]
        qpe_f = lsb("qpe_f", [128, NQ], F32)
        w1q = lsb("w1q", [128, NQ], BF16)
        p_sb = [lsb(f"p_sb{i}", [128, S], BF16) for i in range(2)]
        ptbuf = [lsb(f"ptb{i}", [128, S], BF16) for i in range(2)]
        rowr = lsb("rowr", [8, 128], F32)
        ohf = lsb("ohf", [128, NQ], F32)
        rrow2 = lsb("rrow2", [1, NQ], F32)

        oh_hbuf = [lsb(f"ohb{i}", [128, NQ], BF16) for i in range(2)]
        arena = lps("arena", [128, 4096], F32)
        arena_b = arena[:].bitcast(BF16)
        pexp_gate = [0, 0]
        oh_st_tgt = [0, 0]

        def load_pair(pi, ridx):
            kb.op("g", lambda g, pi=pi, ridx=ridx: g.dma_start(
                wbn_ring[ridx][:].rearrange("p (cc f) -> p cc f", f=128),
                wkvbnT[:, 128 * pi:128 * pi + 128]
                .rearrange("(cc p) f -> p cc f", p=128)), inc="DG", incby=16)
            kb.op("g", lambda g, pi=pi, ridx=ridx: g.dma_start(
                wbv_ring[ridx][:].rearrange("p (cc f) -> p cc f", f=256),
                wkvbvT[:, 256 * pi:256 * pi + 256]
                .rearrange("(cc p) f -> p cc f", p=128)), inc="DG", incby=16)
            kb.op("g", lambda g, pi=pi, ridx=ridx: g.dma_start(
                wqn_ring[ridx][:].rearrange("p (ci f) -> p ci f", f=128),
                wqbnT[:, 128 * pi:128 * pi + 128]
                .rearrange("(ci p) f -> p ci f", p=128)), inc="DG", incby=16)
            return kb.op("g", lambda g, pi=pi, ridx=ridx: g.dma_start(
                wqp_ring[ridx][:].rearrange("p (ci f) -> p ci f", f=256),
                wqbpT[:, 256 * pi:256 * pi + 256]
                .rearrange("(ci p) f -> p ci f", p=128)), inc="DG", incby=16)

        pair_w_tgt = {0: load_pair(0, 0)}

        for pi in range(16):
            ridx = pi % 2
            kb.wait("t", "DG", pair_w_tgt[pi])

            for kc in range(8):
                kb.wait("t", "V")
                kb.wait("t", "Sg")
                def nope_mm(t, ridx=ridx, kc=kc):
                    t.matmul(arena[:, 0:512], wbn_ring[ridx][:, 0:128],
                             ckv_n[:, 0 * S + 512 * kc:0 * S + 512 * kc + 512],
                             start=True, stop=False)
                    return t.matmul(arena[:, 0:512], wbn_ring[ridx][:, 128:256],
                                    ckv_n[:, 1 * S + 512 * kc:1 * S + 512 * kc + 512],
                                    start=False, stop=True)
                mtg = kb.op("t", nope_mm, inc="T")
                kb.wait("v", "T", mtg)
                def nope_ev(v, kc=kc):
                    v.tensor_copy(kf[0][0:64, 512 * kc:512 * kc + 512],
                                  arena[0:64, 0:512])
                    return v.tensor_copy(kf[1][64:128, 512 * kc:512 * kc + 512],
                                         arena[64:128, 0:512])
                kb.op("v", nope_ev, inc="V")
            def kpe_cp(v):
                v.tensor_copy(kf[0][64:128, :], kpe2[64:128, :])
                return v.tensor_copy(kf[1][0:64, :], kpe2[0:64, :])
            kb.op("v", kpe_cp, inc="V")

            for kg in range(16):
                kb.wait("t", "V")
                def v_mm(t, ridx=ridx, kg=kg):
                    last = None
                    for u in range(2):
                        kt = 2 * kg + u
                        t.matmul(arena[:, 512 + 256 * u:512 + 256 * u + 256],
                                 ckv_n[:, 0 * S + 128 * kt:0 * S + 128 * kt + 128],
                                 wbv_ring[ridx][:, 0:256], start=True, stop=False)
                        last = t.matmul(
                            arena[:, 512 + 256 * u:512 + 256 * u + 256],
                            ckv_n[:, 1 * S + 128 * kt:1 * S + 128 * kt + 128],
                            wbv_ring[ridx][:, 256:512], start=False, stop=True)
                    return last
                mtg = kb.op("t", v_mm, inc="T")
                kb.wait("v", "T", mtg)
                def v_ev(v, kg=kg):
                    src = arena[:, 512:1024].rearrange("p (u hv) -> p u hv", hv=256)
                    v.tensor_copy(vv[0][:, 256 * kg:256 * kg + 256]
                                  .rearrange("p (u f) -> p u f", f=128),
                                  src[:, :, 0:128])
                    return v.tensor_copy(vv[1][:, 256 * kg:256 * kg + 256]
                                         .rearrange("p (u f) -> p u f", f=128),
                                         src[:, :, 128:256])
                kb.op("v", v_ev, inc="V")

            for th in range(2):
                kb.wait("t", "V")
                def qn_mm(t, ridx=ridx, th=th):
                    last = None
                    for ci in range(8):
                        last = t.matmul(
                            arena[:, 0:512],
                            wqn_ring[ridx][:, 128 * ci:128 * ci + 128],
                            q1n[:, ci * NQ + 512 * th:ci * NQ + 512 * th + 512],
                            start=(ci == 0), stop=(ci == 7))
                    return last
                mtg = kb.op("t", qn_mm, inc="T")
                kb.wait("v", "T", mtg)
                def qn_ev(v, th=th):
                    v.tensor_copy(qf[0][0:64, 512 * th:512 * th + 512],
                                  arena[0:64, 0:512])
                    return v.tensor_copy(qf[1][64:128, 512 * th:512 * th + 512],
                                         arena[64:128, 0:512])
                kb.op("v", qn_ev, inc="V")

            for hh in range(2):
                for th in range(2):
                    kb.wait("t", "V")
                    def qp_mm(t, ridx=ridx, hh=hh, th=th):
                        last = None
                        for ci in range(8):
                            last = t.matmul(
                                arena[:, 0:512],
                                wqp_ring[ridx][:, 256 * ci + 128 * hh:
                                               256 * ci + 128 * hh + 128],
                                q1n[:, ci * NQ + 512 * th:ci * NQ + 512 * th + 512],
                                start=(ci == 0), stop=(ci == 7))
                        return last
                    mtg = kb.op("t", qp_mm, inc="T")
                    kb.wait("v", "T", mtg)
                    kb.op("v", lambda v, th=th: v.tensor_copy(
                        qpe_f[:, 512 * th:512 * th + 512], arena[:, 0:512]),
                        inc="V")
                kb.op("v", lambda v: v.tensor_tensor(w1q[:], qpe_f[:], csq[:],
                                                     op=ALU.mult), inc="V")
                fold_m = fold_hi if hh == 0 else fold_lo
                for th in range(2):
                    kb.wait("t", "V")
                    kb.wait("t", "Sg")
                    mtg = kb.op("t", lambda t, fold_m=fold_m, th=th: t.matmul(
                        arena[:, 512:1024], fold_m[:],
                        w1q[:, 512 * th:512 * th + 512], start=True, stop=True),
                        inc="T")
                    kb.wait("s", "T", mtg)
                    def qpe_ev(s, hh=hh, th=th):
                        if hh == 0:
                            return s.copy(qf[0][64:128, 512 * th:512 * th + 512],
                                          arena[64:128, 512:1024])
                        return s.copy(qf[1][0:64, 512 * th:512 * th + 512],
                                      arena[0:64, 512:1024])
                    kb.op("s", qpe_ev, inc="Sg")

            if pi + 1 < 16:
                kb.wait("g", "T")
                pair_w_tgt[pi + 1] = load_pair(pi + 1, (pi + 1) % 2)

            for hh in range(2):
                h = 2 * pi + hh

                def emit_pv(pj, pnkt, pptg, hh=hh):
                    kb.wait("t", "DS", pptg)
                    kb.wait("t", "V")
                    def pv(t, hh=hh, pnkt=pnkt, pj=pj):
                        last = None
                        for kt in range(pnkt):
                            last = t.matmul(
                                arena[:, 3584:3712],
                                vv[hh][:, 128 * kt:128 * kt + 128],
                                ptbuf[pj % 2][:, 128 * kt:128 * kt + 128],
                                start=(kt == 0), stop=(kt == pnkt - 1))
                        return last
                    mtg2 = kb.op("t", pv, inc="T")
                    kb.wait("v", "T", mtg2)
                    kb.op("v", lambda v, pj=pj: v.tensor_copy(
                        ohf[:, 128 * pj:128 * pj + 128], arena[:, 3584:3712]),
                        inc="V")

                pend = None
                etg_prev = 0
                for j in range(NJ):
                    span = 512 * (j + 1)
                    if j == 7 and pend is not None:
                        emit_pv(*pend)
                        pend = None
                    if j == 0:
                        kb.wait("t", "V")
                        kb.wait("t", "Sg")
                    else:
                        kb.wait("t", "Sg", etg_prev)
                    def sc_mm(t, hh=hh, j=j):
                        last = None
                        for kc in range(j + 1):
                            last = t.matmul(arena[:, 512 * kc:512 * kc + 512],
                                            qf[hh][:, 128 * j:128 * j + 128],
                                            kf[hh][:, 512 * kc:512 * kc + 512],
                                            start=True, stop=True)
                        return last
                    mtg = kb.op("t", sc_mm, inc="T")
                    kb.wait("v", "T", mtg)
                    def mask_mx(v, j=j, span=span):
                        v.tensor_add(arena[:, 512 * j:512 * j + 512],
                                     arena[:, 512 * j:512 * j + 512], maskw[:])
                        return v.reduce_max(sumc[:, 8:9], arena[:, 0:span],
                                            axis=AX.X, negate=True)
                    vtg = kb.op("v", mask_mx, inc="V")
                    kb.wait("s", "V", vtg)
                    kb.wait("s", "DS", pexp_gate[j % 2])
                    def ex(s, span=span, j=j):
                        half = span // 2
                        for sec, (lo, hi) in enumerate([(0, half),
                                                        (half, span)]):
                            last = s.activation(
                                p_sb[j % 2][:, lo:hi], arena[:, lo:hi], AF.Exp,
                                bias=sumc[:, 8:9],
                                accum_out=sumc[:, 10 + sec:11 + sec])
                        return last
                    etg = kb.op("s", ex, inc="Sg")
                    kb.wait("v", "Sg", etg)
                    kb.op("v", lambda v, j=j: v.reduce_sum(
                        sumc[:, j:j + 1], sumc[:, 10:12], axis=AX.X), inc="V")
                    nkt_j = span // 128
                    kb.wait("y", "Sg", etg)
                    kb.wait("y", "T")
                    for kt in range(nkt_j):
                        ptg = kb.op("y", lambda y, kt=kt, j=j:
                                    y.dma_start_transpose(
                                        ptbuf[j % 2][:, 128 * kt:128 * kt + 128],
                                        p_sb[j % 2][:, 128 * kt:128 * kt + 128]),
                                    inc="DS", incby=16)
                    pexp_gate[j % 2] = ptg
                    if pend is not None:
                        emit_pv(*pend)
                    pend = (j, nkt_j, ptg)
                    etg_prev = etg
                if pend is not None:
                    emit_pv(*pend)

                kb.op("v", lambda v: v.reciprocal(sumc[:, 8:16], sumc[:, 0:8]),
                      inc="V")
                kb.wait("t", "V")
                kb.op("t", lambda t: t.transpose(arena[0:8, 0:128],
                                                 sumc[:, 8:16], i128_f[:]),
                      inc="T")
                kb.wait("v", "T")
                kb.op("v", lambda v: v.tensor_copy(rowr[:], arena[0:8, 0:128]),
                      inc="V")
                kb.wait("g", "V")
                fl = kb.op("g", lambda g: g.dma_start(
                    rrow2[:].rearrange("o (a b) -> o a b", a=8),
                    rowr[:]), inc="DG", incby=16)
                kb.wait("t", "DG", fl)
                def recb_outer(t):
                    last = None
                    for u in range(2):
                        last = t.matmul(arena[:, 512 * u:512 * u + 512], onesf_r[:],
                                        rrow2[:, 512 * u:512 * u + 512],
                                        start=True, stop=True)
                    return last
                rtg = kb.op("t", recb_outer, inc="T")
                kb.wait("v", "T", rtg)
                kb.wait("v", "DG", oh_st_tgt[h % 2])
                vtg3 = kb.op("v", lambda v, h=h: v.tensor_mul(
                    oh_hbuf[h % 2][:], ohf[:], arena[:, 0:1024]), inc="V")
                kb.wait("g", "V", vtg3)
                oh_st_tgt[h % 2] = kb.op("g", lambda g, h=h: g.dma_start(
                    oh_scr[128 * h:128 * h + 128, :], oh_hbuf[h % 2][:]),
                    inc="DG", incby=16)
            if debug and pi == 0:
                kb.wait("g", "V")
                kb.op("g", lambda g: g.dma_start(d_kf0[:], kf[0][:]), inc="DG", incby=16)
                kb.op("g", lambda g: g.dma_start(d_kf1[:], kf[1][:]), inc="DG", incby=16)
                kb.op("g", lambda g: g.dma_start(d_vv0[:], vv[0][:]), inc="DG", incby=16)
                kb.op("g", lambda g: g.dma_start(d_qf0[:], qf[0][:]), inc="DG", incby=16)
                kb.op("g", lambda g: g.dma_start(d_qf1[:], qf[1][:]), inc="DG", incby=16)
                kb.op("g", lambda g: g.dma_start(d_p0[:], p_sb[1][:]), inc="DG", incby=16)
                kb.op("g", lambda g: g.dma_start(d_sum[:], sumc[:]), inc="DG", incby=16)
                kb.wait("g", "DG")
                kb.wait("t", "DG")
                kb.wait("v", "DG")
                kb.wait("s", "DG")
        if debug:
            kb.wait("g", "DG")
            kb.op("g", lambda g: g.dma_start(
                d_oh[:].rearrange("p (h q) -> p h q", q=NQ),
                oh_scr[:].rearrange("(h p) q -> p h q", p=128)),
                inc="DG", incby=16)
            kb.wait("g", "DG")
        kb.flush()

    with _ES() as loc:
        lsb = lambda n, s, d: loc.enter_context(nc.sbuf_tensor(n, s, d))
        lps = lambda n, s, d: loc.enter_context(nc.psum_tensor(n, s, d))
        kb.begin()
        kb.wait("g", "DG")
        wo_ring = [lsb(f"wor{i}", [128, 512], BF16) for i in range(2)]
        oh_ring = [lsb(f"ohr{i}", [128, NQ], BF16) for i in range(2)]
        xq_ring = [lsb(f"xqr{i}", [128, NQ], BF16) for i in range(2)]
        wt_ring = [lsb(f"wtr{i}", [128, 512], F32) for i in range(2)]
        tern_ring = [lsb(f"ter{i}", [128, 512], BF16) for i in range(2)]
        sgn_a = lsb("sgn_a", [128, 512], BF16)
        xf_sb = lsb("xf_sb", [128, 4 * NQ], F32)
        acc = lsb("acc", [128, 4 * NQ], F32)
        ybuf = lsb("ybuf", [128, NQ], F32)
        a1bc = lsb("a1bc", [128, NQ], F32)
        bqbc = lsb("bqbc", [128, NQ], F32)
        divbc = lsb("divbc", [128, NQ], F32)
        a1qrow = lsb("a1qrow", [1, NQ], F32)
        bqrow = lsb("bqrow", [1, NQ], F32)
        divrow = lsb("divrow", [1, NQ], F32)
        ctmp8b = lsb("ctmp8b", [128, 8], F32)
        ctmp8c = lsb("ctmp8c", [128, 8], F32)
        rowt8b = lsb("rowt8b", [8, 128], F32)
        y_ps = lps("y_ps", [128, 4096], F32)

        make_row_ops(lambda v: v.tensor_scalar(ctmp8b[:], qrs[:], 1.0,
                                               None, op0=ALU.add),
                     ctmp8b[:], 8, y_ps[0:8, 0:128], rowt8b, a1qrow)
        def bqr_src(v):
            v.tensor_mul(ctmp8c[:], qm[:], qrs[:])
            v.tensor_copy(ctmp8b[:], consts[:, 0:8])
            return v.tensor_scalar(ctmp8b[:], ctmp8c[:], -1.0, None,
                                   op0=ALU.mult)
        make_row_ops(bqr_src, ctmp8b[:], 8, y_ps[0:8, 0:128], rowt8b, bqrow)
        make_row_ops(lambda v: v.tensor_copy(ctmp8b[:], qDIV[:]),
                     ctmp8b[:], 8, y_ps[0:8, 0:128], rowt8b, divrow)

        for rowsrc, dst in [(a1qrow, a1bc), (bqrow, bqbc), (divrow, divbc)]:
            for th in range(2):
                kb.wait("t", "V")
                kb.wait("t", "DG")
                mtg = kb.op("t", lambda t, rowsrc=rowsrc, th=th: t.matmul(
                    y_ps[:, 0:512], onesf_r[:], rowsrc[:, 512 * th:512 * th + 512],
                    start=True, stop=True), inc="T")
                kb.wait("v", "T", mtg)
                kb.op("v", lambda v, dst=dst, th=th: v.tensor_copy(
                    dst[:, 512 * th:512 * th + 512], y_ps[:, 0:512]), inc="V")

        for ocb in range(8):
            ci_mm = {}
            for ci in range(32):
                wbuf = wo_ring[ci % 2]
                kb.wait("g", "T", ci_mm.get(ci - 2, 0))
                dtg = kb.op("g", lambda g, ocb=ocb, ci=ci, wbuf=wbuf: g.dma_start(
                    wbuf[:],
                    woT[128 * ci:128 * ci + 128, 512 * ocb:512 * ocb + 512]),
                    inc="DG", incby=16)
                odt = kb.op("g", lambda g, ci=ci: g.dma_start(
                    oh_ring[ci % 2][:], oh_scr[128 * ci:128 * ci + 128, :]),
                    inc="DG", incby=16)
                kb.wait("t", "DG", odt)
                if ci == 0:
                    kb.wait("t", "V")
                def womm(t, ci=ci, wbuf=wbuf):
                    last = None
                    for oc in range(4):
                        for th in range(2):
                            last = t.matmul(
                                y_ps[:, 1024 * oc + 512 * th:
                                     1024 * oc + 512 * th + 512],
                                wbuf[:, 128 * oc:128 * oc + 128],
                                oh_ring[ci % 2][:, 512 * th:512 * th + 512],
                                start=(ci == 0), stop=(ci == 31))
                    return last
                ci_mm[ci] = kb.op("t", womm, inc="T")
            wot = ci_mm[31]
            kb.wait("y", "V")
            xtg = kb.op("y", lambda y, ocb=ocb: y.dma_start(
                xf_sb[:].rearrange("p (oc q) -> p oc q", q=NQ),
                xqT[512 * ocb:512 * ocb + 512, :]
                .rearrange("(oc p) q -> p oc q", p=128)), inc="DS", incby=16)
            kb.wait("v", "T", wot)
            kb.wait("v", "DS", xtg)
            def accb(v):
                last = None
                for oc in range(4):
                    v.tensor_add(acc[:, NQ * oc:NQ * oc + NQ],
                                 y_ps[:, 1024 * oc:1024 * oc + NQ], bqbc[:])
                    v.tensor_mul(ybuf[:], xf_sb[:, NQ * oc:NQ * oc + NQ], a1bc[:])
                    last = v.tensor_add(acc[:, NQ * oc:NQ * oc + NQ],
                                        acc[:, NQ * oc:NQ * oc + NQ], ybuf[:])
                return last
            kb.op("v", accb, inc="V")
            if debug and ocb == 0:
                kb.wait("y", "V")
                kb.op("y", lambda y: y.dma_start(d_acc1[:], acc[:, 0:NQ]),
                      inc="DS", incby=16)
                kb.op("y", lambda y: y.dma_start(d_rows3[:, 0:NQ], a1qrow[:]),
                      inc="DS", incby=16)
                kb.op("y", lambda y: y.dma_start(d_rows3[:, NQ:2 * NQ],
                                                 bqrow[:]), inc="DS", incby=16)
                kb.op("y", lambda y: y.dma_start(d_rows3[:, 2 * NQ:3 * NQ],
                                                 divrow[:]), inc="DS", incby=16)
                kb.wait("v", "DS")

            ff_mm = {}
            for ci in range(32):
                wbuf = wt_ring[ci % 2]
                tbuf = tern_ring[ci % 2]
                kb.wait("y", "Sg")
                dtg = kb.op("y", lambda y, ocb=ocb, ci=ci, wbuf=wbuf: y.dma_start(
                    wbuf[:],
                    WT[128 * ci:128 * ci + 128, 512 * ocb:512 * ocb + 512]),
                    inc="DS", incby=16)
                kb.wait("g", "T", ff_mm.get(ci - 2, 0))
                xdt = kb.op("g", lambda g, ci=ci: g.dma_start(
                    xq_ring[ci % 2][:], xqf_scr[128 * ci:128 * ci + 128, :]),
                    inc="DG", incby=16)
                kb.wait("s", "DS", dtg)
                kb.wait("s", "T", ff_mm.get(ci - 2, 0))
                kb.wait("s", "V")
                def sgn(s, wbuf=wbuf, tbuf=tbuf):
                    s.activation(sgn_a[:], wbuf[:], AF.Sign,
                                 bias=consts[:, C_BA2:C_BA2 + 1],
                                 scale=swcol[:, 0:1])
                    return s.activation(tbuf[:], wbuf[:], AF.Sign,
                                        bias=consts[:, C_BB2:C_BB2 + 1],
                                        scale=swcol[:, 0:1])
                stg = kb.op("s", sgn, inc="Sg")
                kb.wait("v", "Sg", stg)
                vtg = kb.op("v", lambda v, tbuf=tbuf: v.tensor_add(
                    tbuf[:], tbuf[:], sgn_a[:]), inc="V")
                kb.wait("t", "V", vtg)
                kb.wait("t", "DG", xdt)
                if ci == 0:
                    kb.wait("t", "V")
                def ffmm(t, ci=ci, tbuf=tbuf):
                    last = None
                    for oc in range(4):
                        for th in range(2):
                            last = t.matmul(
                                y_ps[:, 1024 * oc + 512 * th:
                                     1024 * oc + 512 * th + 512],
                                tbuf[:, 128 * oc:128 * oc + 128],
                                xq_ring[ci % 2][:, 512 * th:512 * th + 512],
                                start=(ci == 0), stop=(ci == 31))
                    return last
                ff_mm[ci] = kb.op("t", ffmm, inc="T")
            fft = ff_mm[31]
            kb.wait("v", "T", fft)
            for oc in range(4):
                vtg = kb.op("v", lambda v, oc=oc: v.tensor_mul(
                    ybuf[:], y_ps[:, 1024 * oc:1024 * oc + NQ], divbc[:]), inc="V")
                kb.wait("s", "V", vtg)
                stg = kb.op("s", lambda s, ocb=ocb, oc=oc: s.activation(
                    ybuf[:], ybuf[:], AF.Gelu,
                    bias=bq_sb[:, 4 * ocb + oc:4 * ocb + oc + 1], scale=1.0),
                    inc="Sg")
                kb.wait("v", "Sg", stg)
                if debug and ocb == 0 and oc == 0:
                    kb.wait("y", "Sg", stg)
                    kb.op("y", lambda y: y.dma_start(d_ffny[:], ybuf[:]),
                          inc="DS", incby=16)
                    kb.wait("v", "DS")
                vtg2 = kb.op("v", lambda v, oc=oc: v.tensor_add(
                    acc[:, NQ * oc:NQ * oc + NQ], acc[:, NQ * oc:NQ * oc + NQ],
                    ybuf[:]), inc="V")
                kb.wait("y", "V", vtg2)
                kb.op("y", lambda y, ocb=ocb, oc=oc: y.dma_start(
                    out[128 * (4 * ocb + oc):128 * (4 * ocb + oc) + 128, :],
                    acc[:, NQ * oc:NQ * oc + NQ]), inc="DS", incby=16)
        kb.wait("y", "DS")
        kb.flush()

    stk.close()
    return nc


_PROG = {}


def _get_prog():
    if "nc" not in _PROG:
        _PROG["nc"] = build_program()
    return _PROG["nc"]


def kernel(L0_attn_x, L0_attn_wq_a, L0_attn_q_a_ln_w, L0_attn_wq_b, L0_attn_wkv_a,
           L0_attn_kv_a_ln_w, L0_attn_wkv_b, L0_attn_wo, L0_ffn_W, L0_ffn_b):
    x = np.asarray(L0_attn_x, dtype=np.float32)
    wq_a = np.asarray(L0_attn_wq_a, dtype=np.float32)
    q_a_ln = np.asarray(L0_attn_q_a_ln_w, dtype=np.float32)
    wq_b = np.asarray(L0_attn_wq_b, dtype=np.float32)
    wkv_a = np.asarray(L0_attn_wkv_a, dtype=np.float32)
    kv_a_ln = np.asarray(L0_attn_kv_a_ln_w, dtype=np.float32)
    wkv_b = np.asarray(L0_attn_wkv_b, dtype=np.float32)
    wo = np.asarray(L0_attn_wo, dtype=np.float32)
    W = np.asarray(L0_ffn_W, dtype=np.float32)
    bias = np.asarray(L0_ffn_b, dtype=np.float32)

    C = np.ascontiguousarray
    deint = np.concatenate([np.arange(0, 64, 2), np.arange(1, 64, 2)])
    rotp = np.concatenate([np.arange(32, 64), np.arange(0, 32)])

    idx_n = np.concatenate([h * 128 + np.arange(64) for h in range(H)])
    wqbnT = C(wq_b[idx_n].T)
    idx_p = []
    for h in range(H):
        dr = h * 128 + 64 + deint
        idx_p.append(dr)
        idx_p.append(dr[rotp])
    idx_p = np.concatenate(idx_p)
    wqbpT = C(wq_b[idx_p].T)

    kpe_rows = 256 + deint
    wkvaT = C(np.concatenate([wkv_a[0:256], wkv_a[kpe_rows],
                              wkv_a[kpe_rows][rotp]], axis=0).T)
    idx_kn = np.concatenate([h * 192 + np.arange(64) for h in range(H)])
    wkvbnT = C(wkv_b[idx_kn].T)
    idx_kv = np.concatenate([h * 192 + 64 + np.arange(128) for h in range(H)])
    wkvbvT = C(wkv_b[idx_kv].T)

    wqaT = C(wq_a.T)
    woT = C(wo.T)
    WT = C(W.T)

    qlnv = C(q_a_ln.reshape(8, 128).T)
    kvlnv = C(kv_a_ln.reshape(2, 128).T)
    bqvv = C(bias.reshape(32, 128).T)

    cos, sin = _yarn_cos_sin(S)
    sgnv = np.concatenate([-np.ones(32, np.float32), np.ones(32, np.float32)])
    ssin = sin * sgnv[None, :]
    CSKv = C(np.concatenate([cos.T, ssin.T], axis=0))

    FOLD_Knp = np.zeros((128, 64), np.float32)
    for jj in range(64):
        FOLD_Knp[jj, jj] = 1.0
        FOLD_Knp[jj + 64, jj] = 1.0
    FOLD_LOnp = np.zeros((128, 128), np.float32)
    FOLD_LOnp[:, 0:64] = FOLD_Knp
    FOLD_HInp = np.zeros((128, 128), np.float32)
    FOLD_HInp[:, 64:128] = FOLD_Knp

    constsv = np.zeros((128, 8), np.float32)
    constsv[:, C_EPSLN] = EPS_LN
    constsv[:, C_EPSAT] = EPS_ATTN
    constsv[:, C_NEGH] = -0.5
    constsv[:, C_I1024] = 1.0 / 1024.0
    constsv[:, C_I256] = 1.0 / 256.0
    constsv[:, C_I4096] = 1.0 / 4096.0
    constsv[:, C_BA2] = -0.5
    constsv[:, C_BB2] = 0.5

    bf = ml_dtypes.bfloat16
    shared = dict(
        wqaT=wqaT, wqbnT=wqbnT, wqbpT=wqbpT, wkvaT=wkvaT, wkvbnT=wkvbnT,
        wkvbvT=wkvbvT, woT=woT, WT=WT, qln=qlnv, kvln=kvlnv, bqv=bqvv, CSK=CSKv,
        FOLD_K=FOLD_Knp.astype(bf), FOLD_LO=FOLD_LOnp.astype(bf),
        FOLD_HI=FOLD_HInp.astype(bf),
        I128B=np.eye(128, dtype=np.float32).astype(bf),
        I128F=np.eye(128, dtype=np.float32),
        ONESB=np.ones((128, 1), np.float32).astype(bf),
        ONESFR=np.ones((1, 128), np.float32),
        ONESFC=np.ones((128, 1), np.float32),
        CONSTS=constsv,
    )

    in_maps = []
    xT_b = [C(x[b_].T) for b_ in range(B)]
    for core in range(8):
        b_, c_ = core // 4, core % 4
        qp = _qpos(c_)
        mw = np.where(np.arange(512)[None, :] <= 128 * c_ + np.arange(128)[:, None],
                      0.0, NEG).astype(np.float32)
        m = dict(shared)
        m.update(xT=xT_b[b_], x_tm=x[b_], xq_tm=C(x[b_][qp]),
                 xqT=C(xT_b[b_][:, qp]), CSQ=C(CSKv[:, qp]), MASKW=mw)
        in_maps.append(m)

    nc = _get_prog()
    res = run_bass_kernel_spmd(nc, in_maps, core_ids=list(range(8)))

    outf = np.zeros((B, S, D), dtype=np.float32)
    for core in range(8):
        b_, c_ = core // 4, core % 4
        qp = _qpos(c_)
        outf[b_, qp, :] = np.asarray(res.results[core]["out"]).T
    return outf


# version 10
# speedup vs baseline: 1.3025x; 1.3025x over previous
import math
import numpy as np
import ml_dtypes
import concourse.bass as bass
import concourse.mybir as mybir
from concourse.bass_utils import run_bass_kernel_spmd

F32 = mybir.dt.float32
BF16 = mybir.dt.bfloat16
AF = mybir.ActivationFunctionType
ALU = mybir.AluOpType
AX = mybir.AxisListType

B, S, D = 2, 4096, 4096
H, NOPE, ROPE, VD, KVR = 32, 64, 64, 128, 256
NQ = 1024
NJ = 8
TG = 512
NTG = S // TG
EPS_LN = 1e-5
EPS_ATTN = 1e-6
NEG = -1.0e30
_mm_ = 0.1 * 1.0 * math.log(128.0) + 1.0
SOFTMAX_SCALE = (128.0 ** -0.5) * _mm_ * _mm_

C_EPSLN, C_EPSAT, C_NEGH, C_I1024, C_I256, C_I4096, C_BA2, C_BB2 = range(8)


def _yarn_cos_sin(seq_len, dim=ROPE, base=10000.0, factor=128.0, beta_fast=32.0,
                  beta_slow=1.0, orig_max=8192, mscale=1.0, mscale_all_dim=1.0):
    def corr_dim(r):
        return dim * math.log(orig_max / (r * 2 * math.pi)) / (2 * math.log(base))
    low = max(math.floor(corr_dim(beta_fast)), 0)
    high = min(math.ceil(corr_dim(beta_slow)), dim - 1)
    hi = high + 0.001 if low == high else float(high)
    ramp = np.clip((np.arange(dim // 2, dtype=np.float32) - low) / (hi - low), 0.0, 1.0)
    inv_freq_mask = 1.0 - ramp
    freq_extra = 1.0 / base ** (np.arange(0, dim, 2, dtype=np.float32) / dim)
    freq_inter = freq_extra / factor
    inv_freq = freq_inter * (1.0 - inv_freq_mask) + freq_extra * inv_freq_mask
    t = np.arange(seq_len, dtype=np.float32)
    freqs = np.outer(t, inv_freq)
    emb = np.concatenate([freqs, freqs], axis=-1)
    def gm(s, m):
        return 1.0 if s <= 1 else 0.1 * m * math.log(s) + 1.0
    ms = gm(factor, mscale) / gm(factor, mscale_all_dim)
    return (np.cos(emb) * ms).astype(np.float32), (np.sin(emb) * ms).astype(np.float32)


def _qpos(c):
    return np.concatenate([512 * j + 128 * c + np.arange(128) for j in range(NJ)])


class KB:
    def __init__(self, nc):
        self.nc = nc
        self.sems = {}
        self.cnt = {}
        self.q = None

    def mksem(self, stk, *names):
        for n in names:
            self.sems[n] = stk.enter_context(self.nc.semaphore(n))
            self.cnt[n] = 0

    def begin(self):
        self.q = {e: [] for e in "tvsgy"}

    def wait(self, eng, sname, val=None):
        v = self.cnt[sname] if val is None else val
        if v > 0:
            h = self.sems[sname]
            self.q[eng].append(lambda e, h=h, v=v: e.wait_ge(h, v))
        return v

    def op(self, eng, fn, inc=None, incby=1):
        if inc is not None:
            h = self.sems[inc]
            self.cnt[inc] += incby
            self.q[eng].append(lambda e, fn=fn, h=h, k=incby: fn(e).then_inc(h, k))
            return self.cnt[inc]
        self.q[eng].append(fn)
        return None

    def flush(self):
        qq = self.q
        with self.nc.Block() as blk:
            if qq["y"]:
                @blk.sync
                def _(e, _q=qq["y"]):
                    for f in _q:
                        f(e)
            if qq["g"]:
                @blk.gpsimd
                def _(e, _q=qq["g"]):
                    for f in _q:
                        f(e)
            if qq["t"]:
                @blk.tensor
                def _(e, _q=qq["t"]):
                    for f in _q:
                        f(e)
            if qq["v"]:
                @blk.vector
                def _(e, _q=qq["v"]):
                    for f in _q:
                        f(e)
            if qq["s"]:
                @blk.scalar
                def _(e, _q=qq["s"]):
                    for f in _q:
                        f(e)
        self.q = None


def build_program(debug=False):
    from contextlib import ExitStack
    nc = bass.Bass("TRN2", target_bir_lowering=False)

    din = lambda n, s, d=F32: nc.dram_tensor(n, s, d, kind="ExternalInput")
    xT = din("xT", [D, S])
    x_tm = din("x_tm", [S, D])
    xq_tm = din("xq_tm", [NQ, D])
    xqT = din("xqT", [D, NQ])
    wqaT = din("wqaT", [D, 1024])
    wqbnT = din("wqbnT", [1024, 2048])
    wqbpT = din("wqbpT", [1024, 4096])
    wkvaT = din("wkvaT", [D, 384])
    wkvbnT = din("wkvbnT", [KVR, 2048])
    wkvbvT = din("wkvbvT", [KVR, 4096])
    woT = din("woT", [D, D])
    WT = din("WT", [D, D])
    qln = din("qln", [128, 8])
    kvln = din("kvln", [128, 2])
    bqv = din("bqv", [128, 32])
    CSK = din("CSK", [128, S])
    CSQ = din("CSQ", [128, NQ])
    MASKW = din("MASKW", [128, 512])
    FOLD_K = din("FOLD_K", [128, 64], BF16)
    FOLD_LO = din("FOLD_LO", [128, 128], BF16)
    FOLD_HI = din("FOLD_HI", [128, 128], BF16)
    I128B = din("I128B", [128, 128], BF16)
    I128F = din("I128F", [128, 128])
    ONESB = din("ONESB", [128, 1], BF16)
    ONESFR = din("ONESFR", [1, 128])
    ONESFC = din("ONESFC", [128, 1])
    CONSTS = din("CONSTS", [128, 8])

    xqf_scr = nc.dram_tensor("xqf_scr", [D, NQ], BF16,
                             kind="ExternalOutput" if debug else "Internal")
    out = nc.dram_tensor("out", [D, NQ], F32, kind="ExternalOutput")
    if debug:
        d_statc = nc.dram_tensor("d_statc", [128, 96], F32, kind="ExternalOutput")
        d_qstat = nc.dram_tensor("d_qstat", [128, 96], F32, kind="ExternalOutput")
        d_sw = nc.dram_tensor("d_sw", [128, 2], F32, kind="ExternalOutput")
        d_ckvn = nc.dram_tensor("d_ckvn", [128, 2 * S], BF16, kind="ExternalOutput")
        d_kpe2 = nc.dram_tensor("d_kpe2", [128, S], BF16, kind="ExternalOutput")
        d_q1n = nc.dram_tensor("d_q1n", [128, 8 * NQ], BF16, kind="ExternalOutput")
        d_oh = nc.dram_tensor("d_oh", [128, H * NQ], BF16, kind="ExternalOutput")
        d_kf0 = nc.dram_tensor("d_kf0", [128, S], BF16, kind="ExternalOutput")
        d_kf1 = nc.dram_tensor("d_kf1", [128, S], BF16, kind="ExternalOutput")
        d_vv0 = nc.dram_tensor("d_vv0", [128, S], BF16, kind="ExternalOutput")
        d_qf0 = nc.dram_tensor("d_qf0", [128, NQ], BF16, kind="ExternalOutput")
        d_qf1 = nc.dram_tensor("d_qf1", [128, NQ], BF16, kind="ExternalOutput")
        d_p0 = nc.dram_tensor("d_p0", [128, S], BF16, kind="ExternalOutput")
        d_sum = nc.dram_tensor("d_sum", [128, 16], F32, kind="ExternalOutput")
        d_acc1 = nc.dram_tensor("d_acc1", [128, NQ], F32, kind="ExternalOutput")
        d_ffny = nc.dram_tensor("d_ffny", [128, NQ], F32, kind="ExternalOutput")
        d_rows3 = nc.dram_tensor("d_rows3", [1, 3 * NQ], F32, kind="ExternalOutput")
        d_p1 = nc.dram_tensor("d_p1", [128, 1024], BF16, kind="ExternalOutput")

    from contextlib import ExitStack as _ES
    stk = _ES()
    sb = lambda n, s, d: stk.enter_context(nc.sbuf_tensor(n, s, d))

    kb = KB(nc)
    kb.mksem(stk, "DS", "DG", "T", "V", "Sg", "Ya", "Yb", "Yc", "Ga", "Gb",
             "Gc", "Gd", "Ge", "Gf", "Gg", "Gh", "Gi")

    consts = sb("consts", [128, 8], F32)
    ones_b = sb("ones_b", [128, 1], BF16)
    onesf_r = sb("onesf_r", [1, 128], F32)
    onesf_c = sb("onesf_c", [128, 1], F32)
    i128_b = sb("i128_b", [128, 128], BF16)
    i128_f = sb("i128_f", [128, 128], F32)
    fold_k = sb("fold_k", [128, 64], BF16)
    fold_lo = sb("fold_lo", [128, 128], BF16)
    fold_hi = sb("fold_hi", [128, 128], BF16)
    maskw = sb("maskw", [128, 512], F32)
    csq = sb("csq", [128, NQ], F32)
    qln_sb = sb("qln_sb", [128, 8], F32)
    kvln_sb = sb("kvln_sb", [128, 2], F32)
    bq_sb = sb("bq_sb", [128, 32], F32)

    statc = sb("statc", [128, 96], F32)
    qsum = sb("qsum", [128, 8], F32)
    qss = sb("qss", [128, 8], F32)
    qmx = sb("qmx", [128, 8], F32)
    qmn = sb("qmn", [128, 8], F32)
    qvar = sb("qvar", [128, 8], F32)
    wc = sb("wc", [128, 8], F32)
    wc2 = sb("wc2", [128, 8], F32)
    wc3 = sb("wc3", [128, 8], F32)
    wc4 = sb("wc4", [128, 8], F32)
    sumc = sb("sumc", [128, 16], F32)
    qm = sb("qm", [128, 8], F32)
    qrs = sb("qrs", [128, 8], F32)
    qrs2 = sb("qrs2", [128, 8], F32)
    qR = sb("qR", [128, 8], F32)
    qDIV = sb("qDIV", [128, 8], F32)
    nmR = sb("nmR", [128, 8], F32)
    nmRp = sb("nmRp", [128, 8], F32)
    nmRh = sb("nmRh", [128, 8], F32)
    swcol = sb("swcol", [128, 8], F32)
    winv = sb("winv", [128, 8], F32)

    ckv_n = sb("ckv_n", [128, 2 * S], BF16)
    kpe2 = sb("kpe2", [128, S], BF16)
    q1n = sb("q1n", [128, 8 * NQ], BF16)

    def make_row_ops(colsrc_fn, ctmp_ap, ncols, tp_ps, rowtmp, dstrow):
        kb.wait("v", "Sg")
        kb.wait("v", "DG")
        kb.op("v", colsrc_fn, inc="V")
        kb.wait("t", "V")
        kb.op("t", lambda t: t.transpose(tp_ps[0:ncols, 0:128], ctmp_ap, i128_f[:]),
              inc="T")
        kb.wait("v", "T")
        kb.op("v", lambda v: v.tensor_copy(rowtmp[0:ncols, :], tp_ps[0:ncols, 0:128]),
              inc="V")
        kb.wait("g", "V")
        kb.wait("g", "T")
        kb.op("g", lambda g: g.dma_start(
            dstrow[:].rearrange("o (a b) -> o a b", a=ncols),
            rowtmp[0:ncols, :]), inc="DG", incby=16)

    kb.begin()
    for dst_, src_ in [(consts, CONSTS), (maskw, MASKW), (csq, CSQ),
                       (qln_sb, qln), (kvln_sb, kvln), (bq_sb, bqv),
                       (onesf_r, ONESFR), (onesf_c, ONESFC), (i128_f, I128F),
                       (ones_b, ONESB), (i128_b, I128B), (fold_k, FOLD_K),
                       (fold_lo, FOLD_LO), (fold_hi, FOLD_HI)]:
        kb.op("y", lambda y, d=dst_, s=src_: y.dma_start(d[:], s[:]),
              inc="DS", incby=16)
    kb.wait("y", "DS")
    kb.flush()

    with _ES() as loc:
        lsb = lambda n, s, d: loc.enter_context(nc.sbuf_tensor(n, s, d))
        lps = lambda n, s, d: loc.enter_context(nc.psum_tensor(n, s, d))
        kb.begin()
        xtm_ring = [lsb(f"xtm{i}", [128, D], F32) for i in range(2)]
        sq_scr = lsb("sq_scr", [128, D], BF16)
        sxc = lsb("sxc", [128, 32], F32)
        sxxc = lsb("sxxc", [128, 32], F32)
        wt_ring = [lsb(f"wtp{i}", [128, D], F32) for i in range(2)]
        wabs = lsb("wabs", [128, 32], F32)
        ones8f = lsb("ones8f", [128, 8], F32)
        ones128 = lsb("ones128", [128, 128], F32)
        bc8_ps = lps("bc8_ps", [128, 8], F32)

        kb.op("v", lambda v: v.memset(ones128[:], 1.0), inc="V")
        v_done = {}
        s_done = {}
        for ti in range(32):
            buf = xtm_ring[ti % 2]
            kb.wait("y", "V", v_done.get(ti - 2, 0))
            kb.wait("y", "Sg", s_done.get(ti - 2, 0))
            sm = ("Ya", "Yb")[ti % 2]
            dtgt = kb.op("y", lambda y, buf=buf, ti=ti:
                         y.dma_start(buf[:], x_tm[128 * ti:128 * ti + 128, :]),
                         inc=sm, incby=16)
            kb.wait("v", sm, dtgt)
            v_done[ti] = kb.op("v", lambda v, buf=buf, ti=ti:
                               v.reduce_sum(sxc[:, ti:ti + 1], buf[:], axis=AX.X),
                               inc="V")
            kb.wait("s", sm, dtgt)
            s_done[ti] = kb.op("s", lambda s, buf=buf, ti=ti:
                               s.activation(sq_scr[:], buf[:], AF.Square,
                                            accum_out=sxxc[:, ti:ti + 1]),
                               inc="Sg")

        kb.wait("v", "V")
        kb.wait("v", "Sg")
        def derive_stats(v):
            v.tensor_scalar(statc[:, 0:64:2], sxc[:, 0:32], 1.0 / D, None,
                            op0=ALU.mult)
            v.tensor_scalar(statc[:, 1:64:2], sxxc[:, 0:32], 1.0 / D, None,
                            op0=ALU.mult)
            v.tensor_mul(statc[:, 64:96], statc[:, 0:64:2], statc[:, 0:64:2])
            return v.tensor_sub(statc[:, 1:64:2], statc[:, 1:64:2], statc[:, 64:96])
        kb.op("v", derive_stats, inc="V")
        kb.wait("s", "V")
        def rs_from_var(s):
            s.activation(statc[:, 1:64:2], statc[:, 1:64:2], AF.Ln,
                         bias=consts[:, C_EPSLN:C_EPSLN + 1], scale=1.0)
            return s.activation(statc[:, 1:64:2], statc[:, 1:64:2], AF.Exp,
                                bias=0.0, scale=consts[:, C_NEGH:C_NEGH + 1])
        kb.op("s", rs_from_var, inc="Sg")

        wv_done = {}
        for wi in range(32):
            buf = wt_ring[wi % 2]
            kb.wait("y", "V", wv_done.get(wi - 2, 0))
            sm = ("Ya", "Yb")[wi % 2]
            dtgt = kb.op("y", lambda y, buf=buf, wi=wi:
                         y.dma_start(buf[:], WT[128 * wi:128 * wi + 128, :]),
                         inc=sm, incby=16)
            kb.wait("v", sm, dtgt)
            wv_done[wi] = kb.op("v", lambda v, buf=buf, wi=wi:
                                v.reduce_sum(wabs[:, wi:wi + 1], buf[:], axis=AX.X,
                                             apply_absolute_value=True), inc="V")
        kb.wait("v", "V")
        def wsum_fin(v):
            v.reduce_sum(wabs[:, 0:1], wabs[:, 0:32], axis=AX.X)
            v.tensor_copy(wc2[:], consts[:])
            return v.tensor_scalar(ones8f[:], ones128[:, 0:8], wabs[:, 0:1],
                                   None, op0=ALU.mult)
        kb.op("v", wsum_fin, inc="V")
        kb.wait("t", "V")
        kb.op("t", lambda t: t.matmul(bc8_ps[:], ones128[:], ones8f[:],
                                      start=True, stop=True), inc="T")
        kb.wait("v", "T")
        def sw_fin(v):
            v.tensor_scalar(wc3[:], bc8_ps[:], 1.0 / (4096.0 * 4096.0), None,
                            op0=ALU.mult)
            v.tensor_copy(wc2[:], consts[:])
            v.tensor_scalar(winv[:], wc3[:], 1e-5, None, op0=ALU.max)
            v.tensor_copy(wc2[:], consts[:])
            return v.reciprocal(swcol[:], winv[:])
        kb.op("v", sw_fin, inc="V")

        kb.flush()

    with _ES() as loc:
        lsb = lambda n, s, d: loc.enter_context(nc.sbuf_tensor(n, s, d))
        lps = lambda n, s, d: loc.enter_context(nc.psum_tensor(n, s, d))
        kb.begin()
        kb.wait("y", "V")
        kb.wait("y", "Sg")
        kb.wait("y", "T")
        tp_b = lps("tp_b", [128, 1024], BF16)
        stage = lsb("stage", [128, 1024], BF16)
        qxr = [lsb(f"qxr{i}", [128, D], F32) for i in range(2)]
        sqf = lsb("sqf", [128, D], F32)
        ngf = lsb("ngf", [128, D], F32)
        wbuf = lsb("wbuf", [128, D], F32)
        bmask = lsb("bmask", [128, D], F32)
        xq_t = lsb("xq_t", [128, D], BF16)

        qv_done = {}
        for jt in range(NJ):
            buf = qxr[jt % 2]
            kb.wait("y", "V", qv_done.get(jt - 2, 0))
            sm = ("Ya", "Yb")[jt % 2]
            dtgt = kb.op("y", lambda y, buf=buf, jt=jt:
                         y.dma_start(buf[:], xq_tm[128 * jt:128 * jt + 128, :]),
                         inc=sm, incby=16)
            kb.wait("v", sm, dtgt)
            def qredu(v, buf=buf, jt=jt):
                v.tensor_mul(sqf[:], buf[:], buf[:])
                v.tensor_scalar(ngf[:], buf[:], -1.0, None, op0=ALU.mult)
                v.reduce_sum(qss[:, jt:jt + 1], sqf[:], axis=AX.X)
                v.reduce_max(qmn[:, jt:jt + 1], ngf[:], axis=AX.X)
                v.reduce_sum(qsum[:, jt:jt + 1], buf[:], axis=AX.X)
                return v.reduce_max(qmx[:, jt:jt + 1], buf[:], axis=AX.X)
            qv_done[jt] = kb.op("v", qredu, inc="V")

        kb.wait("v", "V")
        def qderive(v):
            v.tensor_scalar(qm[:], qsum[:], 1.0 / D, None, op0=ALU.mult)
            return v.tensor_scalar(wc[:], qss[:], 1.0 / D, None, op0=ALU.mult)
        kb.op("v", qderive, inc="V")
        kb.wait("s", "V")
        kb.op("s", lambda s: s.activation(wc2[:], qm[:], AF.Square), inc="Sg")
        kb.wait("v", "Sg")
        kb.op("v", lambda v: v.tensor_sub(qvar[:], wc[:], wc2[:]), inc="V")
        kb.wait("s", "V")
        def qrs_op(s):
            s.activation(qrs[:], qvar[:], AF.Ln,
                         bias=consts[:, C_EPSLN:C_EPSLN + 1], scale=1.0)
            return s.activation(qrs[:], qrs[:], AF.Exp, bias=0.0,
                                scale=consts[:, C_NEGH:C_NEGH + 1])
        kb.op("s", qrs_op, inc="Sg")
        kb.wait("v", "Sg")
        def qmean2(v):
            v.tensor_mul(wc2[:], qvar[:], qrs[:])
            v.tensor_copy(wc3[:], consts[:, 0:8])
            return v.tensor_mul(wc[:], wc2[:], qrs[:])
        kb.op("v", qmean2, inc="V")
        kb.wait("s", "V")
        def qrs2_op(s):
            s.activation(qrs2[:], wc[:], AF.Ln,
                         bias=consts[:, C_EPSLN:C_EPSLN + 1], scale=1.0)
            return s.activation(qrs2[:], qrs2[:], AF.Exp, bias=0.0,
                                scale=consts[:, C_NEGH:C_NEGH + 1])
        kb.op("s", qrs2_op, inc="Sg")
        kb.wait("v", "Sg")
        def qfin(v):
            v.tensor_sub(wc[:], qmx[:], qm[:])
            v.tensor_add(wc2[:], qm[:], qmn[:])
            v.tensor_mul(wc3[:], qrs[:], qrs2[:])
            v.tensor_max(wc4[:], wc[:], wc2[:])
            v.tensor_copy(wc[:], consts[:, 0:8])
            v.tensor_mul(wc2[:], wc4[:], wc3[:])
            v.tensor_copy(wc4[:], consts[:, 0:8])
            v.tensor_scalar(wc[:], wc2[:], 1e-5, None, op0=ALU.max)
            v.tensor_copy(wc4[:], consts[:, 0:8])
            v.tensor_mul(wc2[:], wc[:], winv[:])
            v.reciprocal(wc4[:], wc[:])
            v.tensor_scalar(qDIV[:], wc2[:], 0.5 / 127.0, None, op0=ALU.mult)
            v.tensor_mul(wc2[:], wc4[:], wc3[:])
            v.tensor_copy(wc4[:], consts[:, 0:8])
            v.tensor_scalar(qR[:], wc2[:], 127.0, None, op0=ALU.mult)
            v.tensor_copy(wc4[:], consts[:, 0:8])
            v.tensor_mul(wc2[:], qm[:], qR[:])
            v.tensor_copy(wc4[:], consts[:, 0:8])
            v.tensor_scalar(nmR[:], wc2[:], -1.0, None, op0=ALU.mult)
            v.tensor_copy(wc4[:], consts[:, 0:8])
            v.tensor_scalar(nmRp[:], nmR[:], 128.5, None, op0=ALU.add)
            return v.tensor_scalar(nmRh[:], nmR[:], 0.5, None, op0=ALU.add)
        kb.op("v", qfin, inc="V")

        kb.wait("y", "V")
        kb.wait("y", "Sg")
        wq_done = {}
        for jt in range(NJ):
            buf = qxr[jt % 2]
            kb.wait("y", "Sg", wq_done.get(jt - 2, 0))
            sm = ("Ya", "Yb")[jt % 2]
            dtgt = kb.op("y", lambda y, buf=buf, jt=jt:
                         y.dma_start(buf[:], xq_tm[128 * jt:128 * jt + 128, :]),
                         inc=sm, incby=16)
            kb.wait("s", sm, dtgt)
            kb.wait("s", "V")
            wq_done[jt] = kb.op("s", lambda s, buf=buf, jt=jt: s.activation(
                wbuf[:], buf[:], AF.Copy, bias=0.0,
                scale=qR[:, jt:jt + 1]), inc="Sg")
            kb.wait("v", "Sg")
            def qchain(v, jt=jt):
                v.tensor_scalar(sqf[:], wbuf[:], nmRp[:, jt:jt + 1], None,
                                op0=ALU.add)
                cur, nxt = sqf, ngf
                for k in (128.0, 64.0, 32.0, 16.0, 8.0, 4.0, 2.0, 1.0):
                    v.tensor_scalar(bmask[:], cur[:], k, None, op0=ALU.is_ge)
                    v.scalar_tensor_tensor(nxt[:], bmask[:], -k, cur[:],
                                           op0=ALU.mult, op1=ALU.add)
                    cur, nxt = nxt, cur
                v.tensor_sub(bmask[:], wbuf[:], cur[:])
                return v.tensor_scalar(xq_t[:], bmask[:], nmRh[:, jt:jt + 1],
                                       None, op0=ALU.add)
            kb.op("v", qchain, inc="V")
            for cg in range(4):
                kb.wait("t", "V")
                kb.wait("t", "DG")
                def qtr(t, cg=cg):
                    last = None
                    for u in range(8):
                        last = t.transpose(tp_b[:, 128 * u:128 * u + 128],
                                           xq_t[:, 128 * (8 * cg + u):
                                                 128 * (8 * cg + u) + 128],
                                           i128_b[:])
                    return last
                ttg = kb.op("t", qtr, inc="T")
                kb.wait("v", "T", ttg)
                kb.wait("v", "DG")
                kb.op("v", lambda v: v.tensor_copy(stage[:], tp_b[:]), inc="V")
                kb.wait("g", "V")
                def qst(g, jt=jt, cg=cg):
                    r0 = 1024 * cg
                    return g.dma_start(
                        xqf_scr[r0:r0 + 1024, 128 * jt:128 * jt + 128]
                        .rearrange("(u p) q -> p u q", p=128),
                        stage[:].rearrange("p (u q) -> p u q", q=128))
                kb.op("g", qst, inc="DG", incby=16)
        if debug:
            kb.wait("y", "V")
            kb.wait("y", "Sg")
            kb.op("y", lambda y: y.dma_start(d_statc[:], statc[:]), inc="DS", incby=16)
            for ci_, src_ in enumerate([qm, qrs, qrs2, qR, qDIV, qvar,
                                        qmx, qmn, qsum, qss, wc, wc3]):
                kb.op("y", lambda y, ci_=ci_, src_=src_: y.dma_start(
                    d_qstat[:, 8 * ci_:8 * ci_ + 8], src_[:]), inc="DS", incby=16)
            kb.op("y", lambda y: y.dma_start(d_sw[:, 0:1], swcol[:, 0:1], allow_slow_non_contiguous=True), inc="DS", incby=16)
            kb.op("y", lambda y: y.dma_start(d_sw[:, 1:2], winv[:, 0:1], allow_slow_non_contiguous=True), inc="DS", incby=16)
            kb.wait("y", "DS")
        kb.wait("g", "DG")
        kb.flush()

    with _ES() as loc:
        lsb = lambda n, s, d: loc.enter_context(nc.sbuf_tensor(n, s, d))
        lps = lambda n, s, d: loc.enter_context(nc.psum_tensor(n, s, d))
        kb.begin()
        wkva_bf = lsb("wkva_bf", [128, 32 * 384], BF16)
        csk_sb = lsb("csk_sb", [128, S], F32)
        xbf_ring = [lsb(f"xbf{i}", [128, 32 * TG], BF16) for i in range(2)]
        kv_ps = lps("kv_ps", [128, 3 * TG], F32)
        bc_ps = lps("bc_ps", [128, TG], F32)
        st_ps = lps("st_ps", [1, TG], F32)
        fo_ps = lps("fo_ps", [64, TG], F32)
        wsum_ps = lps("wsum_ps", [128, 3], F32)
        tp4_ps = lps("tp4_ps", [4, 128], F32)
        mbc = lsb("mbc", [128, TG], F32)
        nrsbc = lsb("nrsbc", [128, TG], F32)
        rcbc = lsb("rcbc", [128, TG], F32)
        ckv_f = lsb("ckv_f", [128, 2 * TG], F32)
        kpe_f = lsb("kpe_f", [128, TG], F32)
        w1 = lsb("w1", [128, TG], BF16)
        sqb = lsb("sqb", [128, 2 * TG], BF16)
        rrow = lsb("rrow", [1, TG], F32)
        wsumc = lsb("wsumc", [128, 3], F32)
        mseg = lsb("mseg", [1, TG], F32)
        nrseg = lsb("nrseg", [1, TG], F32)
        ctmp4 = lsb("ctmp4", [128, 4], F32)
        rowt4 = lsb("rowt4", [4, 128], F32)

        wk_tgt = kb.op("g", lambda g: g.dma_start(
            wkva_bf[:].rearrange("p (ci o) -> p ci o", o=384),
            wkvaT[:].rearrange("(ci p) o -> p ci o", p=128)), inc="Ge", incby=16)
        csk_tgt = kb.op("y", lambda y: y.dma_start(csk_sb[:], CSK[:]),
                        inc="Yc", incby=16)
        kb.wait("t", "Ge", wk_tgt)
        def wsums(t):
            last = None
            for oc in range(3):
                for ci in range(32):
                    last = t.matmul(
                        wsum_ps[:, oc:oc + 1],
                        wkva_bf[:, ci * 384 + 128 * oc:ci * 384 + 128 * oc + 128],
                        ones_b[:], start=(ci == 0), stop=(ci == 31))
            return last
        kb.op("t", wsums, inc="T")
        kb.wait("v", "T")
        kb.op("v", lambda v: v.tensor_copy(wsumc[:], wsum_ps[:]), inc="V")

        xg_free = {}
        for g_ in range(NTG):
            buf = xbf_ring[g_ % 2]
            kb.wait("g", "T", xg_free.get(g_ - 2, 0))
            sm = ("Ga", "Gb")[g_ % 2]
            dtgt = kb.op("g", lambda g, buf=buf, g_=g_: g.dma_start(
                buf[:].rearrange("p (ci f) -> p ci f", f=TG),
                xT[:, TG * g_:TG * g_ + TG].rearrange("(ci p) f -> p ci f", p=128)),
                inc=sm, incby=16)

            make_row_ops(
                lambda v, g_=g_: v.tensor_copy(ctmp4[:],
                                               statc[:, 8 * g_:8 * g_ + 8:2]),
                ctmp4[:], 4, tp4_ps, rowt4, mseg)
            def nrs_src(v, g_=g_):
                return v.tensor_scalar(ctmp4[:], statc[:, 8 * g_ + 1:8 * g_ + 9:2],
                                       -1.0, None, op0=ALU.mult)
            make_row_ops(nrs_src, ctmp4[:], 4, tp4_ps, rowt4, nrseg)

            kb.wait("t", "V")
            kb.wait("t", "DG")
            mtg = kb.op("t", lambda t: t.matmul(bc_ps[:], onesf_r[:], mseg[:],
                                                start=True, stop=True), inc="T")
            kb.wait("v", "T", mtg)
            kb.op("v", lambda v: v.tensor_copy(mbc[:], bc_ps[:]), inc="V")
            kb.wait("t", "V")
            mtg = kb.op("t", lambda t: t.matmul(bc_ps[:], onesf_r[:], nrseg[:],
                                                start=True, stop=True), inc="T")
            kb.wait("v", "T", mtg)
            kb.op("v", lambda v: v.tensor_copy(nrsbc[:], bc_ps[:]), inc="V")

            kb.wait("t", sm, dtgt)
            kb.wait("t", "V")
            def kva(t, buf=buf):
                last = None
                for oc in range(3):
                    for ci in range(32):
                        last = t.matmul(
                            kv_ps[:, TG * oc:TG * oc + TG],
                            wkva_bf[:, ci * 384 + 128 * oc:ci * 384 + 128 * oc + 128],
                            buf[:, ci * TG:ci * TG + TG],
                            start=(ci == 0), stop=(ci == 31))
                return last
            mm_tgt = kb.op("t", kva, inc="T")
            xg_free[g_] = mm_tgt

            kb.wait("v", "T", mm_tgt)
            def corr(v):
                for oc in range(2):
                    v.scalar_tensor_tensor(ckv_f[:, TG * oc:TG * oc + TG], mbc[:],
                                           wsumc[:, oc:oc + 1],
                                           kv_ps[:, TG * oc:TG * oc + TG],
                                           op0=ALU.mult, op1=ALU.subtract)
                    v.tensor_mul(ckv_f[:, TG * oc:TG * oc + TG],
                                 ckv_f[:, TG * oc:TG * oc + TG], nrsbc[:])
                v.scalar_tensor_tensor(kpe_f[:], mbc[:], wsumc[:, 2:3],
                                       kv_ps[:, 2 * TG:3 * TG],
                                       op0=ALU.mult, op1=ALU.subtract)
                return v.tensor_mul(kpe_f[:], kpe_f[:], nrsbc[:])
            kb.op("v", corr, inc="V")

            kb.wait("s", "V")
            kb.op("s", lambda s: s.activation(sqb[:], ckv_f[:], AF.Square), inc="Sg")
            kb.wait("t", "Sg")
            def sumsq(t):
                t.matmul(st_ps[:], ones_b[:], sqb[:, 0:TG], start=True, stop=False)
                return t.matmul(st_ps[:], ones_b[:], sqb[:, TG:2 * TG],
                                start=False, stop=True)
            mtg = kb.op("t", sumsq, inc="T")
            kb.wait("s", "T", mtg)
            def rsc(s):
                s.activation(rrow[:], st_ps[:], AF.Ln,
                             bias=consts[0:1, C_EPSAT:C_EPSAT + 1],
                             scale=consts[0:1, C_I256:C_I256 + 1])
                return s.activation(rrow[:], rrow[:], AF.Exp, bias=0.0,
                                    scale=consts[0:1, C_NEGH:C_NEGH + 1])
            kb.op("s", rsc, inc="Sg")
            kb.wait("t", "Sg")
            mtg = kb.op("t", lambda t: t.matmul(bc_ps[:], onesf_r[:], rrow[:],
                                                start=True, stop=True), inc="T")
            kb.wait("v", "T", mtg)
            def ckvn(v):
                v.tensor_copy(rcbc[:], bc_ps[:])
                v.tensor_mul(ckv_f[:, 0:TG], ckv_f[:, 0:TG], rcbc[:])
                return v.tensor_mul(ckv_f[:, TG:2 * TG], ckv_f[:, TG:2 * TG],
                                    rcbc[:])
            kb.op("v", ckvn, inc="V")
            kb.wait("s", "V")
            def ckv_gamma(s, g_=g_):
                s.activation(ckv_n[:, 0 * S + TG * g_:0 * S + TG * g_ + TG],
                             ckv_f[:, 0:TG], AF.Copy, bias=0.0,
                             scale=kvln_sb[:, 0:1])
                return s.activation(ckv_n[:, 1 * S + TG * g_:1 * S + TG * g_ + TG],
                                    ckv_f[:, TG:2 * TG], AF.Copy, bias=0.0,
                                    scale=kvln_sb[:, 1:2])
            kb.op("s", ckv_gamma, inc="Sg")

            kb.wait("v", "Yc", csk_tgt)
            def rope1(v, g_=g_):
                return v.tensor_tensor(w1[:], kpe_f[:],
                                       csk_sb[:, TG * g_:TG * g_ + TG], op=ALU.mult)
            kb.op("v", rope1, inc="V")
            kb.wait("t", "V")
            kb.wait("t", "Sg")
            mtg = kb.op("t", lambda t: t.matmul(fo_ps[:], fold_k[:], w1[:],
                                                start=True, stop=True), inc="T")
            kb.wait("s", "T", mtg)
            kb.op("s", lambda s, g_=g_: s.copy(kpe2[0:64, TG * g_:TG * g_ + TG],
                                               fo_ps[:]), inc="Sg")

        kb.wait("g", "Sg")
        kb.op("g", lambda g: g.dma_start(kpe2[64:128, :], kpe2[0:64, :]),
              inc="DG", incby=16)
        kb.wait("g", "DG")
        if debug:
            kb.op("g", lambda g: g.dma_start(d_ckvn[:], ckv_n[:]), inc="DG", incby=16)
            kb.op("g", lambda g: g.dma_start(d_kpe2[:], kpe2[:]), inc="DG", incby=16)
            kb.wait("g", "DG")
        kb.flush()

    with _ES() as loc:
        lsb = lambda n, s, d: loc.enter_context(nc.sbuf_tensor(n, s, d))
        lps = lambda n, s, d: loc.enter_context(nc.psum_tensor(n, s, d))
        kb.begin()
        xbfq = lsb("xbfq", [128, 32 * NQ], BF16)
        wqa_ring = [lsb(f"wqa{i}", [128, 32 * 128], BF16) for i in range(2)]
        q1_ps = lps("q1_ps", [128, 512], F32)
        ws_ps = lps("ws_ps", [128, 1], F32)
        st2_ps = lps("st2_ps", [1, 512], F32)
        bc2_ps = lps("bc2_ps", [128, 512], F32)
        tp8_ps = lps("tp8_ps", [8, 128], F32)
        mbcq = lsb("mbcq", [128, 512], F32)
        nbcq = lsb("nbcq", [128, 512], F32)
        rbcq = lsb("rbcq", [128, 512], F32)
        q1f = lsb("q1f", [128, 8 * 512], F32)
        sq2 = [lsb(f"sq2{i}", [128, 512], BF16) for i in range(2)]
        r2row = lsb("r2row", [1, 512], F32)
        wsq = lsb("wsq", [128, 8], F32)
        mqrow = lsb("mqrow", [1, NQ], F32)
        nrsqrow = lsb("nrsqrow", [1, NQ], F32)
        ctmp8 = lsb("ctmp8", [128, 8], F32)
        rowt8 = lsb("rowt8", [8, 128], F32)

        xq_tgt = kb.op("g", lambda g: g.dma_start(
            xbfq[:].rearrange("p (ci f) -> p ci f", f=NQ),
            xqT[:].rearrange("(ci p) f -> p ci f", p=128)), inc="DG", incby=16)

        make_row_ops(lambda v: v.tensor_copy(ctmp8[:], qm[:]),
                     ctmp8[:], 8, tp8_ps, rowt8, mqrow)
        make_row_ops(lambda v: v.tensor_scalar(ctmp8[:], qrs[:], -1.0,
                                               None, op0=ALU.mult),
                     ctmp8[:], 8, tp8_ps, rowt8, nrsqrow)

        wq_free = {}
        for th in range(2):
            kb.wait("t", "V")
            kb.wait("t", "DG")
            mtg = kb.op("t", lambda t, th=th: t.matmul(
                bc2_ps[:], onesf_r[:], mqrow[:, 512 * th:512 * th + 512],
                start=True, stop=True), inc="T")
            kb.wait("v", "T", mtg)
            kb.op("v", lambda v: v.tensor_copy(mbcq[:], bc2_ps[:]), inc="V")
            kb.wait("t", "V")
            mtg = kb.op("t", lambda t, th=th: t.matmul(
                bc2_ps[:], onesf_r[:], nrsqrow[:, 512 * th:512 * th + 512],
                start=True, stop=True), inc="T")
            kb.wait("v", "T", mtg)
            kb.op("v", lambda v: v.tensor_copy(nbcq[:], bc2_ps[:]), inc="V")

            for oc in range(8):
                wbuf = wqa_ring[oc % 2]
                kb.wait("g", "T", wq_free.get(8 * th + oc - 2, 0))
                dtgt = kb.op("g", lambda g, wbuf=wbuf, oc=oc: g.dma_start(
                    wbuf[:].rearrange("p (ci f) -> p ci f", f=128),
                    wqaT[:, 128 * oc:128 * oc + 128]
                    .rearrange("(ci p) f -> p ci f", p=128)), inc="DG", incby=16)
                kb.wait("t", "DG", dtgt)
                kb.wait("t", "V")
                if th == 0:
                    def wsum_q(t, wbuf=wbuf):
                        last = None
                        for ci in range(32):
                            last = t.matmul(ws_ps[:],
                                            wbuf[:, 128 * ci:128 * ci + 128],
                                            ones_b[:], start=(ci == 0),
                                            stop=(ci == 31))
                        return last
                    kb.op("t", wsum_q, inc="T")
                    kb.wait("v", "T")
                    kb.op("v", lambda v, oc=oc: v.tensor_copy(wsq[:, oc:oc + 1],
                                                              ws_ps[:]), inc="V")
                    kb.wait("t", "V")
                kb.wait("t", "DG", xq_tgt)
                def q1mm(t, wbuf=wbuf, th=th):
                    last = None
                    for ci in range(32):
                        last = t.matmul(
                            q1_ps[:], wbuf[:, 128 * ci:128 * ci + 128],
                            xbfq[:, ci * NQ + 512 * th:ci * NQ + 512 * th + 512],
                            start=(ci == 0), stop=(ci == 31))
                    return last
                mm_tgt = kb.op("t", q1mm, inc="T")
                wq_free[8 * th + oc] = mm_tgt
                kb.wait("v", "T", mm_tgt)
                def q1corr(v, oc=oc):
                    v.scalar_tensor_tensor(q1f[:, 512 * oc:512 * oc + 512], mbcq[:],
                                           wsq[:, oc:oc + 1], q1_ps[:],
                                           op0=ALU.mult, op1=ALU.subtract)
                    return v.tensor_mul(q1f[:, 512 * oc:512 * oc + 512],
                                        q1f[:, 512 * oc:512 * oc + 512], nbcq[:])
                kb.op("v", q1corr, inc="V")

            for oc in range(8):
                kb.wait("s", "V")
                kb.wait("s", "T")
                kb.op("s", lambda s, oc=oc: s.activation(
                    sq2[oc % 2][:], q1f[:, 512 * oc:512 * oc + 512], AF.Square),
                    inc="Sg")
                kb.wait("t", "Sg")
                kb.op("t", lambda t, oc=oc: t.matmul(
                    st2_ps[:], ones_b[:], sq2[oc % 2][:],
                    start=(oc == 0), stop=(oc == 7)), inc="T")
            kb.wait("s", "T")
            def rs2q(s):
                s.activation(r2row[:], st2_ps[:], AF.Ln,
                             bias=consts[0:1, C_EPSAT:C_EPSAT + 1],
                             scale=consts[0:1, C_I1024:C_I1024 + 1])
                return s.activation(r2row[:], r2row[:], AF.Exp, bias=0.0,
                                    scale=consts[0:1, C_NEGH:C_NEGH + 1])
            kb.op("s", rs2q, inc="Sg")
            kb.wait("v", "Sg")
            kb.op("v", lambda v: v.tensor_scalar(r2row[:], r2row[:], SOFTMAX_SCALE,
                                                 None, op0=ALU.mult), inc="V")
            kb.wait("t", "V")
            mtg = kb.op("t", lambda t: t.matmul(bc2_ps[:], onesf_r[:], r2row[:],
                                                start=True, stop=True), inc="T")
            kb.wait("v", "T", mtg)
            def q1n_mul(v):
                v.tensor_copy(rbcq[:], bc2_ps[:])
                last = None
                for oc in range(8):
                    last = v.tensor_mul(q1f[:, 512 * oc:512 * oc + 512],
                                        q1f[:, 512 * oc:512 * oc + 512], rbcq[:])
                return last
            kb.op("v", q1n_mul, inc="V")
            kb.wait("s", "V")
            def q1n_gam(s, th=th):
                last = None
                for oc in range(8):
                    last = s.activation(
                        q1n[:, oc * NQ + 512 * th:oc * NQ + 512 * th + 512],
                        q1f[:, 512 * oc:512 * oc + 512], AF.Copy, bias=0.0,
                        scale=qln_sb[:, oc:oc + 1])
                return last
            kb.op("s", q1n_gam, inc="Sg")
        kb.wait("t", "Sg")
        if debug:
            kb.wait("g", "Sg")
            kb.op("g", lambda g: g.dma_start(d_q1n[:], q1n[:]), inc="DG", incby=16)
            kb.wait("g", "DG")
        kb.flush()

    oh_scr = nc.dram_tensor("oh_scr", [D, NQ], BF16, kind="Internal")

    with _ES() as loc:
        lsb = lambda n, s, d: loc.enter_context(nc.sbuf_tensor(n, s, d))
        lps = lambda n, s, d: loc.enter_context(nc.psum_tensor(n, s, d))
        kb.begin()
        wbn_ring = [lsb(f"wbn{i}", [128, 2 * 128], BF16) for i in range(2)]
        wbv_ring = [lsb(f"wbv{i}", [128, 2 * 256], BF16) for i in range(2)]
        wqn_ring = [lsb(f"wqn{i}", [128, 8 * 128], BF16) for i in range(2)]
        wqp_ring = [lsb(f"wqp{i}", [128, 8 * 256], BF16) for i in range(2)]
        kf = [lsb(f"kf{i}", [128, S], BF16) for i in range(2)]
        vv = [lsb(f"vv{i}", [128, S], BF16) for i in range(2)]
        qf = [lsb(f"qf{i}", [128, NQ], BF16) for i in range(2)]
        qpe_f = lsb("qpe_f", [128, NQ], F32)
        w1q = lsb("w1q", [128, NQ], BF16)
        p_sb = [lsb(f"p_sb{i}", [128, S], BF16) for i in range(2)]
        ptbuf = [lsb(f"ptb{i}", [128, S], BF16) for i in range(2)]
        rowr = lsb("rowr", [8, 128], F32)
        ohf = lsb("ohf", [128, NQ], F32)
        rrow2 = lsb("rrow2", [1, NQ], F32)

        oh_hbuf = [lsb(f"ohb{i}", [128, NQ], BF16) for i in range(2)]
        arena = lps("arena", [128, 4096], F32)
        arena_b = arena[:].bitcast(BF16)
        pexp_gate = [0, 0]
        oh_st_tgt = [0, 0]

        def load_pair(pi, ridx):
            sm = "DG"
            kb.op("g", lambda g, pi=pi, ridx=ridx: g.dma_start(
                wbn_ring[ridx][:].rearrange("p (cc f) -> p cc f", f=128),
                wkvbnT[:, 128 * pi:128 * pi + 128]
                .rearrange("(cc p) f -> p cc f", p=128)), inc=sm, incby=16)
            kb.op("g", lambda g, pi=pi, ridx=ridx: g.dma_start(
                wbv_ring[ridx][:].rearrange("p (cc f) -> p cc f", f=256),
                wkvbvT[:, 256 * pi:256 * pi + 256]
                .rearrange("(cc p) f -> p cc f", p=128)), inc=sm, incby=16)
            kb.op("g", lambda g, pi=pi, ridx=ridx: g.dma_start(
                wqn_ring[ridx][:].rearrange("p (ci f) -> p ci f", f=128),
                wqbnT[:, 128 * pi:128 * pi + 128]
                .rearrange("(ci p) f -> p ci f", p=128)), inc=sm, incby=16)
            return kb.op("g", lambda g, pi=pi, ridx=ridx: g.dma_start(
                wqp_ring[ridx][:].rearrange("p (ci f) -> p ci f", f=256),
                wqbpT[:, 256 * pi:256 * pi + 256]
                .rearrange("(ci p) f -> p ci f", p=128)), inc=sm, incby=16)

        pair_w_tgt = {0: load_pair(0, 0)}

        for pi in range(16):
            ridx = pi % 2
            kb.wait("t", "DG", pair_w_tgt[pi])

            for kc in range(8):
                kb.wait("t", "V")
                kb.wait("t", "Sg")
                def nope_mm(t, ridx=ridx, kc=kc):
                    t.matmul(arena[:, 0:512], wbn_ring[ridx][:, 0:128],
                             ckv_n[:, 0 * S + 512 * kc:0 * S + 512 * kc + 512],
                             start=True, stop=False)
                    return t.matmul(arena[:, 0:512], wbn_ring[ridx][:, 128:256],
                                    ckv_n[:, 1 * S + 512 * kc:1 * S + 512 * kc + 512],
                                    start=False, stop=True)
                mtg = kb.op("t", nope_mm, inc="T")
                kb.wait("v", "T", mtg)
                def nope_ev(v, kc=kc):
                    v.tensor_copy(kf[0][0:64, 512 * kc:512 * kc + 512],
                                  arena[0:64, 0:512])
                    return v.tensor_copy(kf[1][64:128, 512 * kc:512 * kc + 512],
                                         arena[64:128, 0:512])
                kb.op("v", nope_ev, inc="V")
            def kpe_cp(v):
                v.tensor_copy(kf[0][64:128, :], kpe2[64:128, :])
                return v.tensor_copy(kf[1][0:64, :], kpe2[0:64, :])
            kb.op("v", kpe_cp, inc="V")

            for kg in range(16):
                kb.wait("t", "V")
                def v_mm(t, ridx=ridx, kg=kg):
                    last = None
                    for u in range(2):
                        kt = 2 * kg + u
                        t.matmul(arena[:, 512 + 256 * u:512 + 256 * u + 256],
                                 ckv_n[:, 0 * S + 128 * kt:0 * S + 128 * kt + 128],
                                 wbv_ring[ridx][:, 0:256], start=True, stop=False)
                        last = t.matmul(
                            arena[:, 512 + 256 * u:512 + 256 * u + 256],
                            ckv_n[:, 1 * S + 128 * kt:1 * S + 128 * kt + 128],
                            wbv_ring[ridx][:, 256:512], start=False, stop=True)
                    return last
                mtg = kb.op("t", v_mm, inc="T")
                kb.wait("v", "T", mtg)
                def v_ev(v, kg=kg):
                    src = arena[:, 512:1024].rearrange("p (u hv) -> p u hv", hv=256)
                    v.tensor_copy(vv[0][:, 256 * kg:256 * kg + 256]
                                  .rearrange("p (u f) -> p u f", f=128),
                                  src[:, :, 0:128])
                    return v.tensor_copy(vv[1][:, 256 * kg:256 * kg + 256]
                                         .rearrange("p (u f) -> p u f", f=128),
                                         src[:, :, 128:256])
                kb.op("v", v_ev, inc="V")

            for th in range(2):
                kb.wait("t", "V")
                def qn_mm(t, ridx=ridx, th=th):
                    last = None
                    for ci in range(8):
                        last = t.matmul(
                            arena[:, 0:512],
                            wqn_ring[ridx][:, 128 * ci:128 * ci + 128],
                            q1n[:, ci * NQ + 512 * th:ci * NQ + 512 * th + 512],
                            start=(ci == 0), stop=(ci == 7))
                    return last
                mtg = kb.op("t", qn_mm, inc="T")
                kb.wait("v", "T", mtg)
                def qn_ev(v, th=th):
                    v.tensor_copy(qf[0][0:64, 512 * th:512 * th + 512],
                                  arena[0:64, 0:512])
                    return v.tensor_copy(qf[1][64:128, 512 * th:512 * th + 512],
                                         arena[64:128, 0:512])
                kb.op("v", qn_ev, inc="V")

            for hh in range(2):
                for th in range(2):
                    kb.wait("t", "V")
                    def qp_mm(t, ridx=ridx, hh=hh, th=th):
                        last = None
                        for ci in range(8):
                            last = t.matmul(
                                arena[:, 0:512],
                                wqp_ring[ridx][:, 256 * ci + 128 * hh:
                                               256 * ci + 128 * hh + 128],
                                q1n[:, ci * NQ + 512 * th:ci * NQ + 512 * th + 512],
                                start=(ci == 0), stop=(ci == 7))
                        return last
                    mtg = kb.op("t", qp_mm, inc="T")
                    kb.wait("v", "T", mtg)
                    kb.op("v", lambda v, th=th: v.tensor_copy(
                        qpe_f[:, 512 * th:512 * th + 512], arena[:, 0:512]),
                        inc="V")
                kb.op("v", lambda v: v.tensor_tensor(w1q[:], qpe_f[:], csq[:],
                                                     op=ALU.mult), inc="V")
                fold_m = fold_hi if hh == 0 else fold_lo
                for th in range(2):
                    kb.wait("t", "V")
                    kb.wait("t", "Sg")
                    mtg = kb.op("t", lambda t, fold_m=fold_m, th=th: t.matmul(
                        arena[:, 512:1024], fold_m[:],
                        w1q[:, 512 * th:512 * th + 512], start=True, stop=True),
                        inc="T")
                    kb.wait("s", "T", mtg)
                    def qpe_ev(s, hh=hh, th=th):
                        if hh == 0:
                            return s.copy(qf[0][64:128, 512 * th:512 * th + 512],
                                          arena[64:128, 512:1024])
                        return s.copy(qf[1][0:64, 512 * th:512 * th + 512],
                                      arena[0:64, 512:1024])
                    kb.op("s", qpe_ev, inc="Sg")

            if pi + 1 < 16:
                kb.wait("g", "T")
                pair_w_tgt[pi + 1] = load_pair(pi + 1, (pi + 1) % 2)

            for hh in range(2):
                h = 2 * pi + hh

                def emit_pv(pj, pnkt, petg, hh=hh):
                    ttg = 0
                    last_ptev = 0
                    for tb in range(0, pnkt, 8):
                        te = min(pnkt, tb + 8)
                        if tb == 0:
                            kb.wait("t", "Sg", petg)
                        else:
                            kb.wait("t", "V", last_ptev)
                        def ptr(t, tb=tb, te=te, pj=pj):
                            last = None
                            for u in range(te - tb):
                                last = t.transpose(
                                    arena_b[:, 7168 + 128 * u:
                                            7168 + 128 * u + 128],
                                    p_sb[pj % 2][:, 128 * (tb + u):
                                                 128 * (tb + u) + 128],
                                    i128_b[:])
                            return last
                        ttg = kb.op("t", ptr, inc="T")
                        kb.wait("v", "T", ttg)
                        def ptev(v, tb=tb, te=te, pj=pj):
                            return v.tensor_copy(
                                ptbuf[pj % 2][:, 128 * tb:128 * te],
                                arena_b[:, 7168:7168 + 128 * (te - tb)])
                        last_ptev = kb.op("v", ptev, inc="V")
                    pexp_gate[pj % 2] = ttg
                    kb.wait("t", "V", last_ptev)
                    def pv(t, hh=hh, pnkt=pnkt, pj=pj):
                        last = None
                        for kt in range(pnkt):
                            last = t.matmul(
                                arena[:, 3072:3200],
                                vv[hh][:, 128 * kt:128 * kt + 128],
                                ptbuf[pj % 2][:, 128 * kt:128 * kt + 128],
                                start=(kt == 0), stop=(kt == pnkt - 1))
                        return last
                    mtg2 = kb.op("t", pv, inc="T")
                    kb.wait("v", "T", mtg2)
                    kb.op("v", lambda v, pj=pj: v.tensor_copy(
                        ohf[:, 128 * pj:128 * pj + 128], arena[:, 3072:3200]),
                        inc="V")

                pend = None
                etg_prev = 0
                for j in range(NJ):
                    span = 512 * (j + 1)
                    if j >= 6 and pend is not None:
                        emit_pv(*pend)
                        pend = None
                        kb.wait("t", "V")
                    if j == 0:
                        kb.wait("t", "V")
                        kb.wait("t", "Sg")
                    else:
                        kb.wait("t", "Sg", etg_prev)
                    def sc_mm(t, hh=hh, j=j):
                        last = None
                        for kc in range(j + 1):
                            last = t.matmul(arena[:, 512 * kc:512 * kc + 512],
                                            qf[hh][:, 128 * j:128 * j + 128],
                                            kf[hh][:, 512 * kc:512 * kc + 512],
                                            start=True, stop=True)
                        return last
                    mtg = kb.op("t", sc_mm, inc="T")
                    kb.wait("v", "T", mtg)
                    def mask_mx(v, j=j, span=span):
                        v.tensor_add(arena[:, 512 * j:512 * j + 512],
                                     arena[:, 512 * j:512 * j + 512], maskw[:])
                        return v.reduce_max(sumc[:, 8:9], arena[:, 0:span],
                                            axis=AX.X, negate=True)
                    vtg = kb.op("v", mask_mx, inc="V")
                    kb.wait("s", "V", vtg)
                    kb.wait("s", "T", pexp_gate[j % 2])
                    def ex(s, span=span, j=j):
                        half = span // 2
                        for sec, (lo, hi) in enumerate([(0, half),
                                                        (half, span)]):
                            last = s.activation(
                                p_sb[j % 2][:, lo:hi], arena[:, lo:hi], AF.Exp,
                                bias=sumc[:, 8:9],
                                accum_out=sumc[:, 10 + sec:11 + sec])
                        return last
                    etg = kb.op("s", ex, inc="Sg")
                    kb.wait("v", "Sg", etg)
                    kb.op("v", lambda v, j=j: v.reduce_sum(
                        sumc[:, j:j + 1], sumc[:, 10:12], axis=AX.X), inc="V")
                    nkt_j = span // 128
                    LAG = True
                    if LAG:
                        if pend is not None:
                            emit_pv(*pend)
                        pend = (j, nkt_j, etg)
                    else:
                        emit_pv(j, nkt_j, etg)
                    etg_prev = etg
                if pend is not None:
                    emit_pv(*pend)

                kb.op("v", lambda v: v.reciprocal(sumc[:, 8:16], sumc[:, 0:8]),
                      inc="V")
                kb.wait("t", "V")
                kb.op("t", lambda t: t.transpose(arena[0:8, 0:128],
                                                 sumc[:, 8:16], i128_f[:]),
                      inc="T")
                kb.wait("v", "T")
                kb.op("v", lambda v: v.tensor_copy(rowr[:], arena[0:8, 0:128]),
                      inc="V")
                kb.wait("g", "V")
                fl = kb.op("g", lambda g: g.dma_start(
                    rrow2[:].rearrange("o (a b) -> o a b", a=8),
                    rowr[:]), inc="DG", incby=16)
                kb.wait("t", "DG", fl)
                def recb_outer(t):
                    last = None
                    for u in range(2):
                        last = t.matmul(arena[:, 512 * u:512 * u + 512], onesf_r[:],
                                        rrow2[:, 512 * u:512 * u + 512],
                                        start=True, stop=True)
                    return last
                rtg = kb.op("t", recb_outer, inc="T")
                kb.wait("v", "T", rtg)
                kb.wait("v", "DG", oh_st_tgt[h % 2])
                vtg3 = kb.op("v", lambda v, h=h: v.tensor_mul(
                    oh_hbuf[h % 2][:], arena[:, 0:1024], ohf[:]), inc="V")
                kb.wait("g", "V", vtg3)
                oh_st_tgt[h % 2] = kb.op("g", lambda g, h=h: g.dma_start(
                    oh_scr[128 * h:128 * h + 128, :], oh_hbuf[h % 2][:]),
                    inc="DG", incby=16)
            if debug and pi == 0:
                kb.wait("g", "V")
                kb.op("g", lambda g: g.dma_start(d_kf0[:], kf[0][:]), inc="DG", incby=16)
                kb.op("g", lambda g: g.dma_start(d_kf1[:], kf[1][:]), inc="DG", incby=16)
                kb.op("g", lambda g: g.dma_start(d_vv0[:], vv[0][:]), inc="DG", incby=16)
                kb.op("g", lambda g: g.dma_start(d_qf0[:], qf[0][:]), inc="DG", incby=16)
                kb.op("g", lambda g: g.dma_start(d_qf1[:], qf[1][:]), inc="DG", incby=16)
                kb.op("g", lambda g: g.dma_start(d_p0[:], p_sb[1][:]), inc="DG", incby=16)
                kb.op("g", lambda g: g.dma_start(d_sum[:], sumc[:]), inc="DG", incby=16)
                kb.wait("g", "DG")
                kb.wait("t", "DG")
                kb.wait("v", "DG")
                kb.wait("s", "DG")
        if debug:
            kb.wait("g", "DG")
            kb.wait("g", "Gc")
            kb.wait("g", "Gd")
            kb.op("g", lambda g: g.dma_start(
                d_oh[:].rearrange("p (h q) -> p h q", q=NQ),
                oh_scr[:].rearrange("(h p) q -> p h q", p=128)),
                inc="DG", incby=16)
            kb.wait("g", "DG")
        kb.flush()

    with _ES() as loc:
        lsb = lambda n, s, d: loc.enter_context(nc.sbuf_tensor(n, s, d))
        lps = lambda n, s, d: loc.enter_context(nc.psum_tensor(n, s, d))
        kb.begin()
        kb.wait("g", "DG")
        kb.wait("g", "Gc")
        kb.wait("g", "Gd")
        wo_ring = [lsb(f"wor{i}", [128, 512], BF16) for i in range(2)]
        oh_ring = [lsb(f"ohr{i}", [128, NQ], BF16) for i in range(2)]
        xq_ring = [lsb(f"xqr{i}", [128, NQ], BF16) for i in range(2)]
        wt_ring = [lsb(f"wtr{i}", [128, 512], F32) for i in range(2)]
        tern_ring = [lsb(f"ter{i}", [128, 512], BF16) for i in range(2)]
        sgn_a = [lsb(f"sgn_a{i}", [128, 512], BF16) for i in range(4)]
        xf_sb = lsb("xf_sb", [128, 4 * NQ], F32)
        acc = lsb("acc", [128, 4 * NQ], F32)
        ybuf = lsb("ybuf", [128, NQ], F32)
        a1bc = lsb("a1bc", [128, NQ], F32)
        bqbc = lsb("bqbc", [128, NQ], F32)
        divbc = lsb("divbc", [128, NQ], F32)
        a1qrow = lsb("a1qrow", [1, NQ], F32)
        bqrow = lsb("bqrow", [1, NQ], F32)
        divrow = lsb("divrow", [1, NQ], F32)
        ctmp8b = lsb("ctmp8b", [128, 8], F32)
        ctmp8c = lsb("ctmp8c", [128, 8], F32)
        rowt8b = lsb("rowt8b", [8, 128], F32)
        y_ps = lps("y_ps", [128, 4096], F32)

        make_row_ops(lambda v: v.tensor_scalar(ctmp8b[:], qrs[:], 1.0,
                                               None, op0=ALU.add),
                     ctmp8b[:], 8, y_ps[0:8, 0:128], rowt8b, a1qrow)
        def bqr_src(v):
            v.tensor_mul(ctmp8c[:], qm[:], qrs[:])
            v.tensor_copy(ctmp8b[:], consts[:, 0:8])
            return v.tensor_scalar(ctmp8b[:], ctmp8c[:], -1.0, None,
                                   op0=ALU.mult)
        make_row_ops(bqr_src, ctmp8b[:], 8, y_ps[0:8, 0:128], rowt8b, bqrow)
        make_row_ops(lambda v: v.tensor_copy(ctmp8b[:], qDIV[:]),
                     ctmp8b[:], 8, y_ps[0:8, 0:128], rowt8b, divrow)

        for rowsrc, dst in [(a1qrow, a1bc), (bqrow, bqbc), (divrow, divbc)]:
            for th in range(2):
                kb.wait("t", "V")
                kb.wait("t", "DG")
                mtg = kb.op("t", lambda t, rowsrc=rowsrc, th=th: t.matmul(
                    y_ps[:, 0:512], onesf_r[:], rowsrc[:, 512 * th:512 * th + 512],
                    start=True, stop=True), inc="T")
                kb.wait("v", "T", mtg)
                kb.op("v", lambda v, dst=dst, th=th: v.tensor_copy(
                    dst[:, 512 * th:512 * th + 512], y_ps[:, 0:512]), inc="V")

        sgn_done = {}
        tern_done = {}
        ci_mm = {}
        ff_mm = {}
        for ocb in range(8):
            for ci in range(32):
                wbuf = wo_ring[ci % 2]
                kb.wait("g", "T", ci_mm.get(32 * ocb + ci - 2, 0))
                sm = ("Ga", "Gb")[ci % 2]
                dtg = kb.op("g", lambda g, ocb=ocb, ci=ci, wbuf=wbuf: g.dma_start(
                    wbuf[:],
                    woT[128 * ci:128 * ci + 128, 512 * ocb:512 * ocb + 512]),
                    inc=sm, incby=16)
                odt = kb.op("g", lambda g, ci=ci: g.dma_start(
                    oh_ring[ci % 2][:], oh_scr[128 * ci:128 * ci + 128, :]),
                    inc=sm, incby=16)
                kb.wait("t", sm, odt)
                if ci == 0:
                    kb.wait("t", "V")
                def womm(t, ci=ci, wbuf=wbuf):
                    last = None
                    for oc in range(4):
                        for th in range(2):
                            last = t.matmul(
                                y_ps[:, 1024 * oc + 512 * th:
                                     1024 * oc + 512 * th + 512],
                                wbuf[:, 128 * oc:128 * oc + 128],
                                oh_ring[ci % 2][:, 512 * th:512 * th + 512],
                                start=(ci == 0), stop=(ci == 31))
                    return last
                ci_mm[32 * ocb + ci] = kb.op("t", womm, inc="T")
            wot = ci_mm[32 * ocb + 31]
            kb.wait("y", "V")
            xtg = kb.op("y", lambda y, ocb=ocb: y.dma_start(
                xf_sb[:].rearrange("p (oc q) -> p oc q", q=NQ),
                xqT[512 * ocb:512 * ocb + 512, :]
                .rearrange("(oc p) q -> p oc q", p=128)), inc="DS", incby=16)
            kb.wait("v", "T", wot)
            kb.wait("v", "DS", xtg)
            def accb(v):
                last = None
                for oc in range(4):
                    v.tensor_add(acc[:, NQ * oc:NQ * oc + NQ],
                                 y_ps[:, 1024 * oc:1024 * oc + NQ], bqbc[:])
                    v.tensor_mul(ybuf[:], xf_sb[:, NQ * oc:NQ * oc + NQ], a1bc[:])
                    last = v.tensor_add(acc[:, NQ * oc:NQ * oc + NQ],
                                        acc[:, NQ * oc:NQ * oc + NQ], ybuf[:])
                return last
            kb.op("v", accb, inc="V")
            if debug and ocb == 0:
                kb.wait("y", "V")
                kb.op("y", lambda y: y.dma_start(d_acc1[:], acc[:, 0:NQ]),
                      inc="DS", incby=16)
                kb.op("y", lambda y: y.dma_start(d_rows3[:, 0:NQ], a1qrow[:]),
                      inc="DS", incby=16)
                kb.op("y", lambda y: y.dma_start(d_rows3[:, NQ:2 * NQ],
                                                 bqrow[:]), inc="DS", incby=16)
                kb.op("y", lambda y: y.dma_start(d_rows3[:, 2 * NQ:3 * NQ],
                                                 divrow[:]), inc="DS", incby=16)
                kb.wait("v", "DS")

            for ci in range(32):
                wbuf = wt_ring[ci % 2]
                tbuf = tern_ring[ci % 2]
                kb.wait("y", "Sg")
                sm = ("Ya", "Yb")[ci % 2]
                dtg = kb.op("y", lambda y, ocb=ocb, ci=ci, wbuf=wbuf: y.dma_start(
                    wbuf[:],
                    WT[128 * ci:128 * ci + 128, 512 * ocb:512 * ocb + 512]),
                    inc=sm, incby=16)
                kb.wait("g", "T", ff_mm.get(32 * ocb + ci - 2, 0))
                xdt = kb.op("g", lambda g, ci=ci: g.dma_start(
                    xq_ring[ci % 2][:], xqf_scr[128 * ci:128 * ci + 128, :]),
                    inc=("Gh", "Gi")[ci % 2], incby=16)
                kb.wait("s", sm, dtg)
                kb.wait("s", "T", ff_mm.get(32 * ocb + ci - 2, 0))
                kb.wait("s", "V")
                def sgn(s, wbuf=wbuf, tbuf=tbuf, ci=ci):
                    s.activation(sgn_a[ci % 4][:], wbuf[:], AF.Sign,
                                 bias=consts[:, C_BA2:C_BA2 + 1],
                                 scale=swcol[:, 0:1])
                    return s.activation(tbuf[:], wbuf[:], AF.Sign,
                                        bias=consts[:, C_BB2:C_BB2 + 1],
                                        scale=swcol[:, 0:1])
                stg = kb.op("s", sgn, inc="Sg")
                sgn_done[32 * ocb + ci] = stg
                kb.wait("v", "Sg", stg)
                vtg = kb.op("v", lambda v, tbuf=tbuf, ci=ci: v.tensor_add(
                    tbuf[:], tbuf[:], sgn_a[ci % 4][:]), inc="V")
                tern_done[32 * ocb + ci] = vtg
                kb.wait("t", "V", vtg)
                kb.wait("t", ("Gh", "Gi")[ci % 2], xdt)
                if ci == 0:
                    kb.wait("t", "V")
                def ffmm(t, ci=ci, tbuf=tbuf):
                    last = None
                    for oc in range(4):
                        for th in range(2):
                            last = t.matmul(
                                y_ps[:, 1024 * oc + 512 * th:
                                     1024 * oc + 512 * th + 512],
                                tbuf[:, 128 * oc:128 * oc + 128],
                                xq_ring[ci % 2][:, 512 * th:512 * th + 512],
                                start=(ci == 0), stop=(ci == 31))
                    return last
                ff_mm[32 * ocb + ci] = kb.op("t", ffmm, inc="T")
            fft = ff_mm[32 * ocb + 31]
            kb.wait("v", "T", fft)
            for oc in range(4):
                vtg = kb.op("v", lambda v, oc=oc: v.tensor_mul(
                    ybuf[:], y_ps[:, 1024 * oc:1024 * oc + NQ], divbc[:]), inc="V")
                kb.wait("s", "V", vtg)
                stg = kb.op("s", lambda s, ocb=ocb, oc=oc: s.activation(
                    ybuf[:], ybuf[:], AF.Gelu,
                    bias=bq_sb[:, 4 * ocb + oc:4 * ocb + oc + 1], scale=1.0),
                    inc="Sg")
                kb.wait("v", "Sg", stg)
                if debug and ocb == 0 and oc == 0:
                    kb.wait("y", "Sg", stg)
                    kb.op("y", lambda y: y.dma_start(d_ffny[:], ybuf[:]),
                          inc="DS", incby=16)
                    kb.wait("v", "DS")
                vtg2 = kb.op("v", lambda v, oc=oc: v.tensor_add(
                    acc[:, NQ * oc:NQ * oc + NQ], acc[:, NQ * oc:NQ * oc + NQ],
                    ybuf[:]), inc="V")
                kb.wait("y", "V", vtg2)
                kb.op("y", lambda y, ocb=ocb, oc=oc: y.dma_start(
                    out[128 * (4 * ocb + oc):128 * (4 * ocb + oc) + 128, :],
                    acc[:, NQ * oc:NQ * oc + NQ]), inc="DS", incby=16)
        kb.wait("y", "DS")
        kb.flush()

    stk.close()
    return nc


_PROG = {}


def _get_prog():
    if "nc" not in _PROG:
        _PROG["nc"] = build_program()
    return _PROG["nc"]


def kernel(L0_attn_x, L0_attn_wq_a, L0_attn_q_a_ln_w, L0_attn_wq_b, L0_attn_wkv_a,
           L0_attn_kv_a_ln_w, L0_attn_wkv_b, L0_attn_wo, L0_ffn_W, L0_ffn_b):
    x = np.asarray(L0_attn_x, dtype=np.float32)
    wq_a = np.asarray(L0_attn_wq_a, dtype=np.float32)
    q_a_ln = np.asarray(L0_attn_q_a_ln_w, dtype=np.float32)
    wq_b = np.asarray(L0_attn_wq_b, dtype=np.float32)
    wkv_a = np.asarray(L0_attn_wkv_a, dtype=np.float32)
    kv_a_ln = np.asarray(L0_attn_kv_a_ln_w, dtype=np.float32)
    wkv_b = np.asarray(L0_attn_wkv_b, dtype=np.float32)
    wo = np.asarray(L0_attn_wo, dtype=np.float32)
    W = np.asarray(L0_ffn_W, dtype=np.float32)
    bias = np.asarray(L0_ffn_b, dtype=np.float32)

    C = np.ascontiguousarray
    deint = np.concatenate([np.arange(0, 64, 2), np.arange(1, 64, 2)])
    rotp = np.concatenate([np.arange(32, 64), np.arange(0, 32)])

    idx_n = np.concatenate([h * 128 + np.arange(64) for h in range(H)])
    wqbnT = C(wq_b[idx_n].T)
    idx_p = []
    for h in range(H):
        dr = h * 128 + 64 + deint
        idx_p.append(dr)
        idx_p.append(dr[rotp])
    idx_p = np.concatenate(idx_p)
    wqbpT = C(wq_b[idx_p].T)

    kpe_rows = 256 + deint
    wkvaT = C(np.concatenate([wkv_a[0:256], wkv_a[kpe_rows],
                              wkv_a[kpe_rows][rotp]], axis=0).T)
    idx_kn = np.concatenate([h * 192 + np.arange(64) for h in range(H)])
    wkvbnT = C(wkv_b[idx_kn].T)
    idx_kv = np.concatenate([h * 192 + 64 + np.arange(128) for h in range(H)])
    wkvbvT = C(wkv_b[idx_kv].T)

    wqaT = C(wq_a.T)
    woT = C(wo.T)
    WT = C(W.T)

    qlnv = C(q_a_ln.reshape(8, 128).T)
    kvlnv = C(kv_a_ln.reshape(2, 128).T)
    bqvv = C(bias.reshape(32, 128).T)

    cos, sin = _yarn_cos_sin(S)
    sgnv = np.concatenate([-np.ones(32, np.float32), np.ones(32, np.float32)])
    ssin = sin * sgnv[None, :]
    CSKv = C(np.concatenate([cos.T, ssin.T], axis=0))

    FOLD_Knp = np.zeros((128, 64), np.float32)
    for jj in range(64):
        FOLD_Knp[jj, jj] = 1.0
        FOLD_Knp[jj + 64, jj] = 1.0
    FOLD_LOnp = np.zeros((128, 128), np.float32)
    FOLD_LOnp[:, 0:64] = FOLD_Knp
    FOLD_HInp = np.zeros((128, 128), np.float32)
    FOLD_HInp[:, 64:128] = FOLD_Knp

    constsv = np.zeros((128, 8), np.float32)
    constsv[:, C_EPSLN] = EPS_LN
    constsv[:, C_EPSAT] = EPS_ATTN
    constsv[:, C_NEGH] = -0.5
    constsv[:, C_I1024] = 1.0 / 1024.0
    constsv[:, C_I256] = 1.0 / 256.0
    constsv[:, C_I4096] = 1.0 / 4096.0
    constsv[:, C_BA2] = -0.5
    constsv[:, C_BB2] = 0.5

    bf = ml_dtypes.bfloat16
    shared = dict(
        wqaT=wqaT, wqbnT=wqbnT, wqbpT=wqbpT, wkvaT=wkvaT, wkvbnT=wkvbnT,
        wkvbvT=wkvbvT, woT=woT, WT=WT, qln=qlnv, kvln=kvlnv, bqv=bqvv, CSK=CSKv,
        FOLD_K=FOLD_Knp.astype(bf), FOLD_LO=FOLD_LOnp.astype(bf),
        FOLD_HI=FOLD_HInp.astype(bf),
        I128B=np.eye(128, dtype=np.float32).astype(bf),
        I128F=np.eye(128, dtype=np.float32),
        ONESB=np.ones((128, 1), np.float32).astype(bf),
        ONESFR=np.ones((1, 128), np.float32),
        ONESFC=np.ones((128, 1), np.float32),
        CONSTS=constsv,
    )

    in_maps = []
    xT_b = [C(x[b_].T) for b_ in range(B)]
    for core in range(8):
        b_, c_ = core // 4, core % 4
        qp = _qpos(c_)
        mw = np.where(np.arange(512)[None, :] <= 128 * c_ + np.arange(128)[:, None],
                      0.0, NEG).astype(np.float32)
        m = dict(shared)
        m.update(xT=xT_b[b_], x_tm=x[b_], xq_tm=C(x[b_][qp]),
                 xqT=C(xT_b[b_][:, qp]), CSQ=C(CSKv[:, qp]), MASKW=mw)
        in_maps.append(m)

    nc = _get_prog()
    res = run_bass_kernel_spmd(nc, in_maps, core_ids=list(range(8)))

    outf = np.zeros((B, S, D), dtype=np.float32)
    for core in range(8):
        b_, c_ = core // 4, core % 4
        qp = _qpos(c_)
        outf[b_, qp, :] = np.asarray(res.results[core]["out"]).T
    return outf


# version 11
# speedup vs baseline: 1.4855x; 1.1405x over previous
import math
import numpy as np
import ml_dtypes
import concourse.bass as bass
import concourse.mybir as mybir
from concourse.bass_utils import run_bass_kernel_spmd

F32 = mybir.dt.float32
BF16 = mybir.dt.bfloat16
AF = mybir.ActivationFunctionType
ALU = mybir.AluOpType
AX = mybir.AxisListType

B, S, D = 2, 4096, 4096
H, NOPE, ROPE, VD, KVR = 32, 64, 64, 128, 256
NQ = 1024
NJ = 8
TG = 512
NTG = S // TG
EPS_LN = 1e-5
EPS_ATTN = 1e-6
NEG = -1.0e30
_mm_ = 0.1 * 1.0 * math.log(128.0) + 1.0
SOFTMAX_SCALE = (128.0 ** -0.5) * _mm_ * _mm_

C_EPSLN, C_EPSAT, C_NEGH, C_I1024, C_I256, C_I4096, C_BA2, C_BB2 = range(8)


def _yarn_cos_sin(seq_len, dim=ROPE, base=10000.0, factor=128.0, beta_fast=32.0,
                  beta_slow=1.0, orig_max=8192, mscale=1.0, mscale_all_dim=1.0):
    def corr_dim(r):
        return dim * math.log(orig_max / (r * 2 * math.pi)) / (2 * math.log(base))
    low = max(math.floor(corr_dim(beta_fast)), 0)
    high = min(math.ceil(corr_dim(beta_slow)), dim - 1)
    hi = high + 0.001 if low == high else float(high)
    ramp = np.clip((np.arange(dim // 2, dtype=np.float32) - low) / (hi - low), 0.0, 1.0)
    inv_freq_mask = 1.0 - ramp
    freq_extra = 1.0 / base ** (np.arange(0, dim, 2, dtype=np.float32) / dim)
    freq_inter = freq_extra / factor
    inv_freq = freq_inter * (1.0 - inv_freq_mask) + freq_extra * inv_freq_mask
    t = np.arange(seq_len, dtype=np.float32)
    freqs = np.outer(t, inv_freq)
    emb = np.concatenate([freqs, freqs], axis=-1)
    def gm(s, m):
        return 1.0 if s <= 1 else 0.1 * m * math.log(s) + 1.0
    ms = gm(factor, mscale) / gm(factor, mscale_all_dim)
    return (np.cos(emb) * ms).astype(np.float32), (np.sin(emb) * ms).astype(np.float32)


def _qpos(c):
    return np.concatenate([512 * j + 128 * c + np.arange(128) for j in range(NJ)])


class KB:
    def __init__(self, nc):
        self.nc = nc
        self.sems = {}
        self.cnt = {}
        self.q = None

    def mksem(self, stk, *names):
        for n in names:
            self.sems[n] = stk.enter_context(self.nc.semaphore(n))
            self.cnt[n] = 0

    def begin(self):
        self.q = {e: [] for e in "tvsgy"}

    def wait(self, eng, sname, val=None):
        v = self.cnt[sname] if val is None else val
        if v > 0:
            h = self.sems[sname]
            self.q[eng].append(lambda e, h=h, v=v: e.wait_ge(h, v))
        return v

    def op(self, eng, fn, inc=None, incby=1):
        if inc is not None:
            h = self.sems[inc]
            self.cnt[inc] += incby
            self.q[eng].append(lambda e, fn=fn, h=h, k=incby: fn(e).then_inc(h, k))
            return self.cnt[inc]
        self.q[eng].append(fn)
        return None

    def flush(self):
        qq = self.q
        with self.nc.Block() as blk:
            if qq["y"]:
                @blk.sync
                def _(e, _q=qq["y"]):
                    for f in _q:
                        f(e)
            if qq["g"]:
                @blk.gpsimd
                def _(e, _q=qq["g"]):
                    for f in _q:
                        f(e)
            if qq["t"]:
                @blk.tensor
                def _(e, _q=qq["t"]):
                    for f in _q:
                        f(e)
            if qq["v"]:
                @blk.vector
                def _(e, _q=qq["v"]):
                    for f in _q:
                        f(e)
            if qq["s"]:
                @blk.scalar
                def _(e, _q=qq["s"]):
                    for f in _q:
                        f(e)
        self.q = None


def build_program(debug=False):
    from contextlib import ExitStack
    nc = bass.Bass("TRN2", target_bir_lowering=False)

    din = lambda n, s, d=F32: nc.dram_tensor(n, s, d, kind="ExternalInput")
    xT = din("xT", [D, S])
    x_tm = din("x_tm", [S, D])
    xq_tm = din("xq_tm", [NQ, D])
    xqT = din("xqT", [D, NQ])
    wqaT = din("wqaT", [D, 1024])
    wqbnT = din("wqbnT", [1024, 2048])
    wqbpT = din("wqbpT", [1024, 4096])
    wkvaT = din("wkvaT", [D, 384])
    wkvbnT = din("wkvbnT", [KVR, 2048])
    wkvbvT = din("wkvbvT", [KVR, 4096])
    woT = din("woT", [D, D])
    WT = din("WT", [D, D])
    qln = din("qln", [128, 8])
    kvln = din("kvln", [128, 2])
    bqv = din("bqv", [128, 32])
    CSK = din("CSK", [128, S])
    CSQ = din("CSQ", [128, NQ])
    MASKW = din("MASKW", [128, 512])
    FOLD_K = din("FOLD_K", [128, 64], BF16)
    FOLD_LO = din("FOLD_LO", [128, 128], BF16)
    FOLD_HI = din("FOLD_HI", [128, 128], BF16)
    I128B = din("I128B", [128, 128], BF16)
    I128F = din("I128F", [128, 128])
    ONESB = din("ONESB", [128, 1], BF16)
    ONESFR = din("ONESFR", [1, 128])
    ONESFC = din("ONESFC", [128, 1])
    CONSTS = din("CONSTS", [128, 8])

    xqf_scr = nc.dram_tensor("xqf_scr", [D, NQ], BF16,
                             kind="ExternalOutput" if debug else "Internal")
    out = nc.dram_tensor("out", [D, NQ], F32, kind="ExternalOutput")
    if debug:
        d_statc = nc.dram_tensor("d_statc", [128, 96], F32, kind="ExternalOutput")
        d_qstat = nc.dram_tensor("d_qstat", [128, 96], F32, kind="ExternalOutput")
        d_sw = nc.dram_tensor("d_sw", [128, 2], F32, kind="ExternalOutput")
        d_ckvn = nc.dram_tensor("d_ckvn", [128, 2 * S], BF16, kind="ExternalOutput")
        d_kpe2 = nc.dram_tensor("d_kpe2", [128, S], BF16, kind="ExternalOutput")
        d_q1n = nc.dram_tensor("d_q1n", [128, 8 * NQ], BF16, kind="ExternalOutput")
        d_oh = nc.dram_tensor("d_oh", [128, H * NQ], BF16, kind="ExternalOutput")
        d_kf0 = nc.dram_tensor("d_kf0", [128, S], BF16, kind="ExternalOutput")
        d_kf1 = nc.dram_tensor("d_kf1", [128, S], BF16, kind="ExternalOutput")
        d_vv0 = nc.dram_tensor("d_vv0", [128, S], BF16, kind="ExternalOutput")
        d_qf0 = nc.dram_tensor("d_qf0", [128, NQ], BF16, kind="ExternalOutput")
        d_qf1 = nc.dram_tensor("d_qf1", [128, NQ], BF16, kind="ExternalOutput")
        d_p0 = nc.dram_tensor("d_p0", [128, S], BF16, kind="ExternalOutput")
        d_sum = nc.dram_tensor("d_sum", [128, 16], F32, kind="ExternalOutput")
        d_acc1 = nc.dram_tensor("d_acc1", [128, NQ], F32, kind="ExternalOutput")
        d_ffny = nc.dram_tensor("d_ffny", [128, NQ], F32, kind="ExternalOutput")
        d_rows3 = nc.dram_tensor("d_rows3", [1, 3 * NQ], F32, kind="ExternalOutput")
        d_p1 = nc.dram_tensor("d_p1", [128, 1024], BF16, kind="ExternalOutput")

    from contextlib import ExitStack as _ES
    stk = _ES()
    sb = lambda n, s, d: stk.enter_context(nc.sbuf_tensor(n, s, d))

    kb = KB(nc)
    kb.mksem(stk, "DS", "DG", "T", "V", "Sg", "Ya", "Yb", "Yc", "Ga", "Gb",
             "Gc", "Gd", "Ge", "Gf", "Gg", "Gh", "Gi", "Gj", "Gk", "Gl", "Gm",
             "Yd", "Ye")

    consts = sb("consts", [128, 8], F32)
    ones_b = sb("ones_b", [128, 1], BF16)
    onesf_r = sb("onesf_r", [1, 128], F32)
    onesf_c = sb("onesf_c", [128, 1], F32)
    i128_b = sb("i128_b", [128, 128], BF16)
    i128_f = sb("i128_f", [128, 128], F32)
    fold_k = sb("fold_k", [128, 64], BF16)
    fold_lo = sb("fold_lo", [128, 128], BF16)
    fold_hi = sb("fold_hi", [128, 128], BF16)
    maskw = sb("maskw", [128, 512], F32)
    csq = sb("csq", [128, NQ], F32)
    qln_sb = sb("qln_sb", [128, 8], F32)
    kvln_sb = sb("kvln_sb", [128, 2], F32)
    bq_sb = sb("bq_sb", [128, 32], F32)

    statc = sb("statc", [128, 96], F32)
    qsum = sb("qsum", [128, 8], F32)
    qss = sb("qss", [128, 8], F32)
    qmx = sb("qmx", [128, 8], F32)
    qmn = sb("qmn", [128, 8], F32)
    qvar = sb("qvar", [128, 8], F32)
    wc = sb("wc", [128, 8], F32)
    wc2 = sb("wc2", [128, 8], F32)
    wc3 = sb("wc3", [128, 8], F32)
    wc4 = sb("wc4", [128, 8], F32)
    sumc = sb("sumc", [128, 16], F32)
    qm = sb("qm", [128, 8], F32)
    qrs = sb("qrs", [128, 8], F32)
    qrs2 = sb("qrs2", [128, 8], F32)
    qR = sb("qR", [128, 8], F32)
    qDIV = sb("qDIV", [128, 8], F32)
    nmR = sb("nmR", [128, 8], F32)
    nmRp = sb("nmRp", [128, 8], F32)
    nmRh = sb("nmRh", [128, 8], F32)
    swcol = sb("swcol", [128, 8], F32)
    winv = sb("winv", [128, 8], F32)

    ckv_n = sb("ckv_n", [128, 2 * S], BF16)
    kpe2 = sb("kpe2", [128, S], BF16)
    q1n = sb("q1n", [128, 8 * NQ], BF16)

    def make_row_ops(colsrc_fn, ctmp_ap, ncols, tp_ps, rowtmp, dstrow):
        kb.wait("v", "Sg")
        kb.wait("v", "DG")
        kb.op("v", colsrc_fn, inc="V")
        kb.wait("t", "V")
        kb.op("t", lambda t: t.transpose(tp_ps[0:ncols, 0:128], ctmp_ap, i128_f[:]),
              inc="T")
        kb.wait("v", "T")
        kb.op("v", lambda v: v.tensor_copy(rowtmp[0:ncols, :], tp_ps[0:ncols, 0:128]),
              inc="V")
        kb.wait("g", "V")
        kb.wait("g", "T")
        kb.op("g", lambda g: g.dma_start(
            dstrow[:].rearrange("o (a b) -> o a b", a=ncols),
            rowtmp[0:ncols, :]), inc="DG", incby=16)

    kb.begin()
    for dst_, src_ in [(consts, CONSTS), (maskw, MASKW), (csq, CSQ),
                       (qln_sb, qln), (kvln_sb, kvln), (bq_sb, bqv),
                       (onesf_r, ONESFR), (onesf_c, ONESFC), (i128_f, I128F),
                       (ones_b, ONESB), (i128_b, I128B), (fold_k, FOLD_K),
                       (fold_lo, FOLD_LO), (fold_hi, FOLD_HI)]:
        kb.op("y", lambda y, d=dst_, s=src_: y.dma_start(d[:], s[:]),
              inc="DS", incby=16)
    kb.wait("y", "DS")
    kb.flush()

    with _ES() as loc:
        lsb = lambda n, s, d: loc.enter_context(nc.sbuf_tensor(n, s, d))
        lps = lambda n, s, d: loc.enter_context(nc.psum_tensor(n, s, d))
        kb.begin()
        xtm_ring = [lsb(f"xtm{i}", [128, D], F32) for i in range(2)]
        sq_scr = lsb("sq_scr", [128, D], BF16)
        sxc = lsb("sxc", [128, 32], F32)
        sxxc = lsb("sxxc", [128, 32], F32)
        wt_ring = [lsb(f"wtp{i}", [128, D], F32) for i in range(2)]
        wabs = lsb("wabs", [128, 32], F32)
        ones8f = lsb("ones8f", [128, 8], F32)
        ones128 = lsb("ones128", [128, 128], F32)
        bc8_ps = lps("bc8_ps", [128, 8], F32)

        kb.op("v", lambda v: v.memset(ones128[:], 1.0), inc="V")
        v_done = {}
        s_done = {}
        for ti in range(32):
            buf = xtm_ring[ti % 2]
            kb.wait("y", "V", v_done.get(ti - 2, 0))
            kb.wait("y", "Sg", s_done.get(ti - 2, 0))
            sm = ("Ya", "Yb")[ti % 2]
            dtgt = kb.op("y", lambda y, buf=buf, ti=ti:
                         y.dma_start(buf[:], x_tm[128 * ti:128 * ti + 128, :]),
                         inc=sm, incby=16)
            kb.wait("v", sm, dtgt)
            v_done[ti] = kb.op("v", lambda v, buf=buf, ti=ti:
                               v.reduce_sum(sxc[:, ti:ti + 1], buf[:], axis=AX.X),
                               inc="V")
            kb.wait("s", sm, dtgt)
            s_done[ti] = kb.op("s", lambda s, buf=buf, ti=ti:
                               s.activation(sq_scr[:], buf[:], AF.Square,
                                            accum_out=sxxc[:, ti:ti + 1]),
                               inc="Sg")

        kb.wait("v", "V")
        kb.wait("v", "Sg")
        def derive_stats(v):
            v.tensor_scalar(statc[:, 0:64:2], sxc[:, 0:32], 1.0 / D, None,
                            op0=ALU.mult)
            v.tensor_scalar(statc[:, 1:64:2], sxxc[:, 0:32], 1.0 / D, None,
                            op0=ALU.mult)
            v.tensor_mul(statc[:, 64:96], statc[:, 0:64:2], statc[:, 0:64:2])
            return v.tensor_sub(statc[:, 1:64:2], statc[:, 1:64:2], statc[:, 64:96])
        kb.op("v", derive_stats, inc="V")
        kb.wait("s", "V")
        def rs_from_var(s):
            s.activation(statc[:, 1:64:2], statc[:, 1:64:2], AF.Ln,
                         bias=consts[:, C_EPSLN:C_EPSLN + 1], scale=1.0)
            return s.activation(statc[:, 1:64:2], statc[:, 1:64:2], AF.Exp,
                                bias=0.0, scale=consts[:, C_NEGH:C_NEGH + 1])
        kb.op("s", rs_from_var, inc="Sg")

        wv_done = {}
        for wi in range(32):
            buf = wt_ring[wi % 2]
            kb.wait("y", "V", wv_done.get(wi - 2, 0))
            sm = ("Ya", "Yb")[wi % 2]
            dtgt = kb.op("y", lambda y, buf=buf, wi=wi:
                         y.dma_start(buf[:], WT[128 * wi:128 * wi + 128, :]),
                         inc=sm, incby=16)
            kb.wait("v", sm, dtgt)
            wv_done[wi] = kb.op("v", lambda v, buf=buf, wi=wi:
                                v.reduce_sum(wabs[:, wi:wi + 1], buf[:], axis=AX.X,
                                             apply_absolute_value=True), inc="V")
        kb.wait("v", "V")
        def wsum_fin(v):
            v.reduce_sum(wabs[:, 0:1], wabs[:, 0:32], axis=AX.X)
            v.tensor_copy(wc2[:], consts[:])
            return v.tensor_scalar(ones8f[:], ones128[:, 0:8], wabs[:, 0:1],
                                   None, op0=ALU.mult)
        kb.op("v", wsum_fin, inc="V")
        kb.wait("t", "V")
        kb.op("t", lambda t: t.matmul(bc8_ps[:], ones128[:], ones8f[:],
                                      start=True, stop=True), inc="T")
        kb.wait("v", "T")
        def sw_fin(v):
            v.tensor_scalar(wc3[:], bc8_ps[:], 1.0 / (4096.0 * 4096.0), None,
                            op0=ALU.mult)
            v.tensor_copy(wc2[:], consts[:])
            v.tensor_scalar(winv[:], wc3[:], 1e-5, None, op0=ALU.max)
            v.tensor_copy(wc2[:], consts[:])
            return v.reciprocal(swcol[:], winv[:])
        kb.op("v", sw_fin, inc="V")

        kb.flush()

    with _ES() as loc:
        lsb = lambda n, s, d: loc.enter_context(nc.sbuf_tensor(n, s, d))
        lps = lambda n, s, d: loc.enter_context(nc.psum_tensor(n, s, d))
        kb.begin()
        kb.wait("y", "V")
        kb.wait("y", "Sg")
        kb.wait("y", "T")
        tp_b = lps("tp_b", [128, 1024], BF16)
        stage = lsb("stage", [128, 1024], BF16)
        qxr = [lsb(f"qxr{i}", [128, D], F32) for i in range(2)]
        sqf = lsb("sqf", [128, D], F32)
        ngf = lsb("ngf", [128, D], F32)
        wbuf = lsb("wbuf", [128, D], F32)
        bmask = lsb("bmask", [128, D], F32)
        xq_t = lsb("xq_t", [128, D], BF16)

        qv_done = {}
        for jt in range(NJ):
            buf = qxr[jt % 2]
            kb.wait("y", "V", qv_done.get(jt - 2, 0))
            sm = ("Ya", "Yb")[jt % 2]
            dtgt = kb.op("y", lambda y, buf=buf, jt=jt:
                         y.dma_start(buf[:], xq_tm[128 * jt:128 * jt + 128, :]),
                         inc=sm, incby=16)
            kb.wait("v", sm, dtgt)
            def qredu(v, buf=buf, jt=jt):
                v.tensor_mul(sqf[:], buf[:], buf[:])
                v.tensor_scalar(ngf[:], buf[:], -1.0, None, op0=ALU.mult)
                v.reduce_sum(qss[:, jt:jt + 1], sqf[:], axis=AX.X)
                v.reduce_max(qmn[:, jt:jt + 1], ngf[:], axis=AX.X)
                v.reduce_sum(qsum[:, jt:jt + 1], buf[:], axis=AX.X)
                return v.reduce_max(qmx[:, jt:jt + 1], buf[:], axis=AX.X)
            qv_done[jt] = kb.op("v", qredu, inc="V")

        kb.wait("v", "V")
        def qderive(v):
            v.tensor_scalar(qm[:], qsum[:], 1.0 / D, None, op0=ALU.mult)
            return v.tensor_scalar(wc[:], qss[:], 1.0 / D, None, op0=ALU.mult)
        kb.op("v", qderive, inc="V")
        kb.wait("s", "V")
        kb.op("s", lambda s: s.activation(wc2[:], qm[:], AF.Square), inc="Sg")
        kb.wait("v", "Sg")
        kb.op("v", lambda v: v.tensor_sub(qvar[:], wc[:], wc2[:]), inc="V")
        kb.wait("s", "V")
        def qrs_op(s):
            s.activation(qrs[:], qvar[:], AF.Ln,
                         bias=consts[:, C_EPSLN:C_EPSLN + 1], scale=1.0)
            return s.activation(qrs[:], qrs[:], AF.Exp, bias=0.0,
                                scale=consts[:, C_NEGH:C_NEGH + 1])
        kb.op("s", qrs_op, inc="Sg")
        kb.wait("v", "Sg")
        def qmean2(v):
            v.tensor_mul(wc2[:], qvar[:], qrs[:])
            v.tensor_copy(wc3[:], consts[:, 0:8])
            return v.tensor_mul(wc[:], wc2[:], qrs[:])
        kb.op("v", qmean2, inc="V")
        kb.wait("s", "V")
        def qrs2_op(s):
            s.activation(qrs2[:], wc[:], AF.Ln,
                         bias=consts[:, C_EPSLN:C_EPSLN + 1], scale=1.0)
            return s.activation(qrs2[:], qrs2[:], AF.Exp, bias=0.0,
                                scale=consts[:, C_NEGH:C_NEGH + 1])
        kb.op("s", qrs2_op, inc="Sg")
        kb.wait("v", "Sg")
        def qfin(v):
            v.tensor_sub(wc[:], qmx[:], qm[:])
            v.tensor_add(wc2[:], qm[:], qmn[:])
            v.tensor_mul(wc3[:], qrs[:], qrs2[:])
            v.tensor_max(wc4[:], wc[:], wc2[:])
            v.tensor_copy(wc[:], consts[:, 0:8])
            v.tensor_mul(wc2[:], wc4[:], wc3[:])
            v.tensor_copy(wc4[:], consts[:, 0:8])
            v.tensor_scalar(wc[:], wc2[:], 1e-5, None, op0=ALU.max)
            v.tensor_copy(wc4[:], consts[:, 0:8])
            v.tensor_mul(wc2[:], wc[:], winv[:])
            v.reciprocal(wc4[:], wc[:])
            v.tensor_scalar(qDIV[:], wc2[:], 0.5 / 127.0, None, op0=ALU.mult)
            v.tensor_mul(wc2[:], wc4[:], wc3[:])
            v.tensor_copy(wc4[:], consts[:, 0:8])
            v.tensor_scalar(qR[:], wc2[:], 127.0, None, op0=ALU.mult)
            v.tensor_copy(wc4[:], consts[:, 0:8])
            v.tensor_mul(wc2[:], qm[:], qR[:])
            v.tensor_copy(wc4[:], consts[:, 0:8])
            v.tensor_scalar(nmR[:], wc2[:], -1.0, None, op0=ALU.mult)
            v.tensor_copy(wc4[:], consts[:, 0:8])
            v.tensor_scalar(nmRp[:], nmR[:], 128.5, None, op0=ALU.add)
            return v.tensor_scalar(nmRh[:], nmR[:], 0.5, None, op0=ALU.add)
        kb.op("v", qfin, inc="V")

        kb.wait("y", "V")
        kb.wait("y", "Sg")
        wq_done = {}
        for jt in range(NJ):
            buf = qxr[jt % 2]
            kb.wait("y", "Sg", wq_done.get(jt - 2, 0))
            sm = ("Ya", "Yb")[jt % 2]
            dtgt = kb.op("y", lambda y, buf=buf, jt=jt:
                         y.dma_start(buf[:], xq_tm[128 * jt:128 * jt + 128, :]),
                         inc=sm, incby=16)
            kb.wait("s", sm, dtgt)
            kb.wait("s", "V")
            wq_done[jt] = kb.op("s", lambda s, buf=buf, jt=jt: s.activation(
                wbuf[:], buf[:], AF.Copy, bias=0.0,
                scale=qR[:, jt:jt + 1]), inc="Sg")
            kb.wait("v", "Sg")
            def qchain(v, jt=jt):
                v.tensor_scalar(sqf[:], wbuf[:], nmRp[:, jt:jt + 1], None,
                                op0=ALU.add)
                cur, nxt = sqf, ngf
                for k in (128.0, 64.0, 32.0, 16.0, 8.0, 4.0, 2.0, 1.0):
                    v.tensor_scalar(bmask[:], cur[:], k, None, op0=ALU.is_ge)
                    v.scalar_tensor_tensor(nxt[:], bmask[:], -k, cur[:],
                                           op0=ALU.mult, op1=ALU.add)
                    cur, nxt = nxt, cur
                v.tensor_sub(bmask[:], wbuf[:], cur[:])
                return v.tensor_scalar(xq_t[:], bmask[:], nmRh[:, jt:jt + 1],
                                       None, op0=ALU.add)
            kb.op("v", qchain, inc="V")
            for cg in range(4):
                kb.wait("t", "V")
                kb.wait("t", "DG")
                def qtr(t, cg=cg):
                    last = None
                    for u in range(8):
                        last = t.transpose(tp_b[:, 128 * u:128 * u + 128],
                                           xq_t[:, 128 * (8 * cg + u):
                                                 128 * (8 * cg + u) + 128],
                                           i128_b[:])
                    return last
                ttg = kb.op("t", qtr, inc="T")
                kb.wait("v", "T", ttg)
                kb.wait("v", "DG")
                kb.op("v", lambda v: v.tensor_copy(stage[:], tp_b[:]), inc="V")
                kb.wait("g", "V")
                def qst(g, jt=jt, cg=cg):
                    r0 = 1024 * cg
                    return g.dma_start(
                        xqf_scr[r0:r0 + 1024, 128 * jt:128 * jt + 128]
                        .rearrange("(u p) q -> p u q", p=128),
                        stage[:].rearrange("p (u q) -> p u q", q=128))
                kb.op("g", qst, inc="DG", incby=16)
        if debug:
            kb.wait("y", "V")
            kb.wait("y", "Sg")
            kb.op("y", lambda y: y.dma_start(d_statc[:], statc[:]), inc="DS", incby=16)
            for ci_, src_ in enumerate([qm, qrs, qrs2, qR, qDIV, qvar,
                                        qmx, qmn, qsum, qss, wc, wc3]):
                kb.op("y", lambda y, ci_=ci_, src_=src_: y.dma_start(
                    d_qstat[:, 8 * ci_:8 * ci_ + 8], src_[:]), inc="DS", incby=16)
            kb.op("y", lambda y: y.dma_start(d_sw[:, 0:1], swcol[:, 0:1], allow_slow_non_contiguous=True), inc="DS", incby=16)
            kb.op("y", lambda y: y.dma_start(d_sw[:, 1:2], winv[:, 0:1], allow_slow_non_contiguous=True), inc="DS", incby=16)
            kb.wait("y", "DS")
        kb.wait("g", "DG")
        kb.flush()

    with _ES() as loc:
        lsb = lambda n, s, d: loc.enter_context(nc.sbuf_tensor(n, s, d))
        lps = lambda n, s, d: loc.enter_context(nc.psum_tensor(n, s, d))
        kb.begin()
        wkva_bf = lsb("wkva_bf", [128, 32 * 384], BF16)
        csk_sb = lsb("csk_sb", [128, S], F32)
        xbf_ring = [lsb(f"xbf{i}", [128, 32 * TG], BF16) for i in range(2)]
        kv_ps = lps("kv_ps", [128, 3 * TG], F32)
        bc_ps = lps("bc_ps", [128, TG], F32)
        st_ps = lps("st_ps", [1, TG], F32)
        fo_ps = lps("fo_ps", [64, TG], F32)
        wsum_ps = lps("wsum_ps", [128, 3], F32)
        tp4_ps = lps("tp4_ps", [4, 128], F32)
        mbc = lsb("mbc", [128, TG], F32)
        nrsbc = lsb("nrsbc", [128, TG], F32)
        rcbc = lsb("rcbc", [128, TG], F32)
        ckv_f = lsb("ckv_f", [128, 2 * TG], F32)
        kpe_f = lsb("kpe_f", [128, TG], F32)
        w1 = lsb("w1", [128, TG], BF16)
        sqb = lsb("sqb", [128, 2 * TG], BF16)
        rrow = lsb("rrow", [1, TG], F32)
        wsumc = lsb("wsumc", [128, 3], F32)
        mseg = lsb("mseg", [1, TG], F32)
        nrseg = lsb("nrseg", [1, TG], F32)
        ctmp4 = lsb("ctmp4", [128, 4], F32)
        rowt4 = lsb("rowt4", [4, 128], F32)

        wk_tgt = kb.op("g", lambda g: g.dma_start(
            wkva_bf[:].rearrange("p (ci o) -> p ci o", o=384),
            wkvaT[:].rearrange("(ci p) o -> p ci o", p=128)), inc="Ge", incby=16)
        csk_tgt = kb.op("y", lambda y: y.dma_start(csk_sb[:], CSK[:]),
                        inc="Yc", incby=16)
        kb.wait("t", "Ge", wk_tgt)
        def wsums(t):
            last = None
            for oc in range(3):
                for ci in range(32):
                    last = t.matmul(
                        wsum_ps[:, oc:oc + 1],
                        wkva_bf[:, ci * 384 + 128 * oc:ci * 384 + 128 * oc + 128],
                        ones_b[:], start=(ci == 0), stop=(ci == 31))
            return last
        kb.op("t", wsums, inc="T")
        kb.wait("v", "T")
        kb.op("v", lambda v: v.tensor_copy(wsumc[:], wsum_ps[:]), inc="V")

        xg_free = {}
        for g_ in range(NTG):
            buf = xbf_ring[g_ % 2]
            kb.wait("g", "T", xg_free.get(g_ - 2, 0))
            sm = ("Ga", "Gb")[g_ % 2]
            dtgt = kb.op("g", lambda g, buf=buf, g_=g_: g.dma_start(
                buf[:].rearrange("p (ci f) -> p ci f", f=TG),
                xT[:, TG * g_:TG * g_ + TG].rearrange("(ci p) f -> p ci f", p=128)),
                inc=sm, incby=16)

            make_row_ops(
                lambda v, g_=g_: v.tensor_copy(ctmp4[:],
                                               statc[:, 8 * g_:8 * g_ + 8:2]),
                ctmp4[:], 4, tp4_ps, rowt4, mseg)
            def nrs_src(v, g_=g_):
                return v.tensor_scalar(ctmp4[:], statc[:, 8 * g_ + 1:8 * g_ + 9:2],
                                       -1.0, None, op0=ALU.mult)
            make_row_ops(nrs_src, ctmp4[:], 4, tp4_ps, rowt4, nrseg)

            kb.wait("t", "V")
            kb.wait("t", "DG")
            mtg = kb.op("t", lambda t: t.matmul(bc_ps[:], onesf_r[:], mseg[:],
                                                start=True, stop=True), inc="T")
            kb.wait("v", "T", mtg)
            kb.op("v", lambda v: v.tensor_copy(mbc[:], bc_ps[:]), inc="V")
            kb.wait("t", "V")
            mtg = kb.op("t", lambda t: t.matmul(bc_ps[:], onesf_r[:], nrseg[:],
                                                start=True, stop=True), inc="T")
            kb.wait("v", "T", mtg)
            kb.op("v", lambda v: v.tensor_copy(nrsbc[:], bc_ps[:]), inc="V")

            kb.wait("t", sm, dtgt)
            kb.wait("t", "V")
            def kva(t, buf=buf):
                last = None
                for oc in range(3):
                    for ci in range(32):
                        last = t.matmul(
                            kv_ps[:, TG * oc:TG * oc + TG],
                            wkva_bf[:, ci * 384 + 128 * oc:ci * 384 + 128 * oc + 128],
                            buf[:, ci * TG:ci * TG + TG],
                            start=(ci == 0), stop=(ci == 31))
                return last
            mm_tgt = kb.op("t", kva, inc="T")
            xg_free[g_] = mm_tgt

            kb.wait("v", "T", mm_tgt)
            def corr(v):
                for oc in range(2):
                    v.scalar_tensor_tensor(ckv_f[:, TG * oc:TG * oc + TG], mbc[:],
                                           wsumc[:, oc:oc + 1],
                                           kv_ps[:, TG * oc:TG * oc + TG],
                                           op0=ALU.mult, op1=ALU.subtract)
                    v.tensor_mul(ckv_f[:, TG * oc:TG * oc + TG],
                                 ckv_f[:, TG * oc:TG * oc + TG], nrsbc[:])
                v.scalar_tensor_tensor(kpe_f[:], mbc[:], wsumc[:, 2:3],
                                       kv_ps[:, 2 * TG:3 * TG],
                                       op0=ALU.mult, op1=ALU.subtract)
                return v.tensor_mul(kpe_f[:], kpe_f[:], nrsbc[:])
            kb.op("v", corr, inc="V")

            kb.wait("s", "V")
            kb.op("s", lambda s: s.activation(sqb[:], ckv_f[:], AF.Square), inc="Sg")
            kb.wait("t", "Sg")
            def sumsq(t):
                t.matmul(st_ps[:], ones_b[:], sqb[:, 0:TG], start=True, stop=False)
                return t.matmul(st_ps[:], ones_b[:], sqb[:, TG:2 * TG],
                                start=False, stop=True)
            mtg = kb.op("t", sumsq, inc="T")
            kb.wait("s", "T", mtg)
            def rsc(s):
                s.activation(rrow[:], st_ps[:], AF.Ln,
                             bias=consts[0:1, C_EPSAT:C_EPSAT + 1],
                             scale=consts[0:1, C_I256:C_I256 + 1])
                return s.activation(rrow[:], rrow[:], AF.Exp, bias=0.0,
                                    scale=consts[0:1, C_NEGH:C_NEGH + 1])
            kb.op("s", rsc, inc="Sg")
            kb.wait("t", "Sg")
            mtg = kb.op("t", lambda t: t.matmul(bc_ps[:], onesf_r[:], rrow[:],
                                                start=True, stop=True), inc="T")
            kb.wait("v", "T", mtg)
            def ckvn(v):
                v.tensor_copy(rcbc[:], bc_ps[:])
                v.tensor_mul(ckv_f[:, 0:TG], ckv_f[:, 0:TG], rcbc[:])
                return v.tensor_mul(ckv_f[:, TG:2 * TG], ckv_f[:, TG:2 * TG],
                                    rcbc[:])
            kb.op("v", ckvn, inc="V")
            kb.wait("s", "V")
            def ckv_gamma(s, g_=g_):
                s.activation(ckv_n[:, 0 * S + TG * g_:0 * S + TG * g_ + TG],
                             ckv_f[:, 0:TG], AF.Copy, bias=0.0,
                             scale=kvln_sb[:, 0:1])
                return s.activation(ckv_n[:, 1 * S + TG * g_:1 * S + TG * g_ + TG],
                                    ckv_f[:, TG:2 * TG], AF.Copy, bias=0.0,
                                    scale=kvln_sb[:, 1:2])
            kb.op("s", ckv_gamma, inc="Sg")

            kb.wait("v", "Yc", csk_tgt)
            def rope1(v, g_=g_):
                return v.tensor_tensor(w1[:], kpe_f[:],
                                       csk_sb[:, TG * g_:TG * g_ + TG], op=ALU.mult)
            kb.op("v", rope1, inc="V")
            kb.wait("t", "V")
            kb.wait("t", "Sg")
            mtg = kb.op("t", lambda t: t.matmul(fo_ps[:], fold_k[:], w1[:],
                                                start=True, stop=True), inc="T")
            kb.wait("s", "T", mtg)
            kb.op("s", lambda s, g_=g_: s.copy(kpe2[0:64, TG * g_:TG * g_ + TG],
                                               fo_ps[:]), inc="Sg")

        kb.wait("g", "Sg")
        kb.op("g", lambda g: g.dma_start(kpe2[64:128, :], kpe2[0:64, :]),
              inc="DG", incby=16)
        kb.wait("g", "DG")
        if debug:
            kb.op("g", lambda g: g.dma_start(d_ckvn[:], ckv_n[:]), inc="DG", incby=16)
            kb.op("g", lambda g: g.dma_start(d_kpe2[:], kpe2[:]), inc="DG", incby=16)
            kb.wait("g", "DG")
        kb.flush()

    with _ES() as loc:
        lsb = lambda n, s, d: loc.enter_context(nc.sbuf_tensor(n, s, d))
        lps = lambda n, s, d: loc.enter_context(nc.psum_tensor(n, s, d))
        kb.begin()
        xbfq = lsb("xbfq", [128, 32 * NQ], BF16)
        wqa_ring = [lsb(f"wqa{i}", [128, 32 * 128], BF16) for i in range(2)]
        q1_ps = lps("q1_ps", [128, 512], F32)
        ws_ps = lps("ws_ps", [128, 1], F32)
        st2_ps = lps("st2_ps", [1, 512], F32)
        bc2_ps = lps("bc2_ps", [128, 512], F32)
        tp8_ps = lps("tp8_ps", [8, 128], F32)
        mbcq = lsb("mbcq", [128, 512], F32)
        nbcq = lsb("nbcq", [128, 512], F32)
        rbcq = lsb("rbcq", [128, 512], F32)
        q1f = lsb("q1f", [128, 8 * 512], F32)
        sq2 = [lsb(f"sq2{i}", [128, 512], BF16) for i in range(2)]
        r2row = lsb("r2row", [1, 512], F32)
        wsq = lsb("wsq", [128, 8], F32)
        mqrow = lsb("mqrow", [1, NQ], F32)
        nrsqrow = lsb("nrsqrow", [1, NQ], F32)
        ctmp8 = lsb("ctmp8", [128, 8], F32)
        rowt8 = lsb("rowt8", [8, 128], F32)

        xq_tgt = kb.op("g", lambda g: g.dma_start(
            xbfq[:].rearrange("p (ci f) -> p ci f", f=NQ),
            xqT[:].rearrange("(ci p) f -> p ci f", p=128)), inc="DG", incby=16)

        make_row_ops(lambda v: v.tensor_copy(ctmp8[:], qm[:]),
                     ctmp8[:], 8, tp8_ps, rowt8, mqrow)
        make_row_ops(lambda v: v.tensor_scalar(ctmp8[:], qrs[:], -1.0,
                                               None, op0=ALU.mult),
                     ctmp8[:], 8, tp8_ps, rowt8, nrsqrow)

        wq_free = {}
        for th in range(2):
            kb.wait("t", "V")
            kb.wait("t", "DG")
            mtg = kb.op("t", lambda t, th=th: t.matmul(
                bc2_ps[:], onesf_r[:], mqrow[:, 512 * th:512 * th + 512],
                start=True, stop=True), inc="T")
            kb.wait("v", "T", mtg)
            kb.op("v", lambda v: v.tensor_copy(mbcq[:], bc2_ps[:]), inc="V")
            kb.wait("t", "V")
            mtg = kb.op("t", lambda t, th=th: t.matmul(
                bc2_ps[:], onesf_r[:], nrsqrow[:, 512 * th:512 * th + 512],
                start=True, stop=True), inc="T")
            kb.wait("v", "T", mtg)
            kb.op("v", lambda v: v.tensor_copy(nbcq[:], bc2_ps[:]), inc="V")

            for oc in range(8):
                wbuf = wqa_ring[oc % 2]
                kb.wait("g", "T", wq_free.get(8 * th + oc - 2, 0))
                dtgt = kb.op("g", lambda g, wbuf=wbuf, oc=oc: g.dma_start(
                    wbuf[:].rearrange("p (ci f) -> p ci f", f=128),
                    wqaT[:, 128 * oc:128 * oc + 128]
                    .rearrange("(ci p) f -> p ci f", p=128)), inc="DG", incby=16)
                kb.wait("t", "DG", dtgt)
                kb.wait("t", "V")
                if th == 0:
                    def wsum_q(t, wbuf=wbuf):
                        last = None
                        for ci in range(32):
                            last = t.matmul(ws_ps[:],
                                            wbuf[:, 128 * ci:128 * ci + 128],
                                            ones_b[:], start=(ci == 0),
                                            stop=(ci == 31))
                        return last
                    kb.op("t", wsum_q, inc="T")
                    kb.wait("v", "T")
                    kb.op("v", lambda v, oc=oc: v.tensor_copy(wsq[:, oc:oc + 1],
                                                              ws_ps[:]), inc="V")
                    kb.wait("t", "V")
                kb.wait("t", "DG", xq_tgt)
                def q1mm(t, wbuf=wbuf, th=th):
                    last = None
                    for ci in range(32):
                        last = t.matmul(
                            q1_ps[:], wbuf[:, 128 * ci:128 * ci + 128],
                            xbfq[:, ci * NQ + 512 * th:ci * NQ + 512 * th + 512],
                            start=(ci == 0), stop=(ci == 31))
                    return last
                mm_tgt = kb.op("t", q1mm, inc="T")
                wq_free[8 * th + oc] = mm_tgt
                kb.wait("v", "T", mm_tgt)
                def q1corr(v, oc=oc):
                    v.scalar_tensor_tensor(q1f[:, 512 * oc:512 * oc + 512], mbcq[:],
                                           wsq[:, oc:oc + 1], q1_ps[:],
                                           op0=ALU.mult, op1=ALU.subtract)
                    return v.tensor_mul(q1f[:, 512 * oc:512 * oc + 512],
                                        q1f[:, 512 * oc:512 * oc + 512], nbcq[:])
                kb.op("v", q1corr, inc="V")

            for oc in range(8):
                kb.wait("s", "V")
                kb.wait("s", "T")
                kb.op("s", lambda s, oc=oc: s.activation(
                    sq2[oc % 2][:], q1f[:, 512 * oc:512 * oc + 512], AF.Square),
                    inc="Sg")
                kb.wait("t", "Sg")
                kb.op("t", lambda t, oc=oc: t.matmul(
                    st2_ps[:], ones_b[:], sq2[oc % 2][:],
                    start=(oc == 0), stop=(oc == 7)), inc="T")
            kb.wait("s", "T")
            def rs2q(s):
                s.activation(r2row[:], st2_ps[:], AF.Ln,
                             bias=consts[0:1, C_EPSAT:C_EPSAT + 1],
                             scale=consts[0:1, C_I1024:C_I1024 + 1])
                return s.activation(r2row[:], r2row[:], AF.Exp, bias=0.0,
                                    scale=consts[0:1, C_NEGH:C_NEGH + 1])
            kb.op("s", rs2q, inc="Sg")
            kb.wait("v", "Sg")
            kb.op("v", lambda v: v.tensor_scalar(r2row[:], r2row[:], SOFTMAX_SCALE,
                                                 None, op0=ALU.mult), inc="V")
            kb.wait("t", "V")
            mtg = kb.op("t", lambda t: t.matmul(bc2_ps[:], onesf_r[:], r2row[:],
                                                start=True, stop=True), inc="T")
            kb.wait("v", "T", mtg)
            def q1n_mul(v):
                v.tensor_copy(rbcq[:], bc2_ps[:])
                last = None
                for oc in range(8):
                    last = v.tensor_mul(q1f[:, 512 * oc:512 * oc + 512],
                                        q1f[:, 512 * oc:512 * oc + 512], rbcq[:])
                return last
            kb.op("v", q1n_mul, inc="V")
            kb.wait("s", "V")
            def q1n_gam(s, th=th):
                last = None
                for oc in range(8):
                    last = s.activation(
                        q1n[:, oc * NQ + 512 * th:oc * NQ + 512 * th + 512],
                        q1f[:, 512 * oc:512 * oc + 512], AF.Copy, bias=0.0,
                        scale=qln_sb[:, oc:oc + 1])
                return last
            kb.op("s", q1n_gam, inc="Sg")
        kb.wait("t", "Sg")
        if debug:
            kb.wait("g", "Sg")
            kb.op("g", lambda g: g.dma_start(d_q1n[:], q1n[:]), inc="DG", incby=16)
            kb.wait("g", "DG")
        kb.flush()

    oh_scr = nc.dram_tensor("oh_scr", [D, NQ], BF16, kind="Internal")

    with _ES() as loc:
        lsb = lambda n, s, d: loc.enter_context(nc.sbuf_tensor(n, s, d))
        lps = lambda n, s, d: loc.enter_context(nc.psum_tensor(n, s, d))
        kb.begin()
        wbn_ring = [lsb(f"wbn{i}", [128, 2 * 128], BF16) for i in range(2)]
        wbv_ring = [lsb(f"wbv{i}", [128, 2 * 256], BF16) for i in range(2)]
        wqn_ring = [lsb(f"wqn{i}", [128, 8 * 128], BF16) for i in range(2)]
        wqp_ring = [lsb(f"wqp{i}", [128, 8 * 256], BF16) for i in range(2)]
        kf = [lsb(f"kf{i}", [128, S], BF16) for i in range(2)]
        vv = [lsb(f"vv{i}", [128, S], BF16) for i in range(2)]
        qf = [lsb(f"qf{i}", [128, NQ], BF16) for i in range(2)]
        qpe_f = lsb("qpe_f", [128, NQ], F32)
        w1q = lsb("w1q", [128, NQ], BF16)
        p_sb = [lsb(f"p_sb{i}", [128, S], BF16) for i in range(2)]
        ptbuf = [lsb(f"ptb{i}", [128, S], BF16) for i in range(2)]
        rowr = lsb("rowr", [8, 128], F32)
        ohf = lsb("ohf", [128, NQ], F32)
        rrow2 = lsb("rrow2", [1, NQ], F32)

        oh_hbuf = [lsb(f"ohb{i}", [128, NQ], BF16) for i in range(2)]
        arena = lps("arena", [128, 4096], F32)
        arena_b = arena[:].bitcast(BF16)
        pexp_gate = [0, 0]
        oh_st_tgt = [0, 0]

        def load_pair(pi, ridx):
            sm = "DG"
            kb.op("g", lambda g, pi=pi, ridx=ridx: g.dma_start(
                wbn_ring[ridx][:].rearrange("p (cc f) -> p cc f", f=128),
                wkvbnT[:, 128 * pi:128 * pi + 128]
                .rearrange("(cc p) f -> p cc f", p=128)), inc=sm, incby=16)
            kb.op("g", lambda g, pi=pi, ridx=ridx: g.dma_start(
                wbv_ring[ridx][:].rearrange("p (cc f) -> p cc f", f=256),
                wkvbvT[:, 256 * pi:256 * pi + 256]
                .rearrange("(cc p) f -> p cc f", p=128)), inc=sm, incby=16)
            kb.op("g", lambda g, pi=pi, ridx=ridx: g.dma_start(
                wqn_ring[ridx][:].rearrange("p (ci f) -> p ci f", f=128),
                wqbnT[:, 128 * pi:128 * pi + 128]
                .rearrange("(ci p) f -> p ci f", p=128)), inc=sm, incby=16)
            return kb.op("g", lambda g, pi=pi, ridx=ridx: g.dma_start(
                wqp_ring[ridx][:].rearrange("p (ci f) -> p ci f", f=256),
                wqbpT[:, 256 * pi:256 * pi + 256]
                .rearrange("(ci p) f -> p ci f", p=128)), inc=sm, incby=16)

        pair_w_tgt = {0: load_pair(0, 0)}

        for pi in range(16):
            ridx = pi % 2
            kb.wait("t", "DG", pair_w_tgt[pi])

            for kc in range(8):
                kb.wait("t", "V")
                kb.wait("t", "Sg")
                def nope_mm(t, ridx=ridx, kc=kc):
                    t.matmul(arena[:, 0:512], wbn_ring[ridx][:, 0:128],
                             ckv_n[:, 0 * S + 512 * kc:0 * S + 512 * kc + 512],
                             start=True, stop=False)
                    return t.matmul(arena[:, 0:512], wbn_ring[ridx][:, 128:256],
                                    ckv_n[:, 1 * S + 512 * kc:1 * S + 512 * kc + 512],
                                    start=False, stop=True)
                mtg = kb.op("t", nope_mm, inc="T")
                kb.wait("v", "T", mtg)
                def nope_ev(v, kc=kc):
                    v.tensor_copy(kf[0][0:64, 512 * kc:512 * kc + 512],
                                  arena[0:64, 0:512])
                    return v.tensor_copy(kf[1][64:128, 512 * kc:512 * kc + 512],
                                         arena[64:128, 0:512])
                kb.op("v", nope_ev, inc="V")
            def kpe_cp(v):
                v.tensor_copy(kf[0][64:128, :], kpe2[64:128, :])
                return v.tensor_copy(kf[1][0:64, :], kpe2[0:64, :])
            kb.op("v", kpe_cp, inc="V")

            for kg in range(16):
                kb.wait("t", "V")
                def v_mm(t, ridx=ridx, kg=kg):
                    last = None
                    for u in range(2):
                        kt = 2 * kg + u
                        t.matmul(arena[:, 512 + 256 * u:512 + 256 * u + 256],
                                 ckv_n[:, 0 * S + 128 * kt:0 * S + 128 * kt + 128],
                                 wbv_ring[ridx][:, 0:256], start=True, stop=False)
                        last = t.matmul(
                            arena[:, 512 + 256 * u:512 + 256 * u + 256],
                            ckv_n[:, 1 * S + 128 * kt:1 * S + 128 * kt + 128],
                            wbv_ring[ridx][:, 256:512], start=False, stop=True)
                    return last
                mtg = kb.op("t", v_mm, inc="T")
                kb.wait("v", "T", mtg)
                def v_ev(v, kg=kg):
                    src = arena[:, 512:1024].rearrange("p (u hv) -> p u hv", hv=256)
                    v.tensor_copy(vv[0][:, 256 * kg:256 * kg + 256]
                                  .rearrange("p (u f) -> p u f", f=128),
                                  src[:, :, 0:128])
                    return v.tensor_copy(vv[1][:, 256 * kg:256 * kg + 256]
                                         .rearrange("p (u f) -> p u f", f=128),
                                         src[:, :, 128:256])
                kb.op("v", v_ev, inc="V")

            for th in range(2):
                kb.wait("t", "V")
                def qn_mm(t, ridx=ridx, th=th):
                    last = None
                    for ci in range(8):
                        last = t.matmul(
                            arena[:, 0:512],
                            wqn_ring[ridx][:, 128 * ci:128 * ci + 128],
                            q1n[:, ci * NQ + 512 * th:ci * NQ + 512 * th + 512],
                            start=(ci == 0), stop=(ci == 7))
                    return last
                mtg = kb.op("t", qn_mm, inc="T")
                kb.wait("v", "T", mtg)
                def qn_ev(v, th=th):
                    v.tensor_copy(qf[0][0:64, 512 * th:512 * th + 512],
                                  arena[0:64, 0:512])
                    return v.tensor_copy(qf[1][64:128, 512 * th:512 * th + 512],
                                         arena[64:128, 0:512])
                kb.op("v", qn_ev, inc="V")

            for hh in range(2):
                for th in range(2):
                    kb.wait("t", "V")
                    def qp_mm(t, ridx=ridx, hh=hh, th=th):
                        last = None
                        for ci in range(8):
                            last = t.matmul(
                                arena[:, 0:512],
                                wqp_ring[ridx][:, 256 * ci + 128 * hh:
                                               256 * ci + 128 * hh + 128],
                                q1n[:, ci * NQ + 512 * th:ci * NQ + 512 * th + 512],
                                start=(ci == 0), stop=(ci == 7))
                        return last
                    mtg = kb.op("t", qp_mm, inc="T")
                    kb.wait("v", "T", mtg)
                    kb.op("v", lambda v, th=th: v.tensor_copy(
                        qpe_f[:, 512 * th:512 * th + 512], arena[:, 0:512]),
                        inc="V")
                kb.op("v", lambda v: v.tensor_tensor(w1q[:], qpe_f[:], csq[:],
                                                     op=ALU.mult), inc="V")
                fold_m = fold_hi if hh == 0 else fold_lo
                for th in range(2):
                    kb.wait("t", "V")
                    kb.wait("t", "Sg")
                    mtg = kb.op("t", lambda t, fold_m=fold_m, th=th: t.matmul(
                        arena[:, 512:1024], fold_m[:],
                        w1q[:, 512 * th:512 * th + 512], start=True, stop=True),
                        inc="T")
                    kb.wait("s", "T", mtg)
                    def qpe_ev(s, hh=hh, th=th):
                        if hh == 0:
                            return s.copy(qf[0][64:128, 512 * th:512 * th + 512],
                                          arena[64:128, 512:1024])
                        return s.copy(qf[1][0:64, 512 * th:512 * th + 512],
                                      arena[0:64, 512:1024])
                    kb.op("s", qpe_ev, inc="Sg")

            if pi + 1 < 16:
                kb.wait("g", "T")
                pair_w_tgt[pi + 1] = load_pair(pi + 1, (pi + 1) % 2)

            for hh in range(2):
                h = 2 * pi + hh

                def emit_pv(pj, pnkt, petg, hh=hh):
                    ttg = 0
                    last_ptev = 0
                    for tb in range(0, pnkt, 8):
                        te = min(pnkt, tb + 8)
                        if tb == 0:
                            kb.wait("t", "Sg", petg)
                        else:
                            kb.wait("t", "V", last_ptev)
                        def ptr(t, tb=tb, te=te, pj=pj):
                            last = None
                            for u in range(te - tb):
                                last = t.transpose(
                                    arena_b[:, 7168 + 128 * u:
                                            7168 + 128 * u + 128],
                                    p_sb[pj % 2][:, 128 * (tb + u):
                                                 128 * (tb + u) + 128],
                                    i128_b[:])
                            return last
                        ttg = kb.op("t", ptr, inc="T")
                        kb.wait("v", "T", ttg)
                        def ptev(v, tb=tb, te=te, pj=pj):
                            return v.tensor_copy(
                                ptbuf[pj % 2][:, 128 * tb:128 * te],
                                arena_b[:, 7168:7168 + 128 * (te - tb)])
                        last_ptev = kb.op("v", ptev, inc="V")
                    pexp_gate[pj % 2] = ttg
                    kb.wait("t", "V", last_ptev)
                    def pv(t, hh=hh, pnkt=pnkt, pj=pj):
                        last = None
                        for kt in range(pnkt):
                            last = t.matmul(
                                arena[:, 3072:3200],
                                vv[hh][:, 128 * kt:128 * kt + 128],
                                ptbuf[pj % 2][:, 128 * kt:128 * kt + 128],
                                start=(kt == 0), stop=(kt == pnkt - 1))
                        return last
                    mtg2 = kb.op("t", pv, inc="T")
                    kb.wait("v", "T", mtg2)
                    kb.op("v", lambda v, pj=pj: v.tensor_copy(
                        ohf[:, 128 * pj:128 * pj + 128], arena[:, 3072:3200]),
                        inc="V")

                pend = None
                etg_prev = 0
                for j in range(NJ):
                    span = 512 * (j + 1)
                    if j >= 6 and pend is not None:
                        emit_pv(*pend)
                        pend = None
                        kb.wait("t", "V")
                    if j == 0:
                        kb.wait("t", "V")
                        kb.wait("t", "Sg")
                    else:
                        kb.wait("t", "Sg", etg_prev)
                    def sc_mm(t, hh=hh, j=j):
                        last = None
                        for kc in range(j + 1):
                            last = t.matmul(arena[:, 512 * kc:512 * kc + 512],
                                            qf[hh][:, 128 * j:128 * j + 128],
                                            kf[hh][:, 512 * kc:512 * kc + 512],
                                            start=True, stop=True)
                        return last
                    mtg = kb.op("t", sc_mm, inc="T")
                    kb.wait("v", "T", mtg)
                    def mask_mx(v, j=j, span=span):
                        v.tensor_add(arena[:, 512 * j:512 * j + 512],
                                     arena[:, 512 * j:512 * j + 512], maskw[:])
                        return v.reduce_max(sumc[:, 8:9], arena[:, 0:span],
                                            axis=AX.X, negate=True)
                    vtg = kb.op("v", mask_mx, inc="V")
                    kb.wait("s", "V", vtg)
                    kb.wait("s", "T", pexp_gate[j % 2])
                    def ex(s, span=span, j=j):
                        half = span // 2
                        for sec, (lo, hi) in enumerate([(0, half),
                                                        (half, span)]):
                            last = s.activation(
                                p_sb[j % 2][:, lo:hi], arena[:, lo:hi], AF.Exp,
                                bias=sumc[:, 8:9],
                                accum_out=sumc[:, 10 + sec:11 + sec])
                        return last
                    etg = kb.op("s", ex, inc="Sg")
                    kb.wait("v", "Sg", etg)
                    kb.op("v", lambda v, j=j: v.reduce_sum(
                        sumc[:, j:j + 1], sumc[:, 10:12], axis=AX.X), inc="V")
                    nkt_j = span // 128
                    LAG = True
                    if LAG:
                        if pend is not None:
                            emit_pv(*pend)
                        pend = (j, nkt_j, etg)
                    else:
                        emit_pv(j, nkt_j, etg)
                    etg_prev = etg
                if pend is not None:
                    emit_pv(*pend)

                kb.op("v", lambda v: v.reciprocal(sumc[:, 8:16], sumc[:, 0:8]),
                      inc="V")
                kb.wait("t", "V")
                kb.op("t", lambda t: t.transpose(arena[0:8, 0:128],
                                                 sumc[:, 8:16], i128_f[:]),
                      inc="T")
                kb.wait("v", "T")
                kb.op("v", lambda v: v.tensor_copy(rowr[:], arena[0:8, 0:128]),
                      inc="V")
                kb.wait("g", "V")
                fl = kb.op("g", lambda g: g.dma_start(
                    rrow2[:].rearrange("o (a b) -> o a b", a=8),
                    rowr[:]), inc="DG", incby=16)
                kb.wait("t", "DG", fl)
                def recb_outer(t):
                    last = None
                    for u in range(2):
                        last = t.matmul(arena[:, 512 * u:512 * u + 512], onesf_r[:],
                                        rrow2[:, 512 * u:512 * u + 512],
                                        start=True, stop=True)
                    return last
                rtg = kb.op("t", recb_outer, inc="T")
                kb.wait("v", "T", rtg)
                kb.wait("v", "DG", oh_st_tgt[h % 2])
                vtg3 = kb.op("v", lambda v, h=h: v.tensor_mul(
                    oh_hbuf[h % 2][:], arena[:, 0:1024], ohf[:]), inc="V")
                kb.wait("g", "V", vtg3)
                oh_st_tgt[h % 2] = kb.op("g", lambda g, h=h: g.dma_start(
                    oh_scr[128 * h:128 * h + 128, :], oh_hbuf[h % 2][:]),
                    inc="DG", incby=16)
            if debug and pi == 0:
                kb.wait("g", "V")
                kb.op("g", lambda g: g.dma_start(d_kf0[:], kf[0][:]), inc="DG", incby=16)
                kb.op("g", lambda g: g.dma_start(d_kf1[:], kf[1][:]), inc="DG", incby=16)
                kb.op("g", lambda g: g.dma_start(d_vv0[:], vv[0][:]), inc="DG", incby=16)
                kb.op("g", lambda g: g.dma_start(d_qf0[:], qf[0][:]), inc="DG", incby=16)
                kb.op("g", lambda g: g.dma_start(d_qf1[:], qf[1][:]), inc="DG", incby=16)
                kb.op("g", lambda g: g.dma_start(d_p0[:], p_sb[1][:]), inc="DG", incby=16)
                kb.op("g", lambda g: g.dma_start(d_sum[:], sumc[:]), inc="DG", incby=16)
                kb.wait("g", "DG")
                kb.wait("t", "DG")
                kb.wait("v", "DG")
                kb.wait("s", "DG")
        if debug:
            kb.wait("g", "DG")
            kb.wait("g", "Gc")
            kb.wait("g", "Gd")
            kb.op("g", lambda g: g.dma_start(
                d_oh[:].rearrange("p (h q) -> p h q", q=NQ),
                oh_scr[:].rearrange("(h p) q -> p h q", p=128)),
                inc="DG", incby=16)
            kb.wait("g", "DG")
        kb.flush()

    with _ES() as loc:
        lsb = lambda n, s, d: loc.enter_context(nc.sbuf_tensor(n, s, d))
        lps = lambda n, s, d: loc.enter_context(nc.psum_tensor(n, s, d))
        kb.begin()
        kb.wait("g", "DG")
        kb.wait("g", "Gc")
        kb.wait("g", "Gd")
        wo_ring = [lsb(f"wor{i}", [128, 512], BF16) for i in range(4)]
        oh_ring = [lsb(f"ohr{i}", [128, NQ], BF16) for i in range(4)]
        xq_ring = [lsb(f"xqr{i}", [128, NQ], BF16) for i in range(4)]
        wt_ring = [lsb(f"wtr{i}", [128, 512], F32) for i in range(4)]
        tern_ring = [lsb(f"ter{i}", [128, 512], BF16) for i in range(4)]
        sgn_a = [lsb(f"sgn_a{i}", [128, 512], BF16) for i in range(4)]
        xf_sb = lsb("xf_sb", [128, 4 * NQ], F32)
        acc = lsb("acc", [128, 4 * NQ], F32)
        ybuf = lsb("ybuf", [128, NQ], F32)
        a1bc = lsb("a1bc", [128, NQ], F32)
        bqbc = lsb("bqbc", [128, NQ], F32)
        divbc = lsb("divbc", [128, NQ], F32)
        a1qrow = lsb("a1qrow", [1, NQ], F32)
        bqrow = lsb("bqrow", [1, NQ], F32)
        divrow = lsb("divrow", [1, NQ], F32)
        ctmp8b = lsb("ctmp8b", [128, 8], F32)
        ctmp8c = lsb("ctmp8c", [128, 8], F32)
        rowt8b = lsb("rowt8b", [8, 128], F32)
        y_ps = lps("y_ps", [128, 4096], F32)

        make_row_ops(lambda v: v.tensor_scalar(ctmp8b[:], qrs[:], 1.0,
                                               None, op0=ALU.add),
                     ctmp8b[:], 8, y_ps[0:8, 0:128], rowt8b, a1qrow)
        def bqr_src(v):
            v.tensor_mul(ctmp8c[:], qm[:], qrs[:])
            v.tensor_copy(ctmp8b[:], consts[:, 0:8])
            return v.tensor_scalar(ctmp8b[:], ctmp8c[:], -1.0, None,
                                   op0=ALU.mult)
        make_row_ops(bqr_src, ctmp8b[:], 8, y_ps[0:8, 0:128], rowt8b, bqrow)
        make_row_ops(lambda v: v.tensor_copy(ctmp8b[:], qDIV[:]),
                     ctmp8b[:], 8, y_ps[0:8, 0:128], rowt8b, divrow)

        for rowsrc, dst in [(a1qrow, a1bc), (bqrow, bqbc), (divrow, divbc)]:
            for th in range(2):
                kb.wait("t", "V")
                kb.wait("t", "DG")
                mtg = kb.op("t", lambda t, rowsrc=rowsrc, th=th: t.matmul(
                    y_ps[:, 0:512], onesf_r[:], rowsrc[:, 512 * th:512 * th + 512],
                    start=True, stop=True), inc="T")
                kb.wait("v", "T", mtg)
                kb.op("v", lambda v, dst=dst, th=th: v.tensor_copy(
                    dst[:, 512 * th:512 * th + 512], y_ps[:, 0:512]), inc="V")

        sgn_done = {}
        tern_done = {}
        ci_mm = {}
        ff_mm = {}
        for ocb in range(8):
            for ci in range(32):
                wbuf = wo_ring[ci % 4]
                kb.wait("g", "T", ci_mm.get(32 * ocb + ci - 4, 0))
                sm = ("Ga", "Gb", "Gj", "Gk")[ci % 4]
                dtg = kb.op("g", lambda g, ocb=ocb, ci=ci, wbuf=wbuf: g.dma_start(
                    wbuf[:],
                    woT[128 * ci:128 * ci + 128, 512 * ocb:512 * ocb + 512]),
                    inc=sm, incby=16)
                odt = kb.op("g", lambda g, ci=ci: g.dma_start(
                    oh_ring[ci % 4][:], oh_scr[128 * ci:128 * ci + 128, :]),
                    inc=sm, incby=16)
                kb.wait("t", sm, odt)
                if ci == 0:
                    kb.wait("t", "V")
                def womm(t, ci=ci, wbuf=wbuf):
                    last = None
                    for oc in range(4):
                        for th in range(2):
                            last = t.matmul(
                                y_ps[:, 1024 * oc + 512 * th:
                                     1024 * oc + 512 * th + 512],
                                wbuf[:, 128 * oc:128 * oc + 128],
                                oh_ring[ci % 4][:, 512 * th:512 * th + 512],
                                start=(ci == 0), stop=(ci == 31))
                    return last
                ci_mm[32 * ocb + ci] = kb.op("t", womm, inc="T")
            wot = ci_mm[32 * ocb + 31]
            kb.wait("y", "V")
            xtg = kb.op("y", lambda y, ocb=ocb: y.dma_start(
                xf_sb[:].rearrange("p (oc q) -> p oc q", q=NQ),
                xqT[512 * ocb:512 * ocb + 512, :]
                .rearrange("(oc p) q -> p oc q", p=128)), inc="DS", incby=16)
            kb.wait("v", "T", wot)
            kb.wait("v", "DS", xtg)
            def accb(v):
                last = None
                for oc in range(4):
                    v.tensor_add(acc[:, NQ * oc:NQ * oc + NQ],
                                 y_ps[:, 1024 * oc:1024 * oc + NQ], bqbc[:])
                    v.tensor_mul(ybuf[:], xf_sb[:, NQ * oc:NQ * oc + NQ], a1bc[:])
                    last = v.tensor_add(acc[:, NQ * oc:NQ * oc + NQ],
                                        acc[:, NQ * oc:NQ * oc + NQ], ybuf[:])
                return last
            kb.op("v", accb, inc="V")
            if debug and ocb == 0:
                kb.wait("y", "V")
                kb.op("y", lambda y: y.dma_start(d_acc1[:], acc[:, 0:NQ]),
                      inc="DS", incby=16)
                kb.op("y", lambda y: y.dma_start(d_rows3[:, 0:NQ], a1qrow[:]),
                      inc="DS", incby=16)
                kb.op("y", lambda y: y.dma_start(d_rows3[:, NQ:2 * NQ],
                                                 bqrow[:]), inc="DS", incby=16)
                kb.op("y", lambda y: y.dma_start(d_rows3[:, 2 * NQ:3 * NQ],
                                                 divrow[:]), inc="DS", incby=16)
                kb.wait("v", "DS")

            for ci in range(32):
                wbuf = wt_ring[ci % 4]
                tbuf = tern_ring[ci % 4]
                kb.wait("y", "Sg", sgn_done.get(32 * ocb + ci - 4, 0))
                sm = ("Ya", "Yb", "Yd", "Ye")[ci % 4]
                dtg = kb.op("y", lambda y, ocb=ocb, ci=ci, wbuf=wbuf: y.dma_start(
                    wbuf[:],
                    WT[128 * ci:128 * ci + 128, 512 * ocb:512 * ocb + 512]),
                    inc=sm, incby=16)
                kb.wait("g", "T", ff_mm.get(32 * ocb + ci - 4, 0))
                xdt = kb.op("g", lambda g, ci=ci: g.dma_start(
                    xq_ring[ci % 4][:], xqf_scr[128 * ci:128 * ci + 128, :]),
                    inc=("Gh", "Gi", "Gl", "Gm")[ci % 4], incby=16)
                kb.wait("s", sm, dtg)
                kb.wait("s", "T", ff_mm.get(32 * ocb + ci - 4, 0))
                kb.wait("s", "V", tern_done.get(32 * ocb + ci - 4, 0))
                def sgn(s, wbuf=wbuf, tbuf=tbuf, ci=ci):
                    s.activation(sgn_a[ci % 4][:], wbuf[:], AF.Sign,
                                 bias=consts[:, C_BA2:C_BA2 + 1],
                                 scale=swcol[:, 0:1])
                    return s.activation(tbuf[:], wbuf[:], AF.Sign,
                                        bias=consts[:, C_BB2:C_BB2 + 1],
                                        scale=swcol[:, 0:1])
                stg = kb.op("s", sgn, inc="Sg")
                sgn_done[32 * ocb + ci] = stg
                kb.wait("v", "Sg", stg)
                vtg = kb.op("v", lambda v, tbuf=tbuf, ci=ci: v.tensor_add(
                    tbuf[:], tbuf[:], sgn_a[ci % 4][:]), inc="V")
                tern_done[32 * ocb + ci] = vtg
                kb.wait("t", "V", vtg)
                kb.wait("t", ("Gh", "Gi", "Gl", "Gm")[ci % 4], xdt)
                if ci == 0:
                    kb.wait("t", "V")
                def ffmm(t, ci=ci, tbuf=tbuf):
                    last = None
                    for oc in range(4):
                        for th in range(2):
                            last = t.matmul(
                                y_ps[:, 1024 * oc + 512 * th:
                                     1024 * oc + 512 * th + 512],
                                tbuf[:, 128 * oc:128 * oc + 128],
                                xq_ring[ci % 4][:, 512 * th:512 * th + 512],
                                start=(ci == 0), stop=(ci == 31))
                    return last
                ff_mm[32 * ocb + ci] = kb.op("t", ffmm, inc="T")
            fft = ff_mm[32 * ocb + 31]
            kb.wait("v", "T", fft)
            for oc in range(4):
                vtg = kb.op("v", lambda v, oc=oc: v.tensor_mul(
                    ybuf[:], y_ps[:, 1024 * oc:1024 * oc + NQ], divbc[:]), inc="V")
                kb.wait("s", "V", vtg)
                stg = kb.op("s", lambda s, ocb=ocb, oc=oc: s.activation(
                    ybuf[:], ybuf[:], AF.Gelu,
                    bias=bq_sb[:, 4 * ocb + oc:4 * ocb + oc + 1], scale=1.0),
                    inc="Sg")
                kb.wait("v", "Sg", stg)
                if debug and ocb == 0 and oc == 0:
                    kb.wait("y", "Sg", stg)
                    kb.op("y", lambda y: y.dma_start(d_ffny[:], ybuf[:]),
                          inc="DS", incby=16)
                    kb.wait("v", "DS")
                vtg2 = kb.op("v", lambda v, oc=oc: v.tensor_add(
                    acc[:, NQ * oc:NQ * oc + NQ], acc[:, NQ * oc:NQ * oc + NQ],
                    ybuf[:]), inc="V")
                kb.wait("y", "V", vtg2)
                kb.op("y", lambda y, ocb=ocb, oc=oc: y.dma_start(
                    out[128 * (4 * ocb + oc):128 * (4 * ocb + oc) + 128, :],
                    acc[:, NQ * oc:NQ * oc + NQ]), inc="DS", incby=16)
        kb.wait("y", "DS")
        kb.flush()

    stk.close()
    return nc


_PROG = {}


def _get_prog():
    if "nc" not in _PROG:
        _PROG["nc"] = build_program()
    return _PROG["nc"]


def kernel(L0_attn_x, L0_attn_wq_a, L0_attn_q_a_ln_w, L0_attn_wq_b, L0_attn_wkv_a,
           L0_attn_kv_a_ln_w, L0_attn_wkv_b, L0_attn_wo, L0_ffn_W, L0_ffn_b):
    x = np.asarray(L0_attn_x, dtype=np.float32)
    wq_a = np.asarray(L0_attn_wq_a, dtype=np.float32)
    q_a_ln = np.asarray(L0_attn_q_a_ln_w, dtype=np.float32)
    wq_b = np.asarray(L0_attn_wq_b, dtype=np.float32)
    wkv_a = np.asarray(L0_attn_wkv_a, dtype=np.float32)
    kv_a_ln = np.asarray(L0_attn_kv_a_ln_w, dtype=np.float32)
    wkv_b = np.asarray(L0_attn_wkv_b, dtype=np.float32)
    wo = np.asarray(L0_attn_wo, dtype=np.float32)
    W = np.asarray(L0_ffn_W, dtype=np.float32)
    bias = np.asarray(L0_ffn_b, dtype=np.float32)

    C = np.ascontiguousarray
    deint = np.concatenate([np.arange(0, 64, 2), np.arange(1, 64, 2)])
    rotp = np.concatenate([np.arange(32, 64), np.arange(0, 32)])

    idx_n = np.concatenate([h * 128 + np.arange(64) for h in range(H)])
    wqbnT = C(wq_b[idx_n].T)
    idx_p = []
    for h in range(H):
        dr = h * 128 + 64 + deint
        idx_p.append(dr)
        idx_p.append(dr[rotp])
    idx_p = np.concatenate(idx_p)
    wqbpT = C(wq_b[idx_p].T)

    kpe_rows = 256 + deint
    wkvaT = C(np.concatenate([wkv_a[0:256], wkv_a[kpe_rows],
                              wkv_a[kpe_rows][rotp]], axis=0).T)
    idx_kn = np.concatenate([h * 192 + np.arange(64) for h in range(H)])
    wkvbnT = C(wkv_b[idx_kn].T)
    idx_kv = np.concatenate([h * 192 + 64 + np.arange(128) for h in range(H)])
    wkvbvT = C(wkv_b[idx_kv].T)

    wqaT = C(wq_a.T)
    woT = C(wo.T)
    WT = C(W.T)

    qlnv = C(q_a_ln.reshape(8, 128).T)
    kvlnv = C(kv_a_ln.reshape(2, 128).T)
    bqvv = C(bias.reshape(32, 128).T)

    cos, sin = _yarn_cos_sin(S)
    sgnv = np.concatenate([-np.ones(32, np.float32), np.ones(32, np.float32)])
    ssin = sin * sgnv[None, :]
    CSKv = C(np.concatenate([cos.T, ssin.T], axis=0))

    FOLD_Knp = np.zeros((128, 64), np.float32)
    for jj in range(64):
        FOLD_Knp[jj, jj] = 1.0
        FOLD_Knp[jj + 64, jj] = 1.0
    FOLD_LOnp = np.zeros((128, 128), np.float32)
    FOLD_LOnp[:, 0:64] = FOLD_Knp
    FOLD_HInp = np.zeros((128, 128), np.float32)
    FOLD_HInp[:, 64:128] = FOLD_Knp

    constsv = np.zeros((128, 8), np.float32)
    constsv[:, C_EPSLN] = EPS_LN
    constsv[:, C_EPSAT] = EPS_ATTN
    constsv[:, C_NEGH] = -0.5
    constsv[:, C_I1024] = 1.0 / 1024.0
    constsv[:, C_I256] = 1.0 / 256.0
    constsv[:, C_I4096] = 1.0 / 4096.0
    constsv[:, C_BA2] = -0.5
    constsv[:, C_BB2] = 0.5

    bf = ml_dtypes.bfloat16
    shared = dict(
        wqaT=wqaT, wqbnT=wqbnT, wqbpT=wqbpT, wkvaT=wkvaT, wkvbnT=wkvbnT,
        wkvbvT=wkvbvT, woT=woT, WT=WT, qln=qlnv, kvln=kvlnv, bqv=bqvv, CSK=CSKv,
        FOLD_K=FOLD_Knp.astype(bf), FOLD_LO=FOLD_LOnp.astype(bf),
        FOLD_HI=FOLD_HInp.astype(bf),
        I128B=np.eye(128, dtype=np.float32).astype(bf),
        I128F=np.eye(128, dtype=np.float32),
        ONESB=np.ones((128, 1), np.float32).astype(bf),
        ONESFR=np.ones((1, 128), np.float32),
        ONESFC=np.ones((128, 1), np.float32),
        CONSTS=constsv,
    )

    in_maps = []
    xT_b = [C(x[b_].T) for b_ in range(B)]
    for core in range(8):
        b_, c_ = core // 4, core % 4
        qp = _qpos(c_)
        mw = np.where(np.arange(512)[None, :] <= 128 * c_ + np.arange(128)[:, None],
                      0.0, NEG).astype(np.float32)
        m = dict(shared)
        m.update(xT=xT_b[b_], x_tm=x[b_], xq_tm=C(x[b_][qp]),
                 xqT=C(xT_b[b_][:, qp]), CSQ=C(CSKv[:, qp]), MASKW=mw)
        in_maps.append(m)

    nc = _get_prog()
    res = run_bass_kernel_spmd(nc, in_maps, core_ids=list(range(8)))

    outf = np.zeros((B, S, D), dtype=np.float32)
    for core in range(8):
        b_, c_ = core // 4, core % 4
        qp = _qpos(c_)
        outf[b_, qp, :] = np.asarray(res.results[core]["out"]).T
    return outf


# version 12
# speedup vs baseline: 1.4899x; 1.0030x over previous
import math
import numpy as np
import ml_dtypes
import concourse.bass as bass
import concourse.mybir as mybir
from concourse.bass_utils import run_bass_kernel_spmd

F32 = mybir.dt.float32
BF16 = mybir.dt.bfloat16
AF = mybir.ActivationFunctionType
ALU = mybir.AluOpType
AX = mybir.AxisListType

B, S, D = 2, 4096, 4096
H, NOPE, ROPE, VD, KVR = 32, 64, 64, 128, 256
NQ = 1024
NJ = 8
TG = 512
NTG = S // TG
EPS_LN = 1e-5
EPS_ATTN = 1e-6
NEG = -1.0e30
_mm_ = 0.1 * 1.0 * math.log(128.0) + 1.0
SOFTMAX_SCALE = (128.0 ** -0.5) * _mm_ * _mm_

C_EPSLN, C_EPSAT, C_NEGH, C_I1024, C_I256, C_I4096, C_BA2, C_BB2 = range(8)


def _yarn_cos_sin(seq_len, dim=ROPE, base=10000.0, factor=128.0, beta_fast=32.0,
                  beta_slow=1.0, orig_max=8192, mscale=1.0, mscale_all_dim=1.0):
    def corr_dim(r):
        return dim * math.log(orig_max / (r * 2 * math.pi)) / (2 * math.log(base))
    low = max(math.floor(corr_dim(beta_fast)), 0)
    high = min(math.ceil(corr_dim(beta_slow)), dim - 1)
    hi = high + 0.001 if low == high else float(high)
    ramp = np.clip((np.arange(dim // 2, dtype=np.float32) - low) / (hi - low), 0.0, 1.0)
    inv_freq_mask = 1.0 - ramp
    freq_extra = 1.0 / base ** (np.arange(0, dim, 2, dtype=np.float32) / dim)
    freq_inter = freq_extra / factor
    inv_freq = freq_inter * (1.0 - inv_freq_mask) + freq_extra * inv_freq_mask
    t = np.arange(seq_len, dtype=np.float32)
    freqs = np.outer(t, inv_freq)
    emb = np.concatenate([freqs, freqs], axis=-1)
    def gm(s, m):
        return 1.0 if s <= 1 else 0.1 * m * math.log(s) + 1.0
    ms = gm(factor, mscale) / gm(factor, mscale_all_dim)
    return (np.cos(emb) * ms).astype(np.float32), (np.sin(emb) * ms).astype(np.float32)


def _qpos(c):
    return np.concatenate([512 * j + 128 * c + np.arange(128) for j in range(NJ)])


class KB:
    def __init__(self, nc):
        self.nc = nc
        self.sems = {}
        self.cnt = {}
        self.q = None

    def mksem(self, stk, *names):
        for n in names:
            self.sems[n] = stk.enter_context(self.nc.semaphore(n))
            self.cnt[n] = 0

    def begin(self):
        self.q = {e: [] for e in "tvsgy"}

    def wait(self, eng, sname, val=None):
        v = self.cnt[sname] if val is None else val
        if v > 0:
            h = self.sems[sname]
            self.q[eng].append(lambda e, h=h, v=v: e.wait_ge(h, v))
        return v

    def op(self, eng, fn, inc=None, incby=1):
        if inc is not None:
            h = self.sems[inc]
            self.cnt[inc] += incby
            self.q[eng].append(lambda e, fn=fn, h=h, k=incby: fn(e).then_inc(h, k))
            return self.cnt[inc]
        self.q[eng].append(fn)
        return None

    def flush(self):
        qq = self.q
        with self.nc.Block() as blk:
            if qq["y"]:
                @blk.sync
                def _(e, _q=qq["y"]):
                    for f in _q:
                        f(e)
            if qq["g"]:
                @blk.gpsimd
                def _(e, _q=qq["g"]):
                    for f in _q:
                        f(e)
            if qq["t"]:
                @blk.tensor
                def _(e, _q=qq["t"]):
                    for f in _q:
                        f(e)
            if qq["v"]:
                @blk.vector
                def _(e, _q=qq["v"]):
                    for f in _q:
                        f(e)
            if qq["s"]:
                @blk.scalar
                def _(e, _q=qq["s"]):
                    for f in _q:
                        f(e)
        self.q = None


def build_program(debug=False):
    from contextlib import ExitStack
    nc = bass.Bass("TRN2", target_bir_lowering=False)

    din = lambda n, s, d=F32: nc.dram_tensor(n, s, d, kind="ExternalInput")
    xT = din("xT", [D, S])
    x_tm = din("x_tm", [S, D])
    xq_tm = din("xq_tm", [NQ, D])
    xqT = din("xqT", [D, NQ])
    wqaT = din("wqaT", [D, 1024])
    wqbnT = din("wqbnT", [1024, 2048])
    wqbpT = din("wqbpT", [1024, 4096])
    wkvaT = din("wkvaT", [D, 384])
    wkvbnT = din("wkvbnT", [KVR, 2048])
    wkvbvT = din("wkvbvT", [KVR, 4096])
    woT = din("woT", [D, D])
    WT = din("WT", [D, D])
    qln = din("qln", [128, 8])
    kvln = din("kvln", [128, 2])
    bqv = din("bqv", [128, 32])
    CSK = din("CSK", [128, S])
    CSQ = din("CSQ", [128, NQ])
    MASKW = din("MASKW", [128, 512])
    FOLD_K = din("FOLD_K", [128, 64], BF16)
    FOLD_LO = din("FOLD_LO", [128, 128], BF16)
    FOLD_HI = din("FOLD_HI", [128, 128], BF16)
    I128B = din("I128B", [128, 128], BF16)
    I128F = din("I128F", [128, 128])
    ONESB = din("ONESB", [128, 1], BF16)
    ONESFR = din("ONESFR", [1, 128])
    ONESFC = din("ONESFC", [128, 1])
    CONSTS = din("CONSTS", [128, 8])

    xqf_scr = nc.dram_tensor("xqf_scr", [D, NQ], BF16,
                             kind="ExternalOutput" if debug else "Internal")
    out = nc.dram_tensor("out", [D, NQ], F32, kind="ExternalOutput")
    if debug:
        d_statc = nc.dram_tensor("d_statc", [128, 96], F32, kind="ExternalOutput")
        d_qstat = nc.dram_tensor("d_qstat", [128, 96], F32, kind="ExternalOutput")
        d_sw = nc.dram_tensor("d_sw", [128, 2], F32, kind="ExternalOutput")
        d_ckvn = nc.dram_tensor("d_ckvn", [128, 2 * S], BF16, kind="ExternalOutput")
        d_kpe2 = nc.dram_tensor("d_kpe2", [128, S], BF16, kind="ExternalOutput")
        d_q1n = nc.dram_tensor("d_q1n", [128, 8 * NQ], BF16, kind="ExternalOutput")
        d_oh = nc.dram_tensor("d_oh", [128, H * NQ], BF16, kind="ExternalOutput")
        d_kf0 = nc.dram_tensor("d_kf0", [128, S], BF16, kind="ExternalOutput")
        d_kf1 = nc.dram_tensor("d_kf1", [128, S], BF16, kind="ExternalOutput")
        d_vv0 = nc.dram_tensor("d_vv0", [128, S], BF16, kind="ExternalOutput")
        d_qf0 = nc.dram_tensor("d_qf0", [128, NQ], BF16, kind="ExternalOutput")
        d_qf1 = nc.dram_tensor("d_qf1", [128, NQ], BF16, kind="ExternalOutput")
        d_p0 = nc.dram_tensor("d_p0", [128, S], BF16, kind="ExternalOutput")
        d_sum = nc.dram_tensor("d_sum", [128, 16], F32, kind="ExternalOutput")
        d_acc1 = nc.dram_tensor("d_acc1", [128, NQ], F32, kind="ExternalOutput")
        d_ffny = nc.dram_tensor("d_ffny", [128, NQ], F32, kind="ExternalOutput")
        d_rows3 = nc.dram_tensor("d_rows3", [1, 3 * NQ], F32, kind="ExternalOutput")
        d_p1 = nc.dram_tensor("d_p1", [128, 1024], BF16, kind="ExternalOutput")

    from contextlib import ExitStack as _ES
    stk = _ES()
    sb = lambda n, s, d: stk.enter_context(nc.sbuf_tensor(n, s, d))

    kb = KB(nc)
    kb.mksem(stk, "DS", "DG", "T", "V", "Sg", "Ya", "Yb", "Yc", "Ga", "Gb",
             "Gc", "Gd", "Ge", "Gf", "Gg", "Gh", "Gi", "Gj", "Gk", "Gl", "Gm",
             "Yd", "Ye")

    consts = sb("consts", [128, 8], F32)
    ones_b = sb("ones_b", [128, 1], BF16)
    onesf_r = sb("onesf_r", [1, 128], F32)
    onesf_c = sb("onesf_c", [128, 1], F32)
    i128_b = sb("i128_b", [128, 128], BF16)
    i128_f = sb("i128_f", [128, 128], F32)
    fold_k = sb("fold_k", [128, 64], BF16)
    fold_lo = sb("fold_lo", [128, 128], BF16)
    fold_hi = sb("fold_hi", [128, 128], BF16)
    maskw = sb("maskw", [128, 512], F32)
    csq = sb("csq", [128, NQ], F32)
    qln_sb = sb("qln_sb", [128, 8], F32)
    kvln_sb = sb("kvln_sb", [128, 2], F32)
    bq_sb = sb("bq_sb", [128, 32], F32)

    statc = sb("statc", [128, 96], F32)
    qsum = sb("qsum", [128, 8], F32)
    qss = sb("qss", [128, 8], F32)
    qmx = sb("qmx", [128, 8], F32)
    qmn = sb("qmn", [128, 8], F32)
    qvar = sb("qvar", [128, 8], F32)
    wc = sb("wc", [128, 8], F32)
    wc2 = sb("wc2", [128, 8], F32)
    wc3 = sb("wc3", [128, 8], F32)
    wc4 = sb("wc4", [128, 8], F32)
    sumc = sb("sumc", [128, 16], F32)
    qm = sb("qm", [128, 8], F32)
    qrs = sb("qrs", [128, 8], F32)
    qrs2 = sb("qrs2", [128, 8], F32)
    qR = sb("qR", [128, 8], F32)
    qDIV = sb("qDIV", [128, 8], F32)
    nmR = sb("nmR", [128, 8], F32)
    nmRp = sb("nmRp", [128, 8], F32)
    nmRh = sb("nmRh", [128, 8], F32)
    swcol = sb("swcol", [128, 8], F32)
    winv = sb("winv", [128, 8], F32)

    ckv_n = sb("ckv_n", [128, 2 * S], BF16)
    kpe2 = sb("kpe2", [128, S], BF16)
    q1n = sb("q1n", [128, 8 * NQ], BF16)

    def make_row_ops(colsrc_fn, ctmp_ap, ncols, tp_ps, rowtmp, dstrow):
        kb.wait("v", "Sg")
        kb.wait("v", "DG")
        kb.op("v", colsrc_fn, inc="V")
        kb.wait("t", "V")
        kb.op("t", lambda t: t.transpose(tp_ps[0:ncols, 0:128], ctmp_ap, i128_f[:]),
              inc="T")
        kb.wait("v", "T")
        kb.op("v", lambda v: v.tensor_copy(rowtmp[0:ncols, :], tp_ps[0:ncols, 0:128]),
              inc="V")
        kb.wait("g", "V")
        kb.wait("g", "T")
        kb.op("g", lambda g: g.dma_start(
            dstrow[:].rearrange("o (a b) -> o a b", a=ncols),
            rowtmp[0:ncols, :]), inc="DG", incby=16)

    kb.begin()
    for dst_, src_ in [(consts, CONSTS), (maskw, MASKW), (csq, CSQ),
                       (qln_sb, qln), (kvln_sb, kvln), (bq_sb, bqv),
                       (onesf_r, ONESFR), (onesf_c, ONESFC), (i128_f, I128F),
                       (ones_b, ONESB), (i128_b, I128B), (fold_k, FOLD_K),
                       (fold_lo, FOLD_LO), (fold_hi, FOLD_HI)]:
        kb.op("y", lambda y, d=dst_, s=src_: y.dma_start(d[:], s[:]),
              inc="DS", incby=16)
    kb.wait("y", "DS")
    kb.flush()

    with _ES() as loc:
        lsb = lambda n, s, d: loc.enter_context(nc.sbuf_tensor(n, s, d))
        lps = lambda n, s, d: loc.enter_context(nc.psum_tensor(n, s, d))
        kb.begin()
        xtm_ring = [lsb(f"xtm{i}", [128, D], F32) for i in range(2)]
        sq_scr = lsb("sq_scr", [128, D], BF16)
        sxc = lsb("sxc", [128, 32], F32)
        sxxc = lsb("sxxc", [128, 32], F32)
        wt_ring = [lsb(f"wtp{i}", [128, D], F32) for i in range(2)]
        wabs = lsb("wabs", [128, 32], F32)
        ones8f = lsb("ones8f", [128, 8], F32)
        ones128 = lsb("ones128", [128, 128], F32)
        bc8_ps = lps("bc8_ps", [128, 8], F32)

        kb.op("v", lambda v: v.memset(ones128[:], 1.0), inc="V")
        v_done = {}
        s_done = {}
        for ti in range(32):
            buf = xtm_ring[ti % 2]
            kb.wait("y", "V", v_done.get(ti - 2, 0))
            kb.wait("y", "Sg", s_done.get(ti - 2, 0))
            sm = ("Ya", "Yb")[ti % 2]
            dtgt = kb.op("y", lambda y, buf=buf, ti=ti:
                         y.dma_start(buf[:], x_tm[128 * ti:128 * ti + 128, :]),
                         inc=sm, incby=16)
            kb.wait("v", sm, dtgt)
            v_done[ti] = kb.op("v", lambda v, buf=buf, ti=ti:
                               v.reduce_sum(sxc[:, ti:ti + 1], buf[:], axis=AX.X),
                               inc="V")
            kb.wait("s", sm, dtgt)
            s_done[ti] = kb.op("s", lambda s, buf=buf, ti=ti:
                               s.activation(sq_scr[:], buf[:], AF.Square,
                                            accum_out=sxxc[:, ti:ti + 1]),
                               inc="Sg")

        kb.wait("v", "V")
        kb.wait("v", "Sg")
        def derive_stats(v):
            v.tensor_scalar(statc[:, 0:64:2], sxc[:, 0:32], 1.0 / D, None,
                            op0=ALU.mult)
            v.tensor_scalar(statc[:, 1:64:2], sxxc[:, 0:32], 1.0 / D, None,
                            op0=ALU.mult)
            v.tensor_mul(statc[:, 64:96], statc[:, 0:64:2], statc[:, 0:64:2])
            return v.tensor_sub(statc[:, 1:64:2], statc[:, 1:64:2], statc[:, 64:96])
        kb.op("v", derive_stats, inc="V")
        kb.wait("s", "V")
        def rs_from_var(s):
            s.activation(statc[:, 1:64:2], statc[:, 1:64:2], AF.Ln,
                         bias=consts[:, C_EPSLN:C_EPSLN + 1], scale=1.0)
            return s.activation(statc[:, 1:64:2], statc[:, 1:64:2], AF.Exp,
                                bias=0.0, scale=consts[:, C_NEGH:C_NEGH + 1])
        kb.op("s", rs_from_var, inc="Sg")

        wv_done = {}
        for wi in range(32):
            buf = wt_ring[wi % 2]
            kb.wait("y", "V", wv_done.get(wi - 2, 0))
            sm = ("Ya", "Yb")[wi % 2]
            dtgt = kb.op("y", lambda y, buf=buf, wi=wi:
                         y.dma_start(buf[:], WT[128 * wi:128 * wi + 128, :]),
                         inc=sm, incby=16)
            kb.wait("v", sm, dtgt)
            wv_done[wi] = kb.op("v", lambda v, buf=buf, wi=wi:
                                v.reduce_sum(wabs[:, wi:wi + 1], buf[:], axis=AX.X,
                                             apply_absolute_value=True), inc="V")
        kb.wait("v", "V")
        def wsum_fin(v):
            v.reduce_sum(wabs[:, 0:1], wabs[:, 0:32], axis=AX.X)
            v.tensor_copy(wc2[:], consts[:])
            return v.tensor_scalar(ones8f[:], ones128[:, 0:8], wabs[:, 0:1],
                                   None, op0=ALU.mult)
        kb.op("v", wsum_fin, inc="V")
        kb.wait("t", "V")
        kb.op("t", lambda t: t.matmul(bc8_ps[:], ones128[:], ones8f[:],
                                      start=True, stop=True), inc="T")
        kb.wait("v", "T")
        def sw_fin(v):
            v.tensor_scalar(wc3[:], bc8_ps[:], 1.0 / (4096.0 * 4096.0), None,
                            op0=ALU.mult)
            v.tensor_copy(wc2[:], consts[:])
            v.tensor_scalar(winv[:], wc3[:], 1e-5, None, op0=ALU.max)
            v.tensor_copy(wc2[:], consts[:])
            return v.reciprocal(swcol[:], winv[:])
        kb.op("v", sw_fin, inc="V")

        kb.flush()

    with _ES() as loc:
        lsb = lambda n, s, d: loc.enter_context(nc.sbuf_tensor(n, s, d))
        lps = lambda n, s, d: loc.enter_context(nc.psum_tensor(n, s, d))
        kb.begin()
        kb.wait("y", "V")
        kb.wait("y", "Sg")
        kb.wait("y", "T")
        tp_b = lps("tp_b", [128, 1024], BF16)
        stage = lsb("stage", [128, 1024], BF16)
        qxr = [lsb(f"qxr{i}", [128, D], F32) for i in range(2)]
        sqf = lsb("sqf", [128, D], F32)
        ngf = lsb("ngf", [128, D], F32)
        wbuf = lsb("wbuf", [128, D], F32)
        bmask = lsb("bmask", [128, D], F32)
        xq_t = lsb("xq_t", [128, D], BF16)

        qv_done = {}
        for jt in range(NJ):
            buf = qxr[jt % 2]
            kb.wait("y", "V", qv_done.get(jt - 2, 0))
            sm = ("Ya", "Yb")[jt % 2]
            dtgt = kb.op("y", lambda y, buf=buf, jt=jt:
                         y.dma_start(buf[:], xq_tm[128 * jt:128 * jt + 128, :]),
                         inc=sm, incby=16)
            kb.wait("v", sm, dtgt)
            def qredu(v, buf=buf, jt=jt):
                v.tensor_mul(sqf[:], buf[:], buf[:])
                v.tensor_scalar(ngf[:], buf[:], -1.0, None, op0=ALU.mult)
                v.reduce_sum(qss[:, jt:jt + 1], sqf[:], axis=AX.X)
                v.reduce_max(qmn[:, jt:jt + 1], ngf[:], axis=AX.X)
                v.reduce_sum(qsum[:, jt:jt + 1], buf[:], axis=AX.X)
                return v.reduce_max(qmx[:, jt:jt + 1], buf[:], axis=AX.X)
            qv_done[jt] = kb.op("v", qredu, inc="V")

        kb.wait("v", "V")
        def qderive(v):
            v.tensor_scalar(qm[:], qsum[:], 1.0 / D, None, op0=ALU.mult)
            return v.tensor_scalar(wc[:], qss[:], 1.0 / D, None, op0=ALU.mult)
        kb.op("v", qderive, inc="V")
        kb.wait("s", "V")
        kb.op("s", lambda s: s.activation(wc2[:], qm[:], AF.Square), inc="Sg")
        kb.wait("v", "Sg")
        kb.op("v", lambda v: v.tensor_sub(qvar[:], wc[:], wc2[:]), inc="V")
        kb.wait("s", "V")
        def qrs_op(s):
            s.activation(qrs[:], qvar[:], AF.Ln,
                         bias=consts[:, C_EPSLN:C_EPSLN + 1], scale=1.0)
            return s.activation(qrs[:], qrs[:], AF.Exp, bias=0.0,
                                scale=consts[:, C_NEGH:C_NEGH + 1])
        kb.op("s", qrs_op, inc="Sg")
        kb.wait("v", "Sg")
        def qmean2(v):
            v.tensor_mul(wc2[:], qvar[:], qrs[:])
            v.tensor_copy(wc3[:], consts[:, 0:8])
            return v.tensor_mul(wc[:], wc2[:], qrs[:])
        kb.op("v", qmean2, inc="V")
        kb.wait("s", "V")
        def qrs2_op(s):
            s.activation(qrs2[:], wc[:], AF.Ln,
                         bias=consts[:, C_EPSLN:C_EPSLN + 1], scale=1.0)
            return s.activation(qrs2[:], qrs2[:], AF.Exp, bias=0.0,
                                scale=consts[:, C_NEGH:C_NEGH + 1])
        kb.op("s", qrs2_op, inc="Sg")
        kb.wait("v", "Sg")
        def qfin(v):
            v.tensor_sub(wc[:], qmx[:], qm[:])
            v.tensor_add(wc2[:], qm[:], qmn[:])
            v.tensor_mul(wc3[:], qrs[:], qrs2[:])
            v.tensor_max(wc4[:], wc[:], wc2[:])
            v.tensor_copy(wc[:], consts[:, 0:8])
            v.tensor_mul(wc2[:], wc4[:], wc3[:])
            v.tensor_copy(wc4[:], consts[:, 0:8])
            v.tensor_scalar(wc[:], wc2[:], 1e-5, None, op0=ALU.max)
            v.tensor_copy(wc4[:], consts[:, 0:8])
            v.tensor_mul(wc2[:], wc[:], winv[:])
            v.reciprocal(wc4[:], wc[:])
            v.tensor_scalar(qDIV[:], wc2[:], 0.5 / 127.0, None, op0=ALU.mult)
            v.tensor_mul(wc2[:], wc4[:], wc3[:])
            v.tensor_copy(wc4[:], consts[:, 0:8])
            v.tensor_scalar(qR[:], wc2[:], 127.0, None, op0=ALU.mult)
            v.tensor_copy(wc4[:], consts[:, 0:8])
            v.tensor_mul(wc2[:], qm[:], qR[:])
            v.tensor_copy(wc4[:], consts[:, 0:8])
            v.tensor_scalar(nmR[:], wc2[:], -1.0, None, op0=ALU.mult)
            v.tensor_copy(wc4[:], consts[:, 0:8])
            v.tensor_scalar(nmRp[:], nmR[:], 128.5, None, op0=ALU.add)
            return v.tensor_scalar(nmRh[:], nmR[:], 0.5, None, op0=ALU.add)
        kb.op("v", qfin, inc="V")

        kb.wait("y", "V")
        kb.wait("y", "Sg")
        wq_done = {}
        for jt in range(NJ):
            buf = qxr[jt % 2]
            kb.wait("y", "Sg", wq_done.get(jt - 2, 0))
            sm = ("Ya", "Yb")[jt % 2]
            dtgt = kb.op("y", lambda y, buf=buf, jt=jt:
                         y.dma_start(buf[:], xq_tm[128 * jt:128 * jt + 128, :]),
                         inc=sm, incby=16)
            kb.wait("s", sm, dtgt)
            kb.wait("s", "V")
            wq_done[jt] = kb.op("s", lambda s, buf=buf, jt=jt: s.activation(
                wbuf[:], buf[:], AF.Copy, bias=0.0,
                scale=qR[:, jt:jt + 1]), inc="Sg")
            kb.wait("v", "Sg")
            def qchain(v, jt=jt):
                v.tensor_scalar(sqf[:], wbuf[:], nmRp[:, jt:jt + 1], None,
                                op0=ALU.add)
                cur, nxt = sqf, ngf
                for k in (128.0, 64.0, 32.0, 16.0, 8.0, 4.0, 2.0, 1.0):
                    v.tensor_scalar(bmask[:], cur[:], k, None, op0=ALU.is_ge)
                    v.scalar_tensor_tensor(nxt[:], bmask[:], -k, cur[:],
                                           op0=ALU.mult, op1=ALU.add)
                    cur, nxt = nxt, cur
                v.tensor_sub(bmask[:], wbuf[:], cur[:])
                return v.tensor_scalar(xq_t[:], bmask[:], nmRh[:, jt:jt + 1],
                                       None, op0=ALU.add)
            kb.op("v", qchain, inc="V")
            for cg in range(4):
                kb.wait("t", "V")
                kb.wait("t", "DG")
                def qtr(t, cg=cg):
                    last = None
                    for u in range(8):
                        last = t.transpose(tp_b[:, 128 * u:128 * u + 128],
                                           xq_t[:, 128 * (8 * cg + u):
                                                 128 * (8 * cg + u) + 128],
                                           i128_b[:])
                    return last
                ttg = kb.op("t", qtr, inc="T")
                kb.wait("v", "T", ttg)
                kb.wait("v", "DG")
                kb.op("v", lambda v: v.tensor_copy(stage[:], tp_b[:]), inc="V")
                kb.wait("g", "V")
                def qst(g, jt=jt, cg=cg):
                    r0 = 1024 * cg
                    return g.dma_start(
                        xqf_scr[r0:r0 + 1024, 128 * jt:128 * jt + 128]
                        .rearrange("(u p) q -> p u q", p=128),
                        stage[:].rearrange("p (u q) -> p u q", q=128))
                kb.op("g", qst, inc="DG", incby=16)
        if debug:
            kb.wait("y", "V")
            kb.wait("y", "Sg")
            kb.op("y", lambda y: y.dma_start(d_statc[:], statc[:]), inc="DS", incby=16)
            for ci_, src_ in enumerate([qm, qrs, qrs2, qR, qDIV, qvar,
                                        qmx, qmn, qsum, qss, wc, wc3]):
                kb.op("y", lambda y, ci_=ci_, src_=src_: y.dma_start(
                    d_qstat[:, 8 * ci_:8 * ci_ + 8], src_[:]), inc="DS", incby=16)
            kb.op("y", lambda y: y.dma_start(d_sw[:, 0:1], swcol[:, 0:1], allow_slow_non_contiguous=True), inc="DS", incby=16)
            kb.op("y", lambda y: y.dma_start(d_sw[:, 1:2], winv[:, 0:1], allow_slow_non_contiguous=True), inc="DS", incby=16)
            kb.wait("y", "DS")
        kb.wait("g", "DG")
        kb.flush()

    with _ES() as loc:
        lsb = lambda n, s, d: loc.enter_context(nc.sbuf_tensor(n, s, d))
        lps = lambda n, s, d: loc.enter_context(nc.psum_tensor(n, s, d))
        kb.begin()
        wkva_bf = lsb("wkva_bf", [128, 32 * 384], BF16)
        csk_sb = lsb("csk_sb", [128, S], F32)
        xbf_ring = [lsb(f"xbf{i}", [128, 32 * TG], BF16) for i in range(2)]
        kv_ps = lps("kv_ps", [128, 3 * TG], F32)
        bc_ps = lps("bc_ps", [128, TG], F32)
        st_ps = lps("st_ps", [1, TG], F32)
        fo_ps = lps("fo_ps", [64, TG], F32)
        wsum_ps = lps("wsum_ps", [128, 3], F32)
        tp4_ps = lps("tp4_ps", [4, 128], F32)
        mbc = lsb("mbc", [128, TG], F32)
        nrsbc = lsb("nrsbc", [128, TG], F32)
        rcbc = lsb("rcbc", [128, TG], F32)
        ckv_f = lsb("ckv_f", [128, 2 * TG], F32)
        kpe_f = lsb("kpe_f", [128, TG], F32)
        w1 = lsb("w1", [128, TG], BF16)
        sqb = lsb("sqb", [128, 2 * TG], BF16)
        rrow = lsb("rrow", [1, TG], F32)
        wsumc = lsb("wsumc", [128, 3], F32)
        mseg = lsb("mseg", [1, TG], F32)
        nrseg = lsb("nrseg", [1, TG], F32)
        ctmp4 = lsb("ctmp4", [128, 4], F32)
        rowt4 = lsb("rowt4", [4, 128], F32)

        wk_tgt = kb.op("g", lambda g: g.dma_start(
            wkva_bf[:].rearrange("p (ci o) -> p ci o", o=384),
            wkvaT[:].rearrange("(ci p) o -> p ci o", p=128)), inc="Ge", incby=16)
        csk_tgt = kb.op("y", lambda y: y.dma_start(csk_sb[:], CSK[:]),
                        inc="Yc", incby=16)
        kb.wait("t", "Ge", wk_tgt)
        def wsums(t):
            last = None
            for oc in range(3):
                for ci in range(32):
                    last = t.matmul(
                        wsum_ps[:, oc:oc + 1],
                        wkva_bf[:, ci * 384 + 128 * oc:ci * 384 + 128 * oc + 128],
                        ones_b[:], start=(ci == 0), stop=(ci == 31))
            return last
        kb.op("t", wsums, inc="T")
        kb.wait("v", "T")
        kb.op("v", lambda v: v.tensor_copy(wsumc[:], wsum_ps[:]), inc="V")

        xg_free = {}
        for g_ in range(NTG):
            buf = xbf_ring[g_ % 2]
            kb.wait("g", "T", xg_free.get(g_ - 2, 0))
            sm = ("Ga", "Gb")[g_ % 2]
            dtgt = kb.op("g", lambda g, buf=buf, g_=g_: g.dma_start(
                buf[:].rearrange("p (ci f) -> p ci f", f=TG),
                xT[:, TG * g_:TG * g_ + TG].rearrange("(ci p) f -> p ci f", p=128)),
                inc=sm, incby=16)

            make_row_ops(
                lambda v, g_=g_: v.tensor_copy(ctmp4[:],
                                               statc[:, 8 * g_:8 * g_ + 8:2]),
                ctmp4[:], 4, tp4_ps, rowt4, mseg)
            def nrs_src(v, g_=g_):
                return v.tensor_scalar(ctmp4[:], statc[:, 8 * g_ + 1:8 * g_ + 9:2],
                                       -1.0, None, op0=ALU.mult)
            make_row_ops(nrs_src, ctmp4[:], 4, tp4_ps, rowt4, nrseg)

            kb.wait("t", "V")
            kb.wait("t", "DG")
            mtg = kb.op("t", lambda t: t.matmul(bc_ps[:], onesf_r[:], mseg[:],
                                                start=True, stop=True), inc="T")
            kb.wait("v", "T", mtg)
            kb.op("v", lambda v: v.tensor_copy(mbc[:], bc_ps[:]), inc="V")
            kb.wait("t", "V")
            mtg = kb.op("t", lambda t: t.matmul(bc_ps[:], onesf_r[:], nrseg[:],
                                                start=True, stop=True), inc="T")
            kb.wait("v", "T", mtg)
            kb.op("v", lambda v: v.tensor_copy(nrsbc[:], bc_ps[:]), inc="V")

            kb.wait("t", sm, dtgt)
            kb.wait("t", "V")
            def kva(t, buf=buf):
                last = None
                for oc in range(3):
                    for ci in range(32):
                        last = t.matmul(
                            kv_ps[:, TG * oc:TG * oc + TG],
                            wkva_bf[:, ci * 384 + 128 * oc:ci * 384 + 128 * oc + 128],
                            buf[:, ci * TG:ci * TG + TG],
                            start=(ci == 0), stop=(ci == 31))
                return last
            mm_tgt = kb.op("t", kva, inc="T")
            xg_free[g_] = mm_tgt

            kb.wait("v", "T", mm_tgt)
            def corr(v):
                for oc in range(2):
                    v.scalar_tensor_tensor(ckv_f[:, TG * oc:TG * oc + TG], mbc[:],
                                           wsumc[:, oc:oc + 1],
                                           kv_ps[:, TG * oc:TG * oc + TG],
                                           op0=ALU.mult, op1=ALU.subtract)
                    v.tensor_mul(ckv_f[:, TG * oc:TG * oc + TG],
                                 ckv_f[:, TG * oc:TG * oc + TG], nrsbc[:])
                v.scalar_tensor_tensor(kpe_f[:], mbc[:], wsumc[:, 2:3],
                                       kv_ps[:, 2 * TG:3 * TG],
                                       op0=ALU.mult, op1=ALU.subtract)
                return v.tensor_mul(kpe_f[:], kpe_f[:], nrsbc[:])
            kb.op("v", corr, inc="V")

            kb.wait("s", "V")
            kb.op("s", lambda s: s.activation(sqb[:], ckv_f[:], AF.Square), inc="Sg")
            kb.wait("t", "Sg")
            def sumsq(t):
                t.matmul(st_ps[:], ones_b[:], sqb[:, 0:TG], start=True, stop=False)
                return t.matmul(st_ps[:], ones_b[:], sqb[:, TG:2 * TG],
                                start=False, stop=True)
            mtg = kb.op("t", sumsq, inc="T")
            kb.wait("s", "T", mtg)
            def rsc(s):
                s.activation(rrow[:], st_ps[:], AF.Ln,
                             bias=consts[0:1, C_EPSAT:C_EPSAT + 1],
                             scale=consts[0:1, C_I256:C_I256 + 1])
                return s.activation(rrow[:], rrow[:], AF.Exp, bias=0.0,
                                    scale=consts[0:1, C_NEGH:C_NEGH + 1])
            kb.op("s", rsc, inc="Sg")
            kb.wait("t", "Sg")
            mtg = kb.op("t", lambda t: t.matmul(bc_ps[:], onesf_r[:], rrow[:],
                                                start=True, stop=True), inc="T")
            kb.wait("v", "T", mtg)
            def ckvn(v):
                v.tensor_copy(rcbc[:], bc_ps[:])
                v.tensor_mul(ckv_f[:, 0:TG], ckv_f[:, 0:TG], rcbc[:])
                return v.tensor_mul(ckv_f[:, TG:2 * TG], ckv_f[:, TG:2 * TG],
                                    rcbc[:])
            kb.op("v", ckvn, inc="V")
            kb.wait("s", "V")
            def ckv_gamma(s, g_=g_):
                s.activation(ckv_n[:, 0 * S + TG * g_:0 * S + TG * g_ + TG],
                             ckv_f[:, 0:TG], AF.Copy, bias=0.0,
                             scale=kvln_sb[:, 0:1])
                return s.activation(ckv_n[:, 1 * S + TG * g_:1 * S + TG * g_ + TG],
                                    ckv_f[:, TG:2 * TG], AF.Copy, bias=0.0,
                                    scale=kvln_sb[:, 1:2])
            kb.op("s", ckv_gamma, inc="Sg")

            kb.wait("v", "Yc", csk_tgt)
            def rope1(v, g_=g_):
                return v.tensor_tensor(w1[:], kpe_f[:],
                                       csk_sb[:, TG * g_:TG * g_ + TG], op=ALU.mult)
            kb.op("v", rope1, inc="V")
            kb.wait("t", "V")
            kb.wait("t", "Sg")
            mtg = kb.op("t", lambda t: t.matmul(fo_ps[:], fold_k[:], w1[:],
                                                start=True, stop=True), inc="T")
            kb.wait("s", "T", mtg)
            kb.op("s", lambda s, g_=g_: s.copy(kpe2[0:64, TG * g_:TG * g_ + TG],
                                               fo_ps[:]), inc="Sg")

        kb.wait("g", "Sg")
        kb.op("g", lambda g: g.dma_start(kpe2[64:128, :], kpe2[0:64, :]),
              inc="DG", incby=16)
        kb.wait("g", "DG")
        if debug:
            kb.op("g", lambda g: g.dma_start(d_ckvn[:], ckv_n[:]), inc="DG", incby=16)
            kb.op("g", lambda g: g.dma_start(d_kpe2[:], kpe2[:]), inc="DG", incby=16)
            kb.wait("g", "DG")
        kb.flush()

    with _ES() as loc:
        lsb = lambda n, s, d: loc.enter_context(nc.sbuf_tensor(n, s, d))
        lps = lambda n, s, d: loc.enter_context(nc.psum_tensor(n, s, d))
        kb.begin()
        xbfq = lsb("xbfq", [128, 32 * NQ], BF16)
        wqa_ring = [lsb(f"wqa{i}", [128, 32 * 128], BF16) for i in range(2)]
        q1_ps = lps("q1_ps", [128, 512], F32)
        ws_ps = lps("ws_ps", [128, 1], F32)
        st2_ps = lps("st2_ps", [1, 512], F32)
        bc2_ps = lps("bc2_ps", [128, 512], F32)
        tp8_ps = lps("tp8_ps", [8, 128], F32)
        mbcq = lsb("mbcq", [128, 512], F32)
        nbcq = lsb("nbcq", [128, 512], F32)
        rbcq = lsb("rbcq", [128, 512], F32)
        q1f = lsb("q1f", [128, 8 * 512], F32)
        sq2 = [lsb(f"sq2{i}", [128, 512], BF16) for i in range(2)]
        r2row = lsb("r2row", [1, 512], F32)
        wsq = lsb("wsq", [128, 8], F32)
        mqrow = lsb("mqrow", [1, NQ], F32)
        nrsqrow = lsb("nrsqrow", [1, NQ], F32)
        ctmp8 = lsb("ctmp8", [128, 8], F32)
        rowt8 = lsb("rowt8", [8, 128], F32)

        xq_tgt = kb.op("g", lambda g: g.dma_start(
            xbfq[:].rearrange("p (ci f) -> p ci f", f=NQ),
            xqT[:].rearrange("(ci p) f -> p ci f", p=128)), inc="DG", incby=16)

        make_row_ops(lambda v: v.tensor_copy(ctmp8[:], qm[:]),
                     ctmp8[:], 8, tp8_ps, rowt8, mqrow)
        make_row_ops(lambda v: v.tensor_scalar(ctmp8[:], qrs[:], -1.0,
                                               None, op0=ALU.mult),
                     ctmp8[:], 8, tp8_ps, rowt8, nrsqrow)

        wq_free = {}
        for th in range(2):
            kb.wait("t", "V")
            kb.wait("t", "DG")
            mtg = kb.op("t", lambda t, th=th: t.matmul(
                bc2_ps[:], onesf_r[:], mqrow[:, 512 * th:512 * th + 512],
                start=True, stop=True), inc="T")
            kb.wait("v", "T", mtg)
            kb.op("v", lambda v: v.tensor_copy(mbcq[:], bc2_ps[:]), inc="V")
            kb.wait("t", "V")
            mtg = kb.op("t", lambda t, th=th: t.matmul(
                bc2_ps[:], onesf_r[:], nrsqrow[:, 512 * th:512 * th + 512],
                start=True, stop=True), inc="T")
            kb.wait("v", "T", mtg)
            kb.op("v", lambda v: v.tensor_copy(nbcq[:], bc2_ps[:]), inc="V")

            for oc in range(8):
                wbuf = wqa_ring[oc % 2]
                kb.wait("g", "T", wq_free.get(8 * th + oc - 2, 0))
                dtgt = kb.op("g", lambda g, wbuf=wbuf, oc=oc: g.dma_start(
                    wbuf[:].rearrange("p (ci f) -> p ci f", f=128),
                    wqaT[:, 128 * oc:128 * oc + 128]
                    .rearrange("(ci p) f -> p ci f", p=128)), inc="DG", incby=16)
                kb.wait("t", "DG", dtgt)
                kb.wait("t", "V")
                if th == 0:
                    def wsum_q(t, wbuf=wbuf):
                        last = None
                        for ci in range(32):
                            last = t.matmul(ws_ps[:],
                                            wbuf[:, 128 * ci:128 * ci + 128],
                                            ones_b[:], start=(ci == 0),
                                            stop=(ci == 31))
                        return last
                    kb.op("t", wsum_q, inc="T")
                    kb.wait("v", "T")
                    kb.op("v", lambda v, oc=oc: v.tensor_copy(wsq[:, oc:oc + 1],
                                                              ws_ps[:]), inc="V")
                    kb.wait("t", "V")
                kb.wait("t", "DG", xq_tgt)
                def q1mm(t, wbuf=wbuf, th=th):
                    last = None
                    for ci in range(32):
                        last = t.matmul(
                            q1_ps[:], wbuf[:, 128 * ci:128 * ci + 128],
                            xbfq[:, ci * NQ + 512 * th:ci * NQ + 512 * th + 512],
                            start=(ci == 0), stop=(ci == 31))
                    return last
                mm_tgt = kb.op("t", q1mm, inc="T")
                wq_free[8 * th + oc] = mm_tgt
                kb.wait("v", "T", mm_tgt)
                def q1corr(v, oc=oc):
                    v.scalar_tensor_tensor(q1f[:, 512 * oc:512 * oc + 512], mbcq[:],
                                           wsq[:, oc:oc + 1], q1_ps[:],
                                           op0=ALU.mult, op1=ALU.subtract)
                    return v.tensor_mul(q1f[:, 512 * oc:512 * oc + 512],
                                        q1f[:, 512 * oc:512 * oc + 512], nbcq[:])
                kb.op("v", q1corr, inc="V")

            for oc in range(8):
                kb.wait("s", "V")
                kb.wait("s", "T")
                kb.op("s", lambda s, oc=oc: s.activation(
                    sq2[oc % 2][:], q1f[:, 512 * oc:512 * oc + 512], AF.Square),
                    inc="Sg")
                kb.wait("t", "Sg")
                kb.op("t", lambda t, oc=oc: t.matmul(
                    st2_ps[:], ones_b[:], sq2[oc % 2][:],
                    start=(oc == 0), stop=(oc == 7)), inc="T")
            kb.wait("s", "T")
            def rs2q(s):
                s.activation(r2row[:], st2_ps[:], AF.Ln,
                             bias=consts[0:1, C_EPSAT:C_EPSAT + 1],
                             scale=consts[0:1, C_I1024:C_I1024 + 1])
                return s.activation(r2row[:], r2row[:], AF.Exp, bias=0.0,
                                    scale=consts[0:1, C_NEGH:C_NEGH + 1])
            kb.op("s", rs2q, inc="Sg")
            kb.wait("v", "Sg")
            kb.op("v", lambda v: v.tensor_scalar(r2row[:], r2row[:], SOFTMAX_SCALE,
                                                 None, op0=ALU.mult), inc="V")
            kb.wait("t", "V")
            mtg = kb.op("t", lambda t: t.matmul(bc2_ps[:], onesf_r[:], r2row[:],
                                                start=True, stop=True), inc="T")
            kb.wait("v", "T", mtg)
            def q1n_mul(v):
                v.tensor_copy(rbcq[:], bc2_ps[:])
                last = None
                for oc in range(8):
                    last = v.tensor_mul(q1f[:, 512 * oc:512 * oc + 512],
                                        q1f[:, 512 * oc:512 * oc + 512], rbcq[:])
                return last
            kb.op("v", q1n_mul, inc="V")
            kb.wait("s", "V")
            def q1n_gam(s, th=th):
                last = None
                for oc in range(8):
                    last = s.activation(
                        q1n[:, oc * NQ + 512 * th:oc * NQ + 512 * th + 512],
                        q1f[:, 512 * oc:512 * oc + 512], AF.Copy, bias=0.0,
                        scale=qln_sb[:, oc:oc + 1])
                return last
            kb.op("s", q1n_gam, inc="Sg")
        kb.wait("t", "Sg")
        if debug:
            kb.wait("g", "Sg")
            kb.op("g", lambda g: g.dma_start(d_q1n[:], q1n[:]), inc="DG", incby=16)
            kb.wait("g", "DG")
        kb.flush()

    oh_scr = nc.dram_tensor("oh_scr", [D, NQ], BF16, kind="Internal")

    with _ES() as loc:
        lsb = lambda n, s, d: loc.enter_context(nc.sbuf_tensor(n, s, d))
        lps = lambda n, s, d: loc.enter_context(nc.psum_tensor(n, s, d))
        kb.begin()
        wbn_ring = [lsb(f"wbn{i}", [128, 2 * 128], BF16) for i in range(2)]
        wbv_ring = [lsb(f"wbv{i}", [128, 2 * 256], BF16) for i in range(2)]
        wqn_ring = [lsb(f"wqn{i}", [128, 8 * 128], BF16) for i in range(2)]
        wqp_ring = [lsb(f"wqp{i}", [128, 8 * 256], BF16) for i in range(2)]
        kf = [lsb(f"kf{i}", [128, S], BF16) for i in range(2)]
        vv = [lsb(f"vv{i}", [128, S], BF16) for i in range(2)]
        qf = [lsb(f"qf{i}", [128, NQ], BF16) for i in range(2)]
        qpe_f = lsb("qpe_f", [128, NQ], F32)
        w1q = lsb("w1q", [128, NQ], BF16)
        p_sb = [lsb(f"p_sb{i}", [128, S], BF16) for i in range(2)]
        ptbuf = [lsb(f"ptb{i}", [128, S], BF16) for i in range(2)]
        rowr = lsb("rowr", [8, 128], F32)
        ohf = lsb("ohf", [128, NQ], F32)
        rrow2 = lsb("rrow2", [1, NQ], F32)

        oh_hbuf = [lsb(f"ohb{i}", [128, NQ], BF16) for i in range(2)]
        arena = lps("arena", [128, 4096], F32)
        arena_b = arena[:].bitcast(BF16)
        pexp_gate = [0, 0]
        oh_st_tgt = [0, 0]

        def load_pair(pi, ridx):
            sm = "DG"
            kb.op("g", lambda g, pi=pi, ridx=ridx: g.dma_start(
                wbn_ring[ridx][:].rearrange("p (cc f) -> p cc f", f=128),
                wkvbnT[:, 128 * pi:128 * pi + 128]
                .rearrange("(cc p) f -> p cc f", p=128)), inc=sm, incby=16)
            kb.op("g", lambda g, pi=pi, ridx=ridx: g.dma_start(
                wbv_ring[ridx][:].rearrange("p (cc f) -> p cc f", f=256),
                wkvbvT[:, 256 * pi:256 * pi + 256]
                .rearrange("(cc p) f -> p cc f", p=128)), inc=sm, incby=16)
            kb.op("g", lambda g, pi=pi, ridx=ridx: g.dma_start(
                wqn_ring[ridx][:].rearrange("p (ci f) -> p ci f", f=128),
                wqbnT[:, 128 * pi:128 * pi + 128]
                .rearrange("(ci p) f -> p ci f", p=128)), inc=sm, incby=16)
            return kb.op("g", lambda g, pi=pi, ridx=ridx: g.dma_start(
                wqp_ring[ridx][:].rearrange("p (ci f) -> p ci f", f=256),
                wqbpT[:, 256 * pi:256 * pi + 256]
                .rearrange("(ci p) f -> p ci f", p=128)), inc=sm, incby=16)

        pair_w_tgt = {0: load_pair(0, 0)}

        for pi in range(16):
            ridx = pi % 2
            kb.wait("t", "DG", pair_w_tgt[pi])

            for kc in range(8):
                kb.wait("t", "V")
                kb.wait("t", "Sg")
                def nope_mm(t, ridx=ridx, kc=kc):
                    t.matmul(arena[:, 0:512], wbn_ring[ridx][:, 0:128],
                             ckv_n[:, 0 * S + 512 * kc:0 * S + 512 * kc + 512],
                             start=True, stop=False)
                    return t.matmul(arena[:, 0:512], wbn_ring[ridx][:, 128:256],
                                    ckv_n[:, 1 * S + 512 * kc:1 * S + 512 * kc + 512],
                                    start=False, stop=True)
                mtg = kb.op("t", nope_mm, inc="T")
                kb.wait("v", "T", mtg)
                def nope_ev(v, kc=kc):
                    v.tensor_copy(kf[0][0:64, 512 * kc:512 * kc + 512],
                                  arena[0:64, 0:512])
                    return v.tensor_copy(kf[1][64:128, 512 * kc:512 * kc + 512],
                                         arena[64:128, 0:512])
                kb.op("v", nope_ev, inc="V")
            def kpe_cp(v):
                v.tensor_copy(kf[0][64:128, :], kpe2[64:128, :])
                return v.tensor_copy(kf[1][0:64, :], kpe2[0:64, :])
            kb.op("v", kpe_cp, inc="V")

            for kg in range(16):
                kb.wait("t", "V")
                def v_mm(t, ridx=ridx, kg=kg):
                    last = None
                    for u in range(2):
                        kt = 2 * kg + u
                        t.matmul(arena[:, 512 + 256 * u:512 + 256 * u + 256],
                                 ckv_n[:, 0 * S + 128 * kt:0 * S + 128 * kt + 128],
                                 wbv_ring[ridx][:, 0:256], start=True, stop=False)
                        last = t.matmul(
                            arena[:, 512 + 256 * u:512 + 256 * u + 256],
                            ckv_n[:, 1 * S + 128 * kt:1 * S + 128 * kt + 128],
                            wbv_ring[ridx][:, 256:512], start=False, stop=True)
                    return last
                mtg = kb.op("t", v_mm, inc="T")
                kb.wait("v", "T", mtg)
                def v_ev(v, kg=kg):
                    src = arena[:, 512:1024].rearrange("p (u hv) -> p u hv", hv=256)
                    v.tensor_copy(vv[0][:, 256 * kg:256 * kg + 256]
                                  .rearrange("p (u f) -> p u f", f=128),
                                  src[:, :, 0:128])
                    return v.tensor_copy(vv[1][:, 256 * kg:256 * kg + 256]
                                         .rearrange("p (u f) -> p u f", f=128),
                                         src[:, :, 128:256])
                kb.op("v", v_ev, inc="V")

            for th in range(2):
                kb.wait("t", "V")
                def qn_mm(t, ridx=ridx, th=th):
                    last = None
                    for ci in range(8):
                        last = t.matmul(
                            arena[:, 0:512],
                            wqn_ring[ridx][:, 128 * ci:128 * ci + 128],
                            q1n[:, ci * NQ + 512 * th:ci * NQ + 512 * th + 512],
                            start=(ci == 0), stop=(ci == 7))
                    return last
                mtg = kb.op("t", qn_mm, inc="T")
                kb.wait("v", "T", mtg)
                def qn_ev(v, th=th):
                    v.tensor_copy(qf[0][0:64, 512 * th:512 * th + 512],
                                  arena[0:64, 0:512])
                    return v.tensor_copy(qf[1][64:128, 512 * th:512 * th + 512],
                                         arena[64:128, 0:512])
                kb.op("v", qn_ev, inc="V")

            for hh in range(2):
                for th in range(2):
                    kb.wait("t", "V")
                    def qp_mm(t, ridx=ridx, hh=hh, th=th):
                        last = None
                        for ci in range(8):
                            last = t.matmul(
                                arena[:, 0:512],
                                wqp_ring[ridx][:, 256 * ci + 128 * hh:
                                               256 * ci + 128 * hh + 128],
                                q1n[:, ci * NQ + 512 * th:ci * NQ + 512 * th + 512],
                                start=(ci == 0), stop=(ci == 7))
                        return last
                    mtg = kb.op("t", qp_mm, inc="T")
                    kb.wait("v", "T", mtg)
                    kb.op("v", lambda v, th=th: v.tensor_copy(
                        qpe_f[:, 512 * th:512 * th + 512], arena[:, 0:512]),
                        inc="V")
                kb.op("v", lambda v: v.tensor_tensor(w1q[:], qpe_f[:], csq[:],
                                                     op=ALU.mult), inc="V")
                fold_m = fold_hi if hh == 0 else fold_lo
                for th in range(2):
                    kb.wait("t", "V")
                    kb.wait("t", "Sg")
                    mtg = kb.op("t", lambda t, fold_m=fold_m, th=th: t.matmul(
                        arena[:, 512:1024], fold_m[:],
                        w1q[:, 512 * th:512 * th + 512], start=True, stop=True),
                        inc="T")
                    kb.wait("s", "T", mtg)
                    def qpe_ev(s, hh=hh, th=th):
                        if hh == 0:
                            return s.copy(qf[0][64:128, 512 * th:512 * th + 512],
                                          arena[64:128, 512:1024])
                        return s.copy(qf[1][0:64, 512 * th:512 * th + 512],
                                      arena[0:64, 512:1024])
                    kb.op("s", qpe_ev, inc="Sg")

            if pi + 1 < 16:
                kb.wait("g", "T")
                pair_w_tgt[pi + 1] = load_pair(pi + 1, (pi + 1) % 2)

            head_etg = [0]
            for hh in range(2):
                h = 2 * pi + hh

                def emit_pv(pj, pnkt, petg, hh=hh):
                    ttg = 0
                    last_ptev = 0
                    for tb in range(0, pnkt, 8):
                        te = min(pnkt, tb + 8)
                        if tb == 0:
                            kb.wait("t", "Sg", petg)
                            if pj == 0:
                                kb.wait("t", "V")
                        else:
                            kb.wait("t", "V", last_ptev)
                        def ptr(t, tb=tb, te=te, pj=pj):
                            last = None
                            for u in range(te - tb):
                                last = t.transpose(
                                    arena_b[:, 7168 + 128 * u:
                                            7168 + 128 * u + 128],
                                    p_sb[pj % 2][:, 128 * (tb + u):
                                                 128 * (tb + u) + 128],
                                    i128_b[:])
                            return last
                        ttg = kb.op("t", ptr, inc="T")
                        kb.wait("v", "T", ttg)
                        def ptev(v, tb=tb, te=te, pj=pj):
                            return v.tensor_copy(
                                ptbuf[pj % 2][:, 128 * tb:128 * te],
                                arena_b[:, 7168:7168 + 128 * (te - tb)])
                        last_ptev = kb.op("v", ptev, inc="V")
                    pexp_gate[pj % 2] = ttg
                    kb.wait("t", "V", last_ptev)
                    def pv(t, hh=hh, pnkt=pnkt, pj=pj):
                        last = None
                        for kt in range(pnkt):
                            last = t.matmul(
                                arena[:, 3072:3200],
                                vv[hh][:, 128 * kt:128 * kt + 128],
                                ptbuf[pj % 2][:, 128 * kt:128 * kt + 128],
                                start=(kt == 0), stop=(kt == pnkt - 1))
                        return last
                    mtg2 = kb.op("t", pv, inc="T")
                    kb.wait("v", "T", mtg2)
                    kb.op("v", lambda v, pj=pj: v.tensor_copy(
                        ohf[:, 128 * pj:128 * pj + 128], arena[:, 3072:3200]),
                        inc="V")

                pend = None
                etg_prev = head_etg[0]
                for j in range(NJ):
                    span = 512 * (j + 1)
                    if j >= 6 and pend is not None:
                        emit_pv(*pend)
                        pend = None
                        kb.wait("t", "V")
                    if j == 0 and hh == 0:
                        kb.wait("t", "V")
                        kb.wait("t", "Sg")
                    elif etg_prev:
                        kb.wait("t", "Sg", etg_prev)
                    def sc_mm(t, hh=hh, j=j):
                        last = None
                        for kc in range(j + 1):
                            last = t.matmul(arena[:, 512 * kc:512 * kc + 512],
                                            qf[hh][:, 128 * j:128 * j + 128],
                                            kf[hh][:, 512 * kc:512 * kc + 512],
                                            start=True, stop=True)
                        return last
                    mtg = kb.op("t", sc_mm, inc="T")
                    kb.wait("v", "T", mtg)
                    def mask_mx(v, j=j, span=span):
                        v.tensor_add(arena[:, 512 * j:512 * j + 512],
                                     arena[:, 512 * j:512 * j + 512], maskw[:])
                        return v.reduce_max(sumc[:, 8:9], arena[:, 0:span],
                                            axis=AX.X, negate=True)
                    vtg = kb.op("v", mask_mx, inc="V")
                    kb.wait("s", "V", vtg)
                    kb.wait("s", "T", pexp_gate[j % 2])
                    def ex(s, span=span, j=j):
                        half = span // 2
                        for sec, (lo, hi) in enumerate([(0, half),
                                                        (half, span)]):
                            last = s.activation(
                                p_sb[j % 2][:, lo:hi], arena[:, lo:hi], AF.Exp,
                                bias=sumc[:, 8:9],
                                accum_out=sumc[:, 10 + sec:11 + sec])
                        return last
                    etg = kb.op("s", ex, inc="Sg")
                    kb.wait("v", "Sg", etg)
                    kb.op("v", lambda v, j=j: v.reduce_sum(
                        sumc[:, j:j + 1], sumc[:, 10:12], axis=AX.X), inc="V")
                    nkt_j = span // 128
                    LAG = True
                    if LAG:
                        if pend is not None:
                            emit_pv(*pend)
                        pend = (j, nkt_j, etg)
                    else:
                        emit_pv(j, nkt_j, etg)
                    etg_prev = etg
                if pend is not None:
                    emit_pv(*pend)
                head_etg[0] = etg_prev

                kb.op("v", lambda v: v.reciprocal(sumc[:, 8:16], sumc[:, 0:8]),
                      inc="V")
                kb.wait("t", "V")
                kb.op("t", lambda t: t.transpose(arena[0:8, 3072:3200],
                                                 sumc[:, 8:16], i128_f[:]),
                      inc="T")
                kb.wait("v", "T")
                kb.op("v", lambda v: v.tensor_copy(rowr[:],
                                                   arena[0:8, 3072:3200]),
                      inc="V")
                kb.wait("g", "V")
                fl = kb.op("g", lambda g: g.dma_start(
                    rrow2[:].rearrange("o (a b) -> o a b", a=8),
                    rowr[:]), inc="DG", incby=16)
                kb.wait("t", "DG", fl)
                def recb_outer(t):
                    last = None
                    for u in range(2):
                        last = t.matmul(arena[:, 3072 + 512 * u:3584 + 512 * u],
                                        onesf_r[:],
                                        rrow2[:, 512 * u:512 * u + 512],
                                        start=True, stop=True)
                    return last
                rtg = kb.op("t", recb_outer, inc="T")
                kb.wait("v", "T", rtg)
                kb.wait("v", "DG", oh_st_tgt[h % 2])
                vtg3 = kb.op("v", lambda v, h=h: v.tensor_mul(
                    oh_hbuf[h % 2][:], arena[:, 3072:4096], ohf[:]), inc="V")
                kb.wait("g", "V", vtg3)
                oh_st_tgt[h % 2] = kb.op("g", lambda g, h=h: g.dma_start(
                    oh_scr[128 * h:128 * h + 128, :], oh_hbuf[h % 2][:]),
                    inc="DG", incby=16)
            if debug and pi == 0:
                kb.wait("g", "V")
                kb.op("g", lambda g: g.dma_start(d_kf0[:], kf[0][:]), inc="DG", incby=16)
                kb.op("g", lambda g: g.dma_start(d_kf1[:], kf[1][:]), inc="DG", incby=16)
                kb.op("g", lambda g: g.dma_start(d_vv0[:], vv[0][:]), inc="DG", incby=16)
                kb.op("g", lambda g: g.dma_start(d_qf0[:], qf[0][:]), inc="DG", incby=16)
                kb.op("g", lambda g: g.dma_start(d_qf1[:], qf[1][:]), inc="DG", incby=16)
                kb.op("g", lambda g: g.dma_start(d_p0[:], p_sb[1][:]), inc="DG", incby=16)
                kb.op("g", lambda g: g.dma_start(d_sum[:], sumc[:]), inc="DG", incby=16)
                kb.wait("g", "DG")
                kb.wait("t", "DG")
                kb.wait("v", "DG")
                kb.wait("s", "DG")
        if debug:
            kb.wait("g", "DG")
            kb.wait("g", "Gc")
            kb.wait("g", "Gd")
            kb.op("g", lambda g: g.dma_start(
                d_oh[:].rearrange("p (h q) -> p h q", q=NQ),
                oh_scr[:].rearrange("(h p) q -> p h q", p=128)),
                inc="DG", incby=16)
            kb.wait("g", "DG")
        kb.flush()

    with _ES() as loc:
        lsb = lambda n, s, d: loc.enter_context(nc.sbuf_tensor(n, s, d))
        lps = lambda n, s, d: loc.enter_context(nc.psum_tensor(n, s, d))
        kb.begin()
        kb.wait("g", "DG")
        kb.wait("g", "Gc")
        kb.wait("g", "Gd")
        wo_ring = [lsb(f"wor{i}", [128, 512], BF16) for i in range(4)]
        oh_ring = [lsb(f"ohr{i}", [128, NQ], BF16) for i in range(4)]
        xq_ring = [lsb(f"xqr{i}", [128, NQ], BF16) for i in range(4)]
        wt_ring = [lsb(f"wtr{i}", [128, 512], F32) for i in range(4)]
        tern_ring = [lsb(f"ter{i}", [128, 512], BF16) for i in range(4)]
        sgn_a = [lsb(f"sgn_a{i}", [128, 512], BF16) for i in range(4)]
        xf_sb = lsb("xf_sb", [128, 4 * NQ], F32)
        acc = lsb("acc", [128, 4 * NQ], F32)
        ybuf = lsb("ybuf", [128, NQ], F32)
        a1bc = lsb("a1bc", [128, NQ], F32)
        bqbc = lsb("bqbc", [128, NQ], F32)
        divbc = lsb("divbc", [128, NQ], F32)
        a1qrow = lsb("a1qrow", [1, NQ], F32)
        bqrow = lsb("bqrow", [1, NQ], F32)
        divrow = lsb("divrow", [1, NQ], F32)
        ctmp8b = lsb("ctmp8b", [128, 8], F32)
        ctmp8c = lsb("ctmp8c", [128, 8], F32)
        rowt8b = lsb("rowt8b", [8, 128], F32)
        y_ps = lps("y_ps", [128, 4096], F32)

        make_row_ops(lambda v: v.tensor_scalar(ctmp8b[:], qrs[:], 1.0,
                                               None, op0=ALU.add),
                     ctmp8b[:], 8, y_ps[0:8, 0:128], rowt8b, a1qrow)
        def bqr_src(v):
            v.tensor_mul(ctmp8c[:], qm[:], qrs[:])
            v.tensor_copy(ctmp8b[:], consts[:, 0:8])
            return v.tensor_scalar(ctmp8b[:], ctmp8c[:], -1.0, None,
                                   op0=ALU.mult)
        make_row_ops(bqr_src, ctmp8b[:], 8, y_ps[0:8, 0:128], rowt8b, bqrow)
        make_row_ops(lambda v: v.tensor_copy(ctmp8b[:], qDIV[:]),
                     ctmp8b[:], 8, y_ps[0:8, 0:128], rowt8b, divrow)

        for rowsrc, dst in [(a1qrow, a1bc), (bqrow, bqbc), (divrow, divbc)]:
            for th in range(2):
                kb.wait("t", "V")
                kb.wait("t", "DG")
                mtg = kb.op("t", lambda t, rowsrc=rowsrc, th=th: t.matmul(
                    y_ps[:, 0:512], onesf_r[:], rowsrc[:, 512 * th:512 * th + 512],
                    start=True, stop=True), inc="T")
                kb.wait("v", "T", mtg)
                kb.op("v", lambda v, dst=dst, th=th: v.tensor_copy(
                    dst[:, 512 * th:512 * th + 512], y_ps[:, 0:512]), inc="V")

        sgn_done = {}
        tern_done = {}
        ci_mm = {}
        ff_mm = {}
        for ocb in range(8):
            for ci in range(32):
                wbuf = wo_ring[ci % 4]
                kb.wait("g", "T", ci_mm.get(32 * ocb + ci - 4, 0))
                sm = ("Ga", "Gb", "Gj", "Gk")[ci % 4]
                dtg = kb.op("g", lambda g, ocb=ocb, ci=ci, wbuf=wbuf: g.dma_start(
                    wbuf[:],
                    woT[128 * ci:128 * ci + 128, 512 * ocb:512 * ocb + 512]),
                    inc=sm, incby=16)
                odt = kb.op("g", lambda g, ci=ci: g.dma_start(
                    oh_ring[ci % 4][:], oh_scr[128 * ci:128 * ci + 128, :]),
                    inc=sm, incby=16)
                kb.wait("t", sm, odt)
                if ci == 0:
                    kb.wait("t", "V")
                def womm(t, ci=ci, wbuf=wbuf):
                    last = None
                    for oc in range(4):
                        for th in range(2):
                            last = t.matmul(
                                y_ps[:, 1024 * oc + 512 * th:
                                     1024 * oc + 512 * th + 512],
                                wbuf[:, 128 * oc:128 * oc + 128],
                                oh_ring[ci % 4][:, 512 * th:512 * th + 512],
                                start=(ci == 0), stop=(ci == 31))
                    return last
                ci_mm[32 * ocb + ci] = kb.op("t", womm, inc="T")
            wot = ci_mm[32 * ocb + 31]
            kb.wait("y", "V")
            xtg = kb.op("y", lambda y, ocb=ocb: y.dma_start(
                xf_sb[:].rearrange("p (oc q) -> p oc q", q=NQ),
                xqT[512 * ocb:512 * ocb + 512, :]
                .rearrange("(oc p) q -> p oc q", p=128)), inc="DS", incby=16)
            kb.wait("v", "T", wot)
            kb.wait("v", "DS", xtg)
            def accb(v):
                last = None
                for oc in range(4):
                    v.tensor_add(acc[:, NQ * oc:NQ * oc + NQ],
                                 y_ps[:, 1024 * oc:1024 * oc + NQ], bqbc[:])
                    v.tensor_mul(ybuf[:], xf_sb[:, NQ * oc:NQ * oc + NQ], a1bc[:])
                    last = v.tensor_add(acc[:, NQ * oc:NQ * oc + NQ],
                                        acc[:, NQ * oc:NQ * oc + NQ], ybuf[:])
                return last
            kb.op("v", accb, inc="V")
            if debug and ocb == 0:
                kb.wait("y", "V")
                kb.op("y", lambda y: y.dma_start(d_acc1[:], acc[:, 0:NQ]),
                      inc="DS", incby=16)
                kb.op("y", lambda y: y.dma_start(d_rows3[:, 0:NQ], a1qrow[:]),
                      inc="DS", incby=16)
                kb.op("y", lambda y: y.dma_start(d_rows3[:, NQ:2 * NQ],
                                                 bqrow[:]), inc="DS", incby=16)
                kb.op("y", lambda y: y.dma_start(d_rows3[:, 2 * NQ:3 * NQ],
                                                 divrow[:]), inc="DS", incby=16)
                kb.wait("v", "DS")

            for ci in range(32):
                wbuf = wt_ring[ci % 4]
                tbuf = tern_ring[ci % 4]
                kb.wait("y", "Sg", sgn_done.get(32 * ocb + ci - 4, 0))
                sm = ("Ya", "Yb", "Yd", "Ye")[ci % 4]
                dtg = kb.op("y", lambda y, ocb=ocb, ci=ci, wbuf=wbuf: y.dma_start(
                    wbuf[:],
                    WT[128 * ci:128 * ci + 128, 512 * ocb:512 * ocb + 512]),
                    inc=sm, incby=16)
                kb.wait("g", "T", ff_mm.get(32 * ocb + ci - 4, 0))
                xdt = kb.op("g", lambda g, ci=ci: g.dma_start(
                    xq_ring[ci % 4][:], xqf_scr[128 * ci:128 * ci + 128, :]),
                    inc=("Gh", "Gi", "Gl", "Gm")[ci % 4], incby=16)
                kb.wait("s", sm, dtg)
                kb.wait("s", "T", ff_mm.get(32 * ocb + ci - 4, 0))
                kb.wait("s", "V", tern_done.get(32 * ocb + ci - 4, 0))
                def sgn(s, wbuf=wbuf, tbuf=tbuf, ci=ci):
                    s.activation(sgn_a[ci % 4][:], wbuf[:], AF.Sign,
                                 bias=consts[:, C_BA2:C_BA2 + 1],
                                 scale=swcol[:, 0:1])
                    return s.activation(tbuf[:], wbuf[:], AF.Sign,
                                        bias=consts[:, C_BB2:C_BB2 + 1],
                                        scale=swcol[:, 0:1])
                stg = kb.op("s", sgn, inc="Sg")
                sgn_done[32 * ocb + ci] = stg
                kb.wait("v", "Sg", stg)
                vtg = kb.op("v", lambda v, tbuf=tbuf, ci=ci: v.tensor_add(
                    tbuf[:], tbuf[:], sgn_a[ci % 4][:]), inc="V")
                tern_done[32 * ocb + ci] = vtg
                kb.wait("t", "V", vtg)
                kb.wait("t", ("Gh", "Gi", "Gl", "Gm")[ci % 4], xdt)
                if ci == 0:
                    kb.wait("t", "V")
                def ffmm(t, ci=ci, tbuf=tbuf):
                    last = None
                    for oc in range(4):
                        for th in range(2):
                            last = t.matmul(
                                y_ps[:, 1024 * oc + 512 * th:
                                     1024 * oc + 512 * th + 512],
                                tbuf[:, 128 * oc:128 * oc + 128],
                                xq_ring[ci % 4][:, 512 * th:512 * th + 512],
                                start=(ci == 0), stop=(ci == 31))
                    return last
                ff_mm[32 * ocb + ci] = kb.op("t", ffmm, inc="T")
            fft = ff_mm[32 * ocb + 31]
            kb.wait("v", "T", fft)
            for oc in range(4):
                vtg = kb.op("v", lambda v, oc=oc: v.tensor_mul(
                    ybuf[:], y_ps[:, 1024 * oc:1024 * oc + NQ], divbc[:]), inc="V")
                kb.wait("s", "V", vtg)
                stg = kb.op("s", lambda s, ocb=ocb, oc=oc: s.activation(
                    ybuf[:], ybuf[:], AF.Gelu,
                    bias=bq_sb[:, 4 * ocb + oc:4 * ocb + oc + 1], scale=1.0),
                    inc="Sg")
                kb.wait("v", "Sg", stg)
                if debug and ocb == 0 and oc == 0:
                    kb.wait("y", "Sg", stg)
                    kb.op("y", lambda y: y.dma_start(d_ffny[:], ybuf[:]),
                          inc="DS", incby=16)
                    kb.wait("v", "DS")
                vtg2 = kb.op("v", lambda v, oc=oc: v.tensor_add(
                    acc[:, NQ * oc:NQ * oc + NQ], acc[:, NQ * oc:NQ * oc + NQ],
                    ybuf[:]), inc="V")
                kb.wait("y", "V", vtg2)
                kb.op("y", lambda y, ocb=ocb, oc=oc: y.dma_start(
                    out[128 * (4 * ocb + oc):128 * (4 * ocb + oc) + 128, :],
                    acc[:, NQ * oc:NQ * oc + NQ]), inc="DS", incby=16)
        kb.wait("y", "DS")
        kb.flush()

    stk.close()
    return nc


_PROG = {}


def _get_prog():
    if "nc" not in _PROG:
        _PROG["nc"] = build_program()
    return _PROG["nc"]


def kernel(L0_attn_x, L0_attn_wq_a, L0_attn_q_a_ln_w, L0_attn_wq_b, L0_attn_wkv_a,
           L0_attn_kv_a_ln_w, L0_attn_wkv_b, L0_attn_wo, L0_ffn_W, L0_ffn_b):
    x = np.asarray(L0_attn_x, dtype=np.float32)
    wq_a = np.asarray(L0_attn_wq_a, dtype=np.float32)
    q_a_ln = np.asarray(L0_attn_q_a_ln_w, dtype=np.float32)
    wq_b = np.asarray(L0_attn_wq_b, dtype=np.float32)
    wkv_a = np.asarray(L0_attn_wkv_a, dtype=np.float32)
    kv_a_ln = np.asarray(L0_attn_kv_a_ln_w, dtype=np.float32)
    wkv_b = np.asarray(L0_attn_wkv_b, dtype=np.float32)
    wo = np.asarray(L0_attn_wo, dtype=np.float32)
    W = np.asarray(L0_ffn_W, dtype=np.float32)
    bias = np.asarray(L0_ffn_b, dtype=np.float32)

    C = np.ascontiguousarray
    deint = np.concatenate([np.arange(0, 64, 2), np.arange(1, 64, 2)])
    rotp = np.concatenate([np.arange(32, 64), np.arange(0, 32)])

    idx_n = np.concatenate([h * 128 + np.arange(64) for h in range(H)])
    wqbnT = C(wq_b[idx_n].T)
    idx_p = []
    for h in range(H):
        dr = h * 128 + 64 + deint
        idx_p.append(dr)
        idx_p.append(dr[rotp])
    idx_p = np.concatenate(idx_p)
    wqbpT = C(wq_b[idx_p].T)

    kpe_rows = 256 + deint
    wkvaT = C(np.concatenate([wkv_a[0:256], wkv_a[kpe_rows],
                              wkv_a[kpe_rows][rotp]], axis=0).T)
    idx_kn = np.concatenate([h * 192 + np.arange(64) for h in range(H)])
    wkvbnT = C(wkv_b[idx_kn].T)
    idx_kv = np.concatenate([h * 192 + 64 + np.arange(128) for h in range(H)])
    wkvbvT = C(wkv_b[idx_kv].T)

    wqaT = C(wq_a.T)
    woT = C(wo.T)
    WT = C(W.T)

    qlnv = C(q_a_ln.reshape(8, 128).T)
    kvlnv = C(kv_a_ln.reshape(2, 128).T)
    bqvv = C(bias.reshape(32, 128).T)

    cos, sin = _yarn_cos_sin(S)
    sgnv = np.concatenate([-np.ones(32, np.float32), np.ones(32, np.float32)])
    ssin = sin * sgnv[None, :]
    CSKv = C(np.concatenate([cos.T, ssin.T], axis=0))

    FOLD_Knp = np.zeros((128, 64), np.float32)
    for jj in range(64):
        FOLD_Knp[jj, jj] = 1.0
        FOLD_Knp[jj + 64, jj] = 1.0
    FOLD_LOnp = np.zeros((128, 128), np.float32)
    FOLD_LOnp[:, 0:64] = FOLD_Knp
    FOLD_HInp = np.zeros((128, 128), np.float32)
    FOLD_HInp[:, 64:128] = FOLD_Knp

    constsv = np.zeros((128, 8), np.float32)
    constsv[:, C_EPSLN] = EPS_LN
    constsv[:, C_EPSAT] = EPS_ATTN
    constsv[:, C_NEGH] = -0.5
    constsv[:, C_I1024] = 1.0 / 1024.0
    constsv[:, C_I256] = 1.0 / 256.0
    constsv[:, C_I4096] = 1.0 / 4096.0
    constsv[:, C_BA2] = -0.5
    constsv[:, C_BB2] = 0.5

    bf = ml_dtypes.bfloat16
    shared = dict(
        wqaT=wqaT, wqbnT=wqbnT, wqbpT=wqbpT, wkvaT=wkvaT, wkvbnT=wkvbnT,
        wkvbvT=wkvbvT, woT=woT, WT=WT, qln=qlnv, kvln=kvlnv, bqv=bqvv, CSK=CSKv,
        FOLD_K=FOLD_Knp.astype(bf), FOLD_LO=FOLD_LOnp.astype(bf),
        FOLD_HI=FOLD_HInp.astype(bf),
        I128B=np.eye(128, dtype=np.float32).astype(bf),
        I128F=np.eye(128, dtype=np.float32),
        ONESB=np.ones((128, 1), np.float32).astype(bf),
        ONESFR=np.ones((1, 128), np.float32),
        ONESFC=np.ones((128, 1), np.float32),
        CONSTS=constsv,
    )

    in_maps = []
    xT_b = [C(x[b_].T) for b_ in range(B)]
    for core in range(8):
        b_, c_ = core // 4, core % 4
        qp = _qpos(c_)
        mw = np.where(np.arange(512)[None, :] <= 128 * c_ + np.arange(128)[:, None],
                      0.0, NEG).astype(np.float32)
        m = dict(shared)
        m.update(xT=xT_b[b_], x_tm=x[b_], xq_tm=C(x[b_][qp]),
                 xqT=C(xT_b[b_][:, qp]), CSQ=C(CSKv[:, qp]), MASKW=mw)
        in_maps.append(m)

    nc = _get_prog()
    res = run_bass_kernel_spmd(nc, in_maps, core_ids=list(range(8)))

    outf = np.zeros((B, S, D), dtype=np.float32)
    for core in range(8):
        b_, c_ = core // 4, core % 4
        qp = _qpos(c_)
        outf[b_, qp, :] = np.asarray(res.results[core]["out"]).T
    return outf
